# version 02
# baseline (speedup 1.0000x reference)
import numpy as np
import ml_dtypes
from contextlib import ExitStack

import concourse.bass as bass
import concourse.mybir as mybir
from concourse.bass_utils import run_bass_kernel_spmd
from concourse.alu_op_type import AluOpType as ALU

F32 = mybir.dt.float32
BF16 = mybir.dt.bfloat16
AF = mybir.ActivationFunctionType
AX = mybir.AxisListType.X

T = 4096
D = 1024
NT = T // 128
NCH = D // 128



class Prog:

    KEYS = ("te", "ve", "se", "ge", "sy")

    def __init__(self, nc):
        self.nc = nc
        self.steps = []
        self.nbar = 0
        self.dma_cnt = {"ge": 0, "sy": 0, "st": 0}
        self.sems = {}
        self.sem_cnt = {}

    def inc(self, name):
        self.sem_cnt[name] = self.sem_cnt.get(name, 0) + 1
        return self.sem_cnt[name]

    def val(self, name):
        return self.sem_cnt.get(name, 0)

    def step(self, key, fn):
        assert key in self.KEYS
        self.steps.append(("step", key, fn))

    def barrier(self):
        self.steps.append(("bar",))
        self.nbar += 1

    def load(self, key, out_ap, in_ap, **kw):
        self.dma_cnt[key] += 1
        tgt = self.dma_cnt[key] * 16
        sem_name = "dma_" + key

        def fn(eng, out_ap=out_ap, in_ap=in_ap, kw=kw, sem_name=sem_name):
            return eng.dma_start(out=out_ap, in_=in_ap, **kw).then_inc(
                self.sems[sem_name], 16)

        fn._dma_kind = key
        self.step(key, fn)
        return tgt

    def store(self, out_ap, in_ap, **kw):
        self.dma_cnt["st"] += 1
        tgt = self.dma_cnt["st"] * 16

        def fn(eng, out_ap=out_ap, in_ap=in_ap, kw=kw):
            return eng.dma_start(out=out_ap, in_=in_ap, **kw).then_inc(
                self.sems["dma_st"], 16)

        fn._dma_kind = "st"
        self.step("sy", fn)
        return tgt

    def finalize(self, block, sem_handles, dummies):
        self.sems = sem_handles
        steps = self.steps

        def run_engine(eng, key):
            bar_i = 0
            seen = {"ge": 0, "sy": 0, "st": 0}
            for st_ in steps:
                if st_[0] == "bar":
                    bar_i += 1
                    if key == "sy":
                        if seen["sy"]:
                            eng.wait_ge(self.sems["dma_sy"], seen["sy"] * 16)
                        if seen["st"]:
                            eng.wait_ge(self.sems["dma_st"], seen["st"] * 16)
                        eng.sem_inc(self.sems["bar"], 1)
                    elif key == "ge":
                        if seen["ge"]:
                            eng.wait_ge(self.sems["dma_ge"], seen["ge"] * 16)
                        eng.drain()
                        eng.tensor_copy(out=dummies["ge"], in_=dummies["ge"]
                                        ).then_inc(self.sems["bar"], 1)
                    elif key == "te":
                        eng.drain()
                        eng.nop().then_inc(self.sems["bar"], 1)
                    elif key == "ve":
                        eng.drain()
                        eng.tensor_copy(out=dummies["ve"], in_=dummies["ve"]
                                        ).then_inc(self.sems["bar"], 1)
                    else:
                        eng.drain()
                        eng.copy(out=dummies["se"], in_=dummies["se"]
                                 ).then_inc(self.sems["bar"], 1)
                    eng.wait_ge(self.sems["bar"], 5 * bar_i)
                    continue
                _, k, fn = st_
                kind = getattr(fn, "_dma_kind", None)
                if kind is not None:
                    seen[kind] += 1
                if k != key:
                    continue
                fn(eng)

        @block.tensor
        def _(te):
            run_engine(te, "te")

        @block.vector
        def _(ve):
            run_engine(ve, "ve")

        @block.scalar
        def _(se):
            run_engine(se, "se")

        @block.gpsimd
        def _(ge):
            run_engine(ge, "ge")

        @block.sync
        def _(sy):
            run_engine(sy, "sy")


def _dram_dedup(nc, fc, name, shape, dt_):
    if fc is not None:
        cache = fc.setdefault("dram_cache", {})
        if name not in cache:
            cache[name] = nc.dram_tensor(name, shape, dt_, kind="ExternalInput").ap()
        return cache[name]
    return nc.dram_tensor(name, shape, dt_, kind="ExternalInput").ap()



def build_l0_attn(T=T, NT=NT, fc=None, phases=('A0','MC','A1','A2','A2qk','A2v','A2n','A2a','A3')):
    n512 = T // 512
    nc = fc["nc"] if fc else bass.Bass("TRN2", target_bir_lowering=False, detect_race_conditions=False)
    px = fc["px"] if fc else ""

    x_in = fc["x_in"] if fc else nc.dram_tensor("x", [T, D], F32, kind="ExternalInput").ap()
    wq_d = nc.dram_tensor("wq", [D, D], F32, kind="ExternalInput").ap()
    wk_d = nc.dram_tensor("wk", [D, D], F32, kind="ExternalInput").ap()
    wv_d = nc.dram_tensor("wv", [D, D], F32, kind="ExternalInput").ap()
    lns_fm_d = nc.dram_tensor("lns_fm", [128, NCH], F32, kind="ExternalInput").ap()
    lnb_fm_d = nc.dram_tensor("lnb_fm", [128, NCH], F32, kind="ExternalInput").ap()
    lns_row_d = nc.dram_tensor("lns_row", [1, D], F32, kind="ExternalInput").ap()
    lnb_row_d = nc.dram_tensor("lnb_row", [1, D], F32, kind="ExternalInput").ap()
    identbf_d = _dram_dedup(nc, fc, "identbf", [128, 128], BF16)
    identf_d = _dram_dedup(nc, fc, "identf", [128, 128], F32)
    bd65_d = nc.dram_tensor("bd65", [128, 65], BF16, kind="ExternalInput").ap()
    x_out = fc["x_out"] if fc else nc.dram_tensor("y", [T, D], F32, kind="ExternalOutput").ap()
    dbg = "DBG" in phases
    if dbg:
        dbg_outs = {n: nc.dram_tensor("dbg_" + n, sh, F32, kind="ExternalOutput").ap()
                    for n, sh in [("qa0", [128, T]), ("ka0", [128, T]), ("qa1", [128, T]),
                                  ("vp0", [128, 130]), ("otm0", [128, D]),
                                  ("Sb", [128, D]), ("Bb", [128, D]),
                                  ("mcol", [128, NT]), ("Ccol", [128, NT]),
                                  ("ssum", [128, NT]), ("ssq", [128, NT]),
                                  ("tmp1", [128, NT]), ("tmp2", [128, NT]),
                                  ("xnfm0", [128, T]), ("mk2a", [128, 1])]}

    x_t = x_in.rearrange("(n p) d -> n p d", p=128)
    y_t = x_out.rearrange("(n p) d -> n p d", p=128)

    p = fc["p"] if fc else Prog(nc)

    with ExitStack() as ctx:
        e = ctx.enter_context

        Xbuf = [e(nc.sbuf_tensor(px + f"Xbuf{i}", [128, D], F32)) for i in range(2)]
        Ybuf = [e(nc.sbuf_tensor(px + f"Ybuf{i}", [128, D], F32)) for i in range(2)]
        ssum = e(nc.sbuf_tensor(px + "ssum", [128, NT], F32))
        ssq = e(nc.sbuf_tensor(px + "ssq", [128, NT], F32))
        mcol = e(nc.sbuf_tensor(px + "mcol", [128, NT], F32))
        Ccol = e(nc.sbuf_tensor(px + "Ccol", [128, NT], F32))
        tmp1 = e(nc.sbuf_tensor(px + "tmp1", [128, NT], F32))
        tmp2 = e(nc.sbuf_tensor(px + "tmp2", [128, NT], F32))
        xnbuf = [e(nc.sbuf_tensor(px + f"xnbuf{i}", [128, D], BF16)) for i in range(2)]
        xn_fm = [e(nc.sbuf_tensor(px + f"xnfm{c}", [128, T], BF16)) for c in range(NCH)]
        o_tm = [e(nc.sbuf_tensor(px + f"otm{t}", [128, D], BF16)) for t in range(NT)]
        v_pair = [e(nc.sbuf_tensor(px + f"vp{t}", [128, 130], BF16)) for t in range(NT)]
        qa = [e(nc.sbuf_tensor(px + f"qa{h}", [128, T], BF16)) for h in range(2)]
        ka = [e(nc.sbuf_tensor(px + f"ka{h}", [128, T], BF16)) for h in range(2)]
        mk2a = e(nc.sbuf_tensor(px + "mk2a", [128, 1], F32))
        mk2t = e(nc.sbuf_tensor(px + "mk2t", [128, 1], F32))
        dumV = e(nc.sbuf_tensor(px + "dumV", [1, 1], F32))
        dumS = e(nc.sbuf_tensor(px + "dumS", [1, 1], F32))
        dumG = e(nc.sbuf_tensor(px + "dumG", [1, 1], F32))
        Pt = [e(nc.sbuf_tensor(px + f"Pt{i}", [128, 512], BF16)) for i in range(2)]
        oT = e(nc.sbuf_tensor(px + "oT", [65, 512], F32))
        den = e(nc.sbuf_tensor(px + "den", [128, 1], F32))
        sqq = e(nc.sbuf_tensor(px + "sqq", [128, 512], BF16))
        sqk = e(nc.sbuf_tensor(px + "sqk", [128, 512], BF16))
        Sb = e(nc.sbuf_tensor(px + "Sb", [128, D], BF16))
        Bb = e(nc.sbuf_tensor(px + "Bb", [128, D], BF16))
        sfm = e(nc.sbuf_tensor(px + "sfm", [128, NCH], F32))
        bfm = e(nc.sbuf_tensor(px + "bfm", [128, NCH], F32))
        identbf = e(nc.sbuf_tensor(px + "identbf_s", [128, 128], BF16))
        identf = e(nc.sbuf_tensor(px + "identf_s", [128, 128], F32))
        bd65 = e(nc.sbuf_tensor(px + "bd65_s", [128, 65], BF16))
        ones1 = e(nc.sbuf_tensor(px + "ones1", [1, 128], BF16))
        wqs = e(nc.sbuf_tensor(px + "wqs", [128, NCH, 128], BF16))
        wks = e(nc.sbuf_tensor(px + "wks", [128, NCH, 128], BF16))
        wvs = e(nc.sbuf_tensor(px + "wvs", [128, NCH, 128], BF16))

        psumT8 = e(nc.psum_tensor(px + "psumT8", [128, 1024], BF16))
        psA = e(nc.psum_tensor(px + "psA", [128, 512], F32))
        psB = e(nc.psum_tensor(px + "psB", [128, 512], F32))
        psC = e(nc.psum_tensor(px + "psC", [128, 512], F32))
        psD = e(nc.psum_tensor(px + "psD", [128, 512], F32))
        psE = e(nc.psum_tensor(px + "psE", [128, 512], F32))
        psF = e(nc.psum_tensor(px + "psF", [128, 512], F32))

        sem_names = ["dma_ge", "dma_sy", "dma_st", "bar", "bc",
                     "a0", "a1x", "a1t", "a1e", "proj",
                     "s_q", "s_k", "s_v", "s_evq", "s_evk", "s_evv",
                     "s_nq", "s_nk", "s_evn",
                     "s_s", "s_p", "s_pv", "s_eo", "s_tr", "s_no",
                     "a3v"]
        S = fc["S"] if fc else {n: e(nc.semaphore(n)) for n in sem_names}
        block = None if fc else e(nc.Block())

        if "noC0sy" not in phases:
            p.load("sy", identbf[:], identbf_d)
            p.load("sy", identf[:], identf_d)
            p.load("sy", bd65[:], bd65_d)
            p.load("sy", sfm[:], lns_fm_d)
            p.load("sy", bfm[:], lnb_fm_d)
        if "noC0ge" not in phases:
            p.load("ge", Sb[0:1, :], lns_row_d)
            p.load("ge", Bb[0:1, :], lnb_row_d)

        wcnt = [None] * 8

        def _wslice(wd, pair):
            return wd[:, pair * 128:(pair + 1) * 128].rearrange(
                "(c p) m -> p c m", p=128)

        if "noC0w" not in phases:
            p.load("ge", wqs[:], _wslice(wq_d, 0))
            p.load("ge", wks[:], _wslice(wk_d, 0))
            wcnt[0] = p.load("ge", wvs[:], _wslice(wv_d, 0))

        def st_memset(key, ap, val):
            def fn(eng, ap=ap, val=val):
                return eng.memset(ap, val)
            p.step(key, fn)

        st_memset("ve", ones1[:], 1.0)
        st_memset("ve", dumV[:], 0.0)
        st_memset("ge", dumG[:], 0.0)

        def fn_dumS(eng):
            return eng.memzero(dumS[:])
        p.step("se", fn_dumS)
        for t in (range(NT) if "noC0ms" not in phases else []):
            st_memset("ve", v_pair[t][:, 64:65], 1.0)
            st_memset("ve", v_pair[t][:, 129:130], 1.0)
        if "noC0ms" not in phases:
            for h in range(2):
                st_memset("ve", qa[h][:], 0.0)
                st_memset("ve", ka[h][:], 0.0)
            st_memset("ve", ka[0][64:65, :], -1.0)
            st_memset("ve", ka[1][32:33, :], -1.0)

        p.barrier()

        _bc = "noC0bc" not in phases
        def fn_bc1(eng):
            eng.matmul(psA[:, 0:512], ones1[:], Sb[0:1, 0:512], start=True, stop=True)
            eng.matmul(psB[:, 0:512], ones1[:], Sb[0:1, 512:1024], start=True, stop=True)
            eng.matmul(psC[:, 0:512], ones1[:], Bb[0:1, 0:512], start=True, stop=True)
            return eng.matmul(psD[:, 0:512], ones1[:], Bb[0:1, 512:1024], start=True,
                              stop=True).then_inc(S["bc"], 1)
        if _bc: p.step("te", fn_bc1)
        p.barrier()

        def fn_bc2(eng):
            eng.wait_ge(S["bc"], 1)
            eng.tensor_copy(out=Sb[:, 0:512], in_=psA[:, 0:512])
            eng.tensor_copy(out=Sb[:, 512:1024], in_=psB[:, 0:512])
            eng.tensor_copy(out=Bb[:, 0:512], in_=psC[:, 0:512])
            return eng.tensor_copy(out=Bb[:, 512:1024], in_=psD[:, 0:512])
        if _bc and "noC0bc2" not in phases: p.step("ve", fn_bc2)
        p.barrier()

        for t in (range(NT) if 'A0' in phases else []):
            if t >= 2:
                need = 2 * (t - 1)

                def fn_w(eng, need=need):
                    eng.wait_ge(S["a0"], need)
                    return None
                p.step("sy", fn_w)
            lc = p.load("sy", Xbuf[t % 2][:], x_t[t])

            def fn_v(eng, t=t, lc=lc):
                eng.wait_ge(S["dma_sy"], lc)
                return eng.tensor_scalar(
                    out=Ybuf[0][:], in0=Xbuf[t % 2][:], scalar1=1.0, scalar2=0.0,
                    op0=ALU.mult, op1=ALU.add,
                    accum_out=ssum[:, t:t + 1]).then_inc(S["a0"], 1)
            p.step("ve", fn_v)
            p.inc("a0")

            def fn_s(eng, t=t, lc=lc):
                eng.wait_ge(S["dma_sy"], lc)
                return eng.activation(
                    out=Ybuf[1][:], in_=Xbuf[t % 2][:], func=AF.Square,
                    accum_out=ssq[:, t:t + 1]).then_inc(S["a0"], 1)
            p.step("se", fn_s)
            p.inc("a0")

        p.barrier()

        _do_mc = 'MC' in phases
        def fn_mc1(eng):
            eng.drain()
            eng.tensor_scalar(out=mcol[:], in0=ssum[:], scalar1=1.0 / D,
                              scalar2=None, op0=ALU.mult, op1=ALU.bypass)
            eng.tensor_scalar(out=tmp1[:], in0=ssq[:], scalar1=1.0 / D,
                              scalar2=None, op0=ALU.mult, op1=ALU.bypass)
            eng.drain()
            eng.tensor_tensor(out=tmp2[:], in0=mcol[:], in1=mcol[:], op=ALU.mult)
            eng.drain()
            eng.tensor_tensor(out=tmp1[:], in0=tmp1[:], in1=tmp2[:], op=ALU.subtract)
            eng.drain()
            eng.tensor_scalar(out=tmp2[:], in0=tmp1[:], scalar1=1e-5,
                              scalar2=None, op0=ALU.add, op1=ALU.bypass)
            eng.drain()
            eng.reciprocal(out=Ccol[:], in_=tmp2[:])
            eng.drain()
            eng.tensor_tensor(out=tmp1[:], in0=tmp1[:], in1=Ccol[:], op=ALU.mult)
            eng.drain()
            eng.tensor_scalar(out=tmp1[:], in0=tmp1[:], scalar1=1e-6,
                              scalar2=None, op0=ALU.add, op1=ALU.bypass)
            eng.drain()
            return eng.tensor_tensor(out=tmp1[:], in0=tmp2[:], in1=tmp1[:], op=ALU.mult)
        if _do_mc: p.step("ve", fn_mc1)
        p.barrier()

        def fn_mc2(eng):
            return eng.activation(out=tmp2[:], in_=tmp1[:], func=AF.Sqrt)
        if _do_mc: p.step("se", fn_mc2)
        p.barrier()

        def fn_mc3(eng):
            return eng.reciprocal(out=Ccol[:], in_=tmp2[:])
        if _do_mc: p.step("ve", fn_mc3)
        p.barrier()

        for t in (range(NT) if 'A1' in phases else []):
            if t >= 2:
                def fn_w(eng, need=t - 1):
                    eng.wait_ge(S["a1x"], need)
                    return None
                p.step("sy", fn_w)
            lc = p.load("sy", Xbuf[t % 2][:], x_t[t])

            def fn_v(eng, t=t, lc=lc):
                eng.wait_ge(S["dma_sy"], lc)
                if t >= 2:
                    eng.wait_ge(S["a1t"], t - 1)
                return eng.tensor_scalar(
                    out=xnbuf[t % 2][:], in0=Xbuf[t % 2][:],
                    scalar1=mcol[:, t:t + 1], scalar2=Ccol[:, t:t + 1],
                    op0=ALU.subtract, op1=ALU.mult).then_inc(S["a1x"], 1)
            p.step("ve", fn_v)
            p.inc("a1x")

            def fn_t(eng, t=t):
                eng.wait_ge(S["a1x"], t + 1)
                if t >= 1:
                    eng.wait_ge(S["a1e"], t)
                r = None
                for c in range(NCH):
                    r = eng.transpose(psumT8[:, c * 128:(c + 1) * 128],
                                      xnbuf[t % 2][:, c * 128:(c + 1) * 128],
                                      identbf[:])
                return r.then_inc(S["a1t"], 1)
            p.step("te", fn_t)
            p.inc("a1t")

            def fn_e(eng, t=t):
                eng.wait_ge(S["a1t"], t + 1)
                r = None
                for c in range(NCH):
                    r = eng.tensor_scalar(
                        out=xn_fm[c][:, t * 128:(t + 1) * 128],
                        in0=psumT8[:, c * 128:(c + 1) * 128],
                        scalar1=sfm[:, c:c + 1], scalar2=bfm[:, c:c + 1],
                        op0=ALU.mult, op1=ALU.add)
                return r.then_inc(S["a1e"], 1)
            p.step("ve", fn_e)
            p.inc("a1e")

        p.barrier()

        for pair in (range(8) if 'A2' in phases else []):
            st_memset("ve", mk2a[:], 0.0)

            for t8 in range(n512):
                sl512 = slice(t8 * 512, (t8 + 1) * 512)

                need = p.val("s_evq")

                def fn_qmm(eng, pair=pair, sl512=sl512, need=need):
                    eng.wait_ge(S["dma_ge"], wcnt[pair])
                    if need:
                        eng.wait_ge(S["s_evq"], need)
                    r = None
                    for c in range(NCH):
                        r = eng.matmul(psA[:, 0:512], wqs[:, c, :],
                                       xn_fm[c][:, sl512],
                                       start=(c == 0), stop=(c == NCH - 1))
                    return r.then_inc(S["s_q"], 1)
                if 'A2qk' in phases:
                    p.step("te", fn_qmm)
                sq_tgt = p.inc("s_q") if 'A2qk' in phases else 0

                def fn_qev(eng, sl512=sl512, tgt=sq_tgt):
                    eng.wait_ge(S["s_q"], tgt)
                    eng.copy(out=qa[0][0:64, sl512], in_=psA[0:64, 0:512])
                    eng.copy(out=qa[1][64:128, sl512], in_=psA[64:128, 0:512])
                    return eng.activation(out=sqq[:], in_=psA[:, 0:512],
                                          func=AF.Square).then_inc(S["s_evq"], 1)
                if 'A2qk' in phases:
                    p.step("se", fn_qev)
                evq_tgt = p.inc("s_evq") if 'A2qk' in phases else 0

                need = p.val("s_evk")

                def fn_kmm(eng, sl512=sl512, need=need):
                    if need:
                        eng.wait_ge(S["s_evk"], need)
                    r = None
                    for c in range(NCH):
                        r = eng.matmul(psB[:, 0:512], wks[:, c, :],
                                       xn_fm[c][:, sl512],
                                       start=(c == 0), stop=(c == NCH - 1))
                    return r.then_inc(S["s_k"], 1)
                if 'A2qk' in phases:
                    p.step("te", fn_kmm)
                sk_tgt = p.inc("s_k") if 'A2qk' in phases else 0

                def fn_kev(eng, sl512=sl512, tgt=sk_tgt):
                    eng.wait_ge(S["s_k"], tgt)
                    eng.copy(out=ka[0][0:64, sl512], in_=psB[0:64, 0:512])
                    eng.copy(out=ka[1][64:128, sl512], in_=psB[64:128, 0:512])
                    return eng.activation(out=sqk[:], in_=psB[:, 0:512],
                                          func=AF.Square).then_inc(S["s_evk"], 1)
                if 'A2qk' in phases:
                    p.step("se", fn_kev)
                evk_tgt = p.inc("s_evk") if 'A2qk' in phases else 0

                for tt in range(4):
                    ti = t8 * 4 + tt
                    need = p.val("s_evv")

                    def fn_vmm(eng, ti=ti, need=need):
                        if need:
                            eng.wait_ge(S["s_evv"], need)
                        r = None
                        for c in range(NCH):
                            r = eng.matmul(psC[:, 0:128],
                                           xn_fm[c][:, ti * 128:(ti + 1) * 128],
                                           wvs[:, c, :],
                                           start=(c == 0), stop=(c == NCH - 1))
                        return r.then_inc(S["s_v"], 1)
                    if 'A2v' in phases:
                        p.step("te", fn_vmm)
                    sv_tgt = p.inc("s_v") if 'A2v' in phases else 0

                    def fn_vev(eng, ti=ti, tgt=sv_tgt):
                        eng.wait_ge(S["s_v"], tgt)
                        eng.tensor_copy(out=v_pair[ti][:, 0:64], in_=psC[:, 0:64])
                        return eng.tensor_copy(out=v_pair[ti][:, 65:129],
                                               in_=psC[:, 64:128]).then_inc(S["s_evv"], 1)
                    if 'A2v' in phases:
                        p.step("ve", fn_vev)
                        p.inc("s_evv")

                need = p.val("s_evn")

                def fn_nmm(eng, eq=evq_tgt, ek=evk_tgt, need=need):
                    eng.wait_ge(S["s_evq"], eq)
                    eng.wait_ge(S["s_evk"], ek)
                    if need:
                        eng.wait_ge(S["s_evn"], need)
                    eng.matmul(psD[0:65, 0:512], bd65[:], sqq[:], start=True,
                               stop=True).then_inc(S["s_nq"], 1)
                    return eng.matmul(psE[0:65, 0:512], bd65[:], sqk[:],
                                      start=True, stop=True).then_inc(S["s_nk"], 1)
                if 'A2n' in phases:
                    p.step("te", fn_nmm)
                nq_tgt = p.inc("s_nq") if 'A2n' in phases else 0
                nk_tgt = p.inc("s_nk") if 'A2n' in phases else 0

                def fn_nev(eng, sl512=sl512, nq=nq_tgt, nk=nk_tgt):
                    eng.wait_ge(S["s_nq"], nq)
                    eng.wait_ge(S["s_nk"], nk)
                    eng.tensor_copy(out=qa[0][64:65, sl512], in_=psD[64:65, 0:512])
                    eng.tensor_copy(out=qa[1][32:33, sl512], in_=psD[32:33, 0:512])
                    eng.reduce_max(out=mk2t[64:65, :], in_=psE[64:65, 0:512],
                                   axis=AX, negate=False)
                    eng.drain()
                    eng.tensor_tensor(out=mk2a[64:65, :], in0=mk2a[64:65, :],
                                      in1=mk2t[64:65, :], op=ALU.max)
                    eng.reduce_max(out=mk2t[32:33, :], in_=psE[32:33, 0:512],
                                   axis=AX, negate=False)
                    eng.drain()
                    return eng.tensor_tensor(out=mk2a[32:33, :],
                                             in0=mk2a[32:33, :],
                                             in1=mk2t[32:33, :],
                                             op=ALU.max).then_inc(S["s_evn"], 1)
                if 'A2n' in phases:
                    p.step("ve", fn_nev)
                    p.inc("s_evn")

            def fn_projdone(eng):
                eng.sem_inc(S["proj"], 1)
                return None
            p.step("te", fn_projdone)
            p.inc("proj")

            p.barrier()

            def fn_brow(eng):
                eng.drain()
                eng.tensor_scalar(out=qa[0][64:65, :], in0=qa[0][64:65, :],
                                  scalar1=mk2a[64:65, :], scalar2=0.5,
                                  op0=ALU.add, op1=ALU.mult)
                return eng.tensor_scalar(out=qa[1][32:33, :],
                                         in0=qa[1][32:33, :],
                                         scalar1=mk2a[32:33, :], scalar2=0.5,
                                         op0=ALU.add, op1=ALU.mult)
            if 'A2a' in phases: p.step("ve", fn_brow)
            p.barrier()

            for hh in (range(2) if 'A2a' in phases else []):
                head = pair * 2 + hh
                vsl = slice(hh * 65, hh * 65 + 65)
                for q8 in range(n512):
                    qsl = slice(q8 * 512, (q8 + 1) * 512)
                    g = p.val("s_eo")
                    pvbank = psC if g % 2 == 0 else psD
                    exp_tgts = []

                    def emit_S(kt):
                        n = p.val("s_s")
                        war = n - 1
                        sbank = psA if (n % 2 == 0) else psB

                        def fn_smm(eng, hh=hh, kt=kt, qsl=qsl, sbank=sbank,
                                   war=war):
                            if war >= 1:
                                eng.wait_ge(S["s_p"], war)
                            return eng.matmul(
                                sbank[:, 0:512],
                                ka[hh][:, kt * 128:(kt + 1) * 128],
                                qa[hh][:, qsl], start=True,
                                stop=True).then_inc(S["s_s"], 1)
                        p.step("te", fn_smm)
                        ss_tgt = p.inc("s_s")

                        ne = p.val("s_p")
                        ptwar = ne - 1

                        def fn_exp(eng, sbank=sbank, ss=ss_tgt, ne=ne,
                                   ptwar=ptwar):
                            eng.wait_ge(S["s_s"], ss)
                            if ptwar >= 1:
                                eng.wait_ge(S["s_pv"], ptwar)
                            eng.activation(
                                out=Pt[ne % 2][:], in_=sbank[:, 0:512],
                                func=AF.Exp, scale=1.0 / 32.0)
                            eng.drain()
                            return eng.copy(out=dumS[:], in_=dumS[:]
                                            ).then_inc(S["s_p"], 1)
                        p.step("se", fn_exp)
                        exp_tgts.append((p.inc("s_p"), (p.val("s_p") - 1) % 2))

                    def emit_PV(kt, g=g, pvbank=pvbank, vsl=vsl):
                        sp_tgt, pt_i = exp_tgts[kt]
                        eo_war = g - 1 if kt == 0 else -1

                        def fn_pv(eng, kt=kt, sp=sp_tgt, pt_i=pt_i,
                                  eo_war=eo_war, pvbank=pvbank, vsl=vsl):
                            eng.wait_ge(S["s_p"], sp)
                            if eo_war >= 1:
                                eng.wait_ge(S["s_eo"], eo_war)
                            return eng.matmul(
                                pvbank[0:65, 0:512], v_pair[kt][:, vsl],
                                Pt[pt_i][:], start=(kt == 0),
                                stop=(kt == NT - 1)).then_inc(S["s_pv"], 1)
                        p.step("te", fn_pv)
                        p.inc("s_pv")

                    emit_S(0)
                    for kt in range(1, NT):
                        emit_S(kt)
                        emit_PV(kt - 1)
                    emit_PV(NT - 1)

                    pv_done = p.val("s_pv")
                    tr_before = p.val("s_tr")

                    def fn_eo(eng, pvbank=pvbank, pv=pv_done, trb=tr_before):
                        eng.wait_ge(S["s_pv"], pv)
                        if trb >= 1:
                            eng.wait_ge(S["s_tr"], trb)
                        eng.copy(out=oT[:], in_=pvbank[0:65, 0:512])
                        eng.drain()
                        return eng.copy(out=dumS[:], in_=dumS[:]
                                        ).then_inc(S["s_eo"], 1)
                    p.step("se", fn_eo)
                    eo_tgt = p.inc("s_eo")

                    for qq in range(4):
                        ti = q8 * 4 + qq
                        j = p.val("s_tr")
                        trbank = psE if j % 2 == 0 else psF
                        no_war = j - 1

                        def fn_tr(eng, qq=qq, eo=eo_tgt, trbank=trbank,
                                  war=no_war):
                            eng.wait_ge(S["s_eo"], eo)
                            if war >= 1:
                                eng.wait_ge(S["s_no"], war)
                            return eng.transpose(
                                trbank[:, 0:65], oT[:, qq * 128:(qq + 1) * 128],
                                identf[0:65, 0:65]).then_inc(S["s_tr"], 1)
                        p.step("te", fn_tr)
                        tr_tgt = p.inc("s_tr")

                        def fn_no(eng, ti=ti, head=head, trbank=trbank,
                                  tr=tr_tgt):
                            eng.wait_ge(S["s_tr"], tr)
                            eng.reciprocal(out=den[:], in_=trbank[:, 64:65])
                            eng.drain()
                            return eng.scalar_tensor_tensor(
                                out=o_tm[ti][:, head * 64:(head + 1) * 64],
                                in0=trbank[:, 0:64], scalar=den[:],
                                in1=Bb[:, head * 64:(head + 1) * 64],
                                op0=ALU.mult, op1=ALU.add).then_inc(S["s_no"], 1)
                        p.step("ve", fn_no)
                        p.inc("s_no")

            if pair + 1 < 8:
                _wq = p.val("s_evq"); _wk = p.val("s_evk")
                _wv = p.val("s_evv"); _wn = p.val("s_evn")

                def fn_wp(eng, a=_wq, b=_wk, c=_wv, d=_wn):
                    if a: eng.wait_ge(S["s_evq"], a)
                    if b: eng.wait_ge(S["s_evk"], b)
                    if c: eng.wait_ge(S["s_evv"], c)
                    if d: eng.wait_ge(S["s_evn"], d)
                    return None
                p.step("ge", fn_wp)
                p.load("ge", wqs[:], _wslice(wq_d, pair + 1))
                p.load("ge", wks[:], _wslice(wk_d, pair + 1))
                wcnt[pair + 1] = p.load("ge", wvs[:], _wslice(wv_d, pair + 1))

            p.barrier()

        for t in (range(NT) if 'A3' in phases else []):
            _a3p = p.val("a3v")
            if _a3p >= 2:
                def fn_w(eng, need=_a3p - 1):
                    eng.wait_ge(S["a3v"], need)
                    return None
                p.step("sy", fn_w)
            lc = p.load("sy", Xbuf[t % 2][:], x_t[t])
            _stp = (p.dma_cnt["st"] - 1) * 16

            def fn_v(eng, t=t, lc=lc, stp=_stp):
                eng.wait_ge(S["dma_sy"], lc)
                if stp >= 16:
                    eng.wait_ge(S["dma_st"], stp)
                yb = Ybuf[t % 2]
                eng.tensor_scalar(out=yb[:], in0=Xbuf[t % 2][:],
                                  scalar1=mcol[:, t:t + 1], scalar2=Ccol[:, t:t + 1],
                                  op0=ALU.subtract, op1=ALU.mult)
                eng.drain()
                eng.tensor_tensor(out=yb[:], in0=yb[:], in1=Sb[:], op=ALU.mult)
                eng.drain()
                eng.tensor_tensor(out=yb[:], in0=yb[:], in1=Xbuf[t % 2][:],
                                  op=ALU.add)
                eng.drain()
                return eng.tensor_tensor(out=yb[:], in0=yb[:],
                                         in1=o_tm[t][:], op=ALU.add).then_inc(S["a3v"], 1)
            p.step("ve", fn_v)
            a3v_tgt = p.inc("a3v")

            def fn_st_w(eng, tgt=a3v_tgt):
                eng.wait_ge(S["a3v"], tgt)
                return None
            p.step("sy", fn_st_w)
            p.store(y_t[t], Ybuf[t % 2][:])

        p.barrier()
        if dbg:
            def _cast(name, src_tile):
                def fn(eng, name=name, src_tile=src_tile):
                    return eng.tensor_copy(out=dbg_sb[name][:], in_=src_tile)
                p.step("ve", fn)
            dbg_sb = {}
            for name, tile in [("qa0", qa[0]), ("ka0", ka[0]), ("qa1", qa[1]),
                               ("vp0", v_pair[0]), ("otm0", o_tm[0]),
                               ("Sb", Sb), ("Bb", Bb)]:
                dbg_sb[name] = e(nc.sbuf_tensor(px + "dbgs_" + name, list(tile.shape), F32))
                _cast(name, tile[:])
            p.barrier()
            for name in dbg_sb:
                p.store(dbg_outs[name], dbg_sb[name][:])
            p.store(dbg_outs["mcol"], mcol[:])
            p.store(dbg_outs["Ccol"], Ccol[:])
            p.store(dbg_outs["ssum"], ssum[:])
            p.store(dbg_outs["ssq"], ssq[:])
            p.store(dbg_outs["tmp1"], tmp1[:])
            p.store(dbg_outs["tmp2"], tmp2[:])
            p.store(dbg_outs["mk2a"], mk2a[:])
            def fnx(eng):
                return eng.tensor_copy(out=dbg_sb["qa0"][:], in_=xn_fm[0][:])
            p.step("ve", fnx)
            p.barrier()
            p.store(dbg_outs["xnfm0"], dbg_sb["qa0"][:])
            p.barrier()
        if not fc:
            p.finalize(block, S, {"ve": dumV[:], "se": dumS[:], "ge": dumG[:]})

    return nc



def build_l0_ffn(T=T, NT=NT, fc=None):
    n512 = T // 512
    NGRP = 2 if T >= 4096 else 1
    GT = T // NGRP
    gNT = GT // 128
    gn512 = GT // 512

    nc = fc["nc"] if fc else bass.Bass("TRN2", target_bir_lowering=False, detect_race_conditions=False)
    px = fc["px"] if fc else ""

    x_in = fc["x_in"] if fc else nc.dram_tensor("x", [T, D], F32, kind="ExternalInput").ap()
    gw_d = nc.dram_tensor("gwT", [D, 8], F32, kind="ExternalInput").ap()
    w1_d = nc.dram_tensor("w1", [9, D, D], F32, kind="ExternalInput").ap()
    w2_d = nc.dram_tensor("w2", [9, D, D], F32, kind="ExternalInput").ap()
    w3_d = nc.dram_tensor("w3", [9, D, D], F32, kind="ExternalInput").ap()
    identf_d = _dram_dedup(nc, fc, "identf", [128, 128], F32)
    identbf_d = _dram_dedup(nc, fc, "identbf", [128, 128], BF16)
    x_out = fc["x_out"] if fc else nc.dram_tensor("y", [T, D], F32, kind="ExternalOutput").ap()

    x_t = x_in.rearrange("(n p) d -> n p d", p=128)
    y_t = x_out.rearrange("(n p) d -> n p d", p=128)

    p = fc["p"] if fc else Prog(nc)

    with ExitStack() as ctx:
        e = ctx.enter_context

        Xbuf = [e(nc.sbuf_tensor(px + f"Xbuf{i}", [128, D], F32)) for i in range(2)]
        Ybuf = [e(nc.sbuf_tensor(px + f"Ybuf{i}", [128, D], F32)) for i in range(2)]
        ssum = e(nc.sbuf_tensor(px + "ssum", [128, NT], F32))
        ssq = e(nc.sbuf_tensor(px + "ssq", [128, NT], F32))
        mcol = e(nc.sbuf_tensor(px + "mcol", [128, NT], F32))
        Ccol = e(nc.sbuf_tensor(px + "Ccol", [128, NT], F32))
        tmp1 = e(nc.sbuf_tensor(px + "tmp1", [128, NT], F32))
        tmp2 = e(nc.sbuf_tensor(px + "tmp2", [128, NT], F32))
        xnbuf = [e(nc.sbuf_tensor(px + f"xnbuf{i}", [128, D], BF16)) for i in range(2)]
        xn_fm = [e(nc.sbuf_tensor(px + f"xnfm{c}", [128, GT], BF16)) for c in range(NCH)]
        yacc = [e(nc.sbuf_tensor(px + f"yacc{t}", [128, D], F32)) for t in range(gNT)]
        g_sb = e(nc.sbuf_tensor(px + "g_sb", [128, 8 * gNT], F32))
        gsc = e(nc.sbuf_tensor(px + "gsc", [128, 8], F32))
        gex = e(nc.sbuf_tensor(px + "gex", [128, 8], F32))
        gm = e(nc.sbuf_tensor(px + "gm", [128, 1], F32))
        gs = e(nc.sbuf_tensor(px + "gs", [128, 1], F32))
        gscr = e(nc.sbuf_tensor(px + "gscr", [128, 8], F32))
        rowsb = e(nc.sbuf_tensor(px + "rowsb", [1, 512], F32))
        ones1f = e(nc.sbuf_tensor(px + "ones1f", [1, 128], F32))
        Ge = e(nc.sbuf_tensor(px + "Ge", [128, 512], BF16))
        xe_fm = [e(nc.sbuf_tensor(px + f"xefm{c}", [128, 512], BF16)) for c in range(NCH)]
        asb = [e(nc.sbuf_tensor(px + f"asb{i}", [128, 512], BF16)) for i in range(2)]
        h_fm = [e(nc.sbuf_tensor(px + f"hfm{c}", [128, 512], BF16)) for c in range(NCH)]
        w1s = e(nc.sbuf_tensor(px + "w1s", [128, NCH, D], BF16))
        w2s = e(nc.sbuf_tensor(px + "w2s", [128, NCH, D], BF16))
        w3s = e(nc.sbuf_tensor(px + "w3s", [128, NCH, D], BF16))
        gwT = e(nc.sbuf_tensor(px + "gwT_s", [128, NCH, 8], BF16))
        identf = e(nc.sbuf_tensor(px + "identf_s", [128, 128], F32))
        identbf = e(nc.sbuf_tensor(px + "identbf_s", [128, 128], BF16))
        ones1 = e(nc.sbuf_tensor(px + "ones1", [1, 128], BF16))
        dumV = e(nc.sbuf_tensor(px + "dumV", [1, 1], F32))
        dumS = e(nc.sbuf_tensor(px + "dumS", [1, 1], F32))
        dumG = e(nc.sbuf_tensor(px + "dumG", [1, 1], F32))

        psumT8 = e(nc.psum_tensor(px + "psumT8", [128, 1024], BF16))
        psL = e(nc.psum_tensor(px + "psL", [128, 512], F32))
        psA = e(nc.psum_tensor(px + "psA", [128, 512], F32))
        psB = e(nc.psum_tensor(px + "psB", [128, 512], F32))
        psC = e(nc.psum_tensor(px + "psC", [128, 512], F32))
        psR = e(nc.psum_tensor(px + "psR", [128, 512], F32))

        sem_names = ["dma_ge", "dma_sy", "dma_st", "bar",
                     "a0", "a1x", "a1t", "a1e",
                     "b_l", "b_g", "b_row", "b_bc", "b_x",
                     "b_a", "b_as", "b_b", "b_h", "b_o", "b_y",
                     "a3v"]
        S = fc["S"] if fc else {n: e(nc.semaphore(n)) for n in sem_names}
        block = None if fc else e(nc.Block())

        p.load("sy", identf[:], identf_d)
        p.load("sy", identbf[:], identbf_d)
        p.load("ge", gwT[:], gw_d.rearrange("(c p) e -> p c e", p=128))

        wcnt = {}

        def load_expert(eidx):
            p.load("ge", w1s[:], w1_d[eidx].rearrange("(c p) n -> p c n", p=128))
            p.load("ge", w3s[:], w3_d[eidx].rearrange("(c p) n -> p c n", p=128))
            wcnt[eidx] = p.load("ge", w2s[:],
                                w2_d[eidx].rearrange("(c p) n -> p c n", p=128))

        load_expert(0)

        def st_memset(key, ap, val):
            def fn(eng, ap=ap, val=val):
                return eng.memset(ap, val)
            p.step(key, fn)

        st_memset("ve", ones1[:], 1.0)
        st_memset("ve", ones1f[:], 1.0)
        st_memset("ve", dumV[:], 0.0)
        st_memset("ge", dumG[:], 0.0)

        def fn_dumS(eng):
            return eng.memzero(dumS[:])
        p.step("se", fn_dumS)

        p.barrier()

        _a0b = p.val("a0")
        for t in range(NT):
            if t >= 2:
                def fn_w(eng, need=_a0b + 2 * (t - 1)):
                    eng.wait_ge(S["a0"], need)
                    return None
                p.step("sy", fn_w)
            lc = p.load("sy", Xbuf[t % 2][:], x_t[t])

            def fn_v(eng, t=t, lc=lc):
                eng.wait_ge(S["dma_sy"], lc)
                return eng.tensor_scalar(
                    out=Ybuf[0][:], in0=Xbuf[t % 2][:], scalar1=1.0, scalar2=0.0,
                    op0=ALU.mult, op1=ALU.add,
                    accum_out=ssum[:, t:t + 1]).then_inc(S["a0"], 1)
            p.step("ve", fn_v)
            p.inc("a0")

            def fn_s(eng, t=t, lc=lc):
                eng.wait_ge(S["dma_sy"], lc)
                return eng.activation(
                    out=Ybuf[1][:], in_=Xbuf[t % 2][:], func=AF.Square,
                    accum_out=ssq[:, t:t + 1]).then_inc(S["a0"], 1)
            p.step("se", fn_s)
            p.inc("a0")

        p.barrier()

        def fn_mc1(eng):
            eng.drain()
            eng.tensor_scalar(out=mcol[:], in0=ssum[:], scalar1=1.0 / D,
                              scalar2=None, op0=ALU.mult, op1=ALU.bypass)
            eng.tensor_scalar(out=tmp1[:], in0=ssq[:], scalar1=1.0 / D,
                              scalar2=None, op0=ALU.mult, op1=ALU.bypass)
            eng.drain()
            eng.tensor_tensor(out=tmp2[:], in0=mcol[:], in1=mcol[:], op=ALU.mult)
            eng.drain()
            eng.tensor_tensor(out=tmp1[:], in0=tmp1[:], in1=tmp2[:], op=ALU.subtract)
            eng.drain()
            return eng.tensor_scalar(out=tmp2[:], in0=tmp1[:], scalar1=1e-5,
                                     scalar2=None, op0=ALU.add, op1=ALU.bypass)
        p.step("ve", fn_mc1)
        p.barrier()

        def fn_mc2(eng):
            return eng.activation(out=tmp1[:], in_=tmp2[:], func=AF.Sqrt)
        p.step("se", fn_mc2)
        p.barrier()

        def fn_mc3(eng):
            return eng.reciprocal(out=Ccol[:], in_=tmp1[:])
        p.step("ve", fn_mc3)
        p.barrier()

        def emit_b1b2(grp):
            for ti in range(gNT):
                t = grp * gNT + ti
                nx = p.val("a1x")
                if nx >= 2:
                    def fn_w(eng, need=nx - 1):
                        eng.wait_ge(S["a1x"], need)
                        return None
                    p.step("sy", fn_w)
                lc = p.load("sy", Xbuf[ti % 2][:], x_t[t])

                _tw = p.val("a1t")

                def fn_v(eng, t=t, ti=ti, lc=lc, tw=_tw):
                    eng.wait_ge(S["dma_sy"], lc)
                    if tw >= 2:
                        eng.wait_ge(S["a1t"], tw - 1)
                    return eng.tensor_scalar(
                        out=xnbuf[ti % 2][:], in0=Xbuf[ti % 2][:],
                        scalar1=mcol[:, t:t + 1], scalar2=Ccol[:, t:t + 1],
                        op0=ALU.subtract, op1=ALU.mult).then_inc(S["a1x"], 1)
                p.step("ve", fn_v)
                x_tgt = p.inc("a1x")

                _ew = p.val("a1e")

                def fn_t(eng, ti=ti, xt=x_tgt, ew=_ew):
                    eng.wait_ge(S["a1x"], xt)
                    if ew >= 1:
                        eng.wait_ge(S["a1e"], ew)
                    r = None
                    for c in range(NCH):
                        r = eng.transpose(psumT8[:, c * 128:(c + 1) * 128],
                                          xnbuf[ti % 2][:, c * 128:(c + 1) * 128],
                                          identbf[:])
                    return r.then_inc(S["a1t"], 1)
                p.step("te", fn_t)
                t_tgt = p.inc("a1t")

                def fn_e(eng, ti=ti, tt=t_tgt):
                    eng.wait_ge(S["a1t"], tt)
                    r = None
                    for c in range(NCH):
                        r = eng.tensor_copy(
                            out=xn_fm[c][:, ti * 128:(ti + 1) * 128],
                            in_=psumT8[:, c * 128:(c + 1) * 128])
                    return r.then_inc(S["a1e"], 1)
                p.step("ve", fn_e)
                p.inc("a1e")

            p.barrier()

            for ti in range(gNT):
                _gw = p.val("b_g")

                def fn_lmm(eng, ti=ti, war=_gw):
                    if war >= 2:
                        eng.wait_ge(S["b_g"], war - 2)
                    r = None
                    for c in range(NCH):
                        r = eng.matmul(psL[:, 0:8],
                                       xn_fm[c][:, ti * 128:(ti + 1) * 128],
                                       gwT[:, c, :], start=(c == 0),
                                       stop=(c == NCH - 1))
                    return r.then_inc(S["b_l"], 1)
                p.step("te", fn_lmm)
                bl_tgt = p.inc("b_l")

                def fn_gate(eng, bl=bl_tgt):
                    eng.wait_ge(S["b_l"], bl)
                    eng.tensor_copy(out=gsc[:], in_=psL[:, 0:8])
                    eng.drain()
                    eng.reduce_max(out=gm[:], in_=gsc[:], axis=AX, negate=False)
                    eng.drain()
                    eng.tensor_scalar(out=gex[:], in0=gsc[:], scalar1=gm[:],
                                      scalar2=None, op0=ALU.subtract, op1=ALU.bypass)
                    eng.drain()
                    return eng.tensor_copy(out=dumV[:], in_=dumV[:]
                                           ).then_inc(S["b_g"], 1)
                p.step("ve", fn_gate)
                gv = p.inc("b_g")

                def fn_sexp(eng, gv=gv):
                    eng.wait_ge(S["b_g"], gv)
                    eng.activation(out=gex[:], in_=gex[:], func=AF.Exp)
                    eng.drain()
                    return eng.copy(out=dumS[:], in_=dumS[:]).then_inc(S["b_g"], 1)
                p.step("se", fn_sexp)
                gv2 = p.inc("b_g")

                def fn_gate3(eng, ti=ti, gv2=gv2):
                    eng.wait_ge(S["b_g"], gv2)
                    eng.tensor_scalar(out=gscr[:], in0=gex[:], scalar1=1.0,
                                      scalar2=0.0, op0=ALU.mult, op1=ALU.add,
                                      accum_out=gs[:])
                    eng.drain()
                    eng.reciprocal(out=gs[:], in_=gs[:])
                    eng.drain()
                    eng.tensor_scalar(out=gsc[:], in0=gex[:], scalar1=gs[:],
                                      scalar2=None, op0=ALU.mult, op1=ALU.bypass)
                    eng.drain()
                    eng.reduce_max(out=gm[:], in_=gsc[:], axis=AX, negate=False)
                    eng.drain()
                    eng.tensor_scalar(out=gscr[:], in0=gsc[:], scalar1=gm[:],
                                      scalar2=None, op0=ALU.is_ge, op1=ALU.bypass)
                    eng.drain()
                    eng.scalar_tensor_tensor(out=gscr[:], in0=gscr[:], scalar=-2.0,
                                             in1=gsc[:], op0=ALU.mult, op1=ALU.add)
                    eng.drain()
                    eng.reduce_max(out=gm[:], in_=gscr[:], axis=AX, negate=False)
                    eng.drain()
                    eng.tensor_scalar(out=gscr[:], in0=gsc[:], scalar1=gm[:],
                                      scalar2=None, op0=ALU.is_ge, op1=ALU.bypass)
                    eng.drain()
                    return eng.tensor_tensor(out=g_sb[:, ti * 8:(ti + 1) * 8],
                                             in0=gsc[:], in1=gscr[:],
                                             op=ALU.mult).then_inc(S["b_g"], 1)
                p.step("ve", fn_gate3)
                p.inc("b_g")

            p.barrier()

        seq = [(g, ei) for g in range(NGRP) for ei in range(9)]
        for si, (grp, ei) in enumerate(seq):
            routed = ei < 8
            if ei == 0:
                emit_b1b2(grp)
                for ti in range(gNT):
                    st_memset("ve", yacc[ti][:], 0.0)
                p.barrier()

            for t8 in range(gn512):
                sl512 = slice(t8 * 512, (t8 + 1) * 512)

                if routed:
                    _xw = p.val("b_x")

                    def fn_row(eng, ei=ei, t8=t8, war=_xw, wc=wcnt[ei]):
                        eng.wait_ge(S["dma_ge"], wc)
                        if war >= 1:
                            eng.wait_ge(S["b_x"], war)
                        r = None
                        for q in range(4):
                            ti = t8 * 4 + q
                            r = eng.matmul(psR[0:1, q * 128:(q + 1) * 128],
                                           g_sb[:, ti * 8 + ei:ti * 8 + ei + 1],
                                           identf[:], start=True, stop=True)
                        return r.then_inc(S["b_row"], 1)
                    p.step("te", fn_row)
                    row_tgt = p.inc("b_row")

                    _bcw = p.val("b_bc")

                    def fn_rowev(eng, tgt=row_tgt, war=_bcw):
                        eng.wait_ge(S["b_row"], tgt)
                        if war >= 1:
                            eng.wait_ge(S["b_bc"], war)
                        eng.tensor_copy(out=rowsb[:], in_=psR[0:1, 0:512])
                        eng.drain()
                        return eng.tensor_copy(out=dumV[:], in_=dumV[:]
                                               ).then_inc(S["b_row"], 1)
                    p.step("ve", fn_rowev)
                    rowev_tgt = p.inc("b_row")

                    _xw2 = p.val("b_x")

                    def fn_bc(eng, tgt=rowev_tgt, war=_xw2):
                        eng.wait_ge(S["b_row"], tgt)
                        if war >= 1:
                            eng.wait_ge(S["b_x"], war)
                        return eng.matmul(psR[:, 0:512], ones1f[:], rowsb[:],
                                          start=True, stop=True).then_inc(S["b_bc"], 1)
                    p.step("te", fn_bc)
                    bc_tgt = p.inc("b_bc")

                    _aw = p.val("b_a"); _bw = p.val("b_b")

                    def fn_xe(eng, tgt=bc_tgt, aw=_aw, bw=_bw, sl512=sl512):
                        eng.wait_ge(S["b_bc"], tgt)
                        if aw >= 1:
                            eng.wait_ge(S["b_a"], aw)
                        if bw >= 1:
                            eng.wait_ge(S["b_b"], bw)
                        eng.tensor_copy(out=Ge[:], in_=psR[:, 0:512])
                        eng.drain()
                        for c in range(NCH):
                            eng.tensor_tensor(out=xe_fm[c][:],
                                              in0=xn_fm[c][:, sl512],
                                              in1=Ge[:], op=ALU.mult)
                        eng.drain()
                        return eng.tensor_copy(out=dumV[:], in_=dumV[:]
                                               ).then_inc(S["b_x"], 1)
                    p.step("ve", fn_xe)
                    p.inc("b_x")

                xw = p.val("b_x")
                for hc in range(NCH):
                    _asw = p.val("b_as")

                    def fn_amm(eng, hc=hc, xw=xw, war=_asw, sl512=sl512,
                               routed=routed, wc=wcnt[ei]):
                        eng.wait_ge(S["dma_ge"], wc)
                        if routed:
                            eng.wait_ge(S["b_x"], xw)
                        if war >= 1:
                            eng.wait_ge(S["b_as"], war)
                        r = None
                        for c in range(NCH):
                            rhs = xe_fm[c][:] if routed else xn_fm[c][:, sl512]
                            r = eng.matmul(psA[:, 0:512],
                                           w1s[:, c, hc * 128:(hc + 1) * 128],
                                           rhs, start=(c == 0), stop=(c == NCH - 1))
                        return r.then_inc(S["b_a"], 1)
                    p.step("te", fn_amm)
                    a_tgt = p.inc("b_a")

                    _hw = p.val("b_h") - 1

                    def fn_silu(eng, hc=hc, a=a_tgt, war=_hw):
                        eng.wait_ge(S["b_a"], a)
                        if war >= 1:
                            eng.wait_ge(S["b_h"], war)
                        eng.activation(out=asb[hc % 2][:], in_=psA[:, 0:512],
                                       func=AF.Silu)
                        eng.drain()
                        return eng.copy(out=dumS[:], in_=dumS[:]
                                        ).then_inc(S["b_as"], 1)
                    p.step("se", fn_silu)
                    as_tgt = p.inc("b_as")

                    _hw2 = p.val("b_h")

                    def fn_bmm(eng, hc=hc, xw=xw, sl512=sl512, routed=routed,
                               war=_hw2):
                        if routed:
                            eng.wait_ge(S["b_x"], xw)
                        if war >= 1:
                            eng.wait_ge(S["b_h"], war)
                        r = None
                        for c in range(NCH):
                            rhs = xe_fm[c][:] if routed else xn_fm[c][:, sl512]
                            r = eng.matmul(psB[:, 0:512],
                                           w3s[:, c, hc * 128:(hc + 1) * 128],
                                           rhs, start=(c == 0), stop=(c == NCH - 1))
                        return r.then_inc(S["b_b"], 1)
                    p.step("te", fn_bmm)
                    b_tgt = p.inc("b_b")

                    _ow = p.val("b_o")

                    def fn_h(eng, hc=hc, a=as_tgt, b=b_tgt, war=_ow):
                        eng.wait_ge(S["b_as"], a)
                        eng.wait_ge(S["b_b"], b)
                        if war >= 1:
                            eng.wait_ge(S["b_o"], war)
                        eng.tensor_tensor(out=h_fm[hc][:], in0=asb[hc % 2][:],
                                          in1=psB[:, 0:512], op=ALU.mult)
                        eng.drain()
                        return eng.tensor_copy(out=dumV[:], in_=dumV[:]
                                               ).then_inc(S["b_h"], 1)
                    p.step("ve", fn_h)
                    p.inc("b_h")

                hdone = p.val("b_h")
                for tt in range(4):
                    ti = t8 * 4 + tt
                    for half in range(2):
                        _yw = p.val("b_y")

                        def fn_omm(eng, tt=tt, half=half, hd=hdone, war=_yw,
                                   wc=wcnt[ei]):
                            eng.wait_ge(S["b_h"], hd)
                            if war >= 1:
                                eng.wait_ge(S["b_y"], war)
                            r = None
                            for hc in range(NCH):
                                r = eng.matmul(
                                    psC[:, 0:512],
                                    h_fm[hc][:, tt * 128:(tt + 1) * 128],
                                    w2s[:, hc, half * 512:(half + 1) * 512],
                                    start=(hc == 0), stop=(hc == NCH - 1))
                            return r.then_inc(S["b_o"], 1)
                        p.step("te", fn_omm)
                        o_tgt = p.inc("b_o")

                        def fn_yadd(eng, ti=ti, half=half, o=o_tgt):
                            eng.wait_ge(S["b_o"], o)
                            return eng.tensor_tensor(
                                out=yacc[ti][:, half * 512:(half + 1) * 512],
                                in0=yacc[ti][:, half * 512:(half + 1) * 512],
                                in1=psC[:, 0:512], op=ALU.add).then_inc(S["b_y"], 1)
                        p.step("ve", fn_yadd)
                        p.inc("b_y")

            if si + 1 < len(seq):
                _b = p.val("b_b"); _o = p.val("b_o")

                def fn_wp(eng, b=_b, o=_o):
                    eng.wait_ge(S["b_b"], b)
                    eng.wait_ge(S["b_o"], o)
                    return None
                p.step("ge", fn_wp)
                load_expert(seq[si + 1][1])

            if ei == 8:
                p.barrier()
                for ti in range(gNT):
                    t = grp * gNT + ti
                    _a3p = p.val("a3v")
                    if _a3p >= 2:
                        def fn_w(eng, need=_a3p - 1):
                            eng.wait_ge(S["a3v"], need)
                            return None
                        p.step("sy", fn_w)
                    lc = p.load("sy", Xbuf[ti % 2][:], x_t[t])
                    _stp = (p.dma_cnt["st"] - 1) * 16

                    def fn_v(eng, t=t, ti=ti, lc=lc, stp=_stp):
                        eng.wait_ge(S["dma_sy"], lc)
                        if stp >= 16:
                            eng.wait_ge(S["dma_st"], stp)
                        return eng.tensor_tensor(out=Ybuf[ti % 2][:],
                                                 in0=Xbuf[ti % 2][:],
                                                 in1=yacc[ti][:],
                                                 op=ALU.add).then_inc(S["a3v"], 1)
                    p.step("ve", fn_v)
                    a3_tgt = p.inc("a3v")

                    def fn_stw(eng, tgt=a3_tgt):
                        eng.wait_ge(S["a3v"], tgt)
                        return None
                    p.step("sy", fn_stw)
                    p.store(y_t[t], Ybuf[ti % 2][:])

                p.barrier()

        if not fc:
            p.finalize(block, S, {"ve": dumV[:], "se": dumS[:], "ge": dumG[:]})

    return nc


def _run_l0_ffn(x, gate_w, w1, w2, w3, w1s, w2s, w3s):
    identf, identbf, _ = _consts()
    w1a = np.concatenate([w1, w1s], axis=0)
    w2a = np.concatenate([w2, w2s], axis=0)
    w3a = np.concatenate([w3, w3s], axis=0)
    if "l0f" not in _CACHE:
        _CACHE["l0f"] = build_l0_ffn()
    nc = _CACHE["l0f"]
    base = {
        "gwT": np.ascontiguousarray(gate_w.T),
        "w1": np.ascontiguousarray(w1a), "w2": np.ascontiguousarray(w2a),
        "w3": np.ascontiguousarray(w3a),
        "identf": identf, "identbf": identbf,
    }
    in_maps = []
    for c in range(8):
        m = dict(base)
        m["x"] = np.ascontiguousarray(x[c])
        in_maps.append(m)
    res = run_bass_kernel_spmd(nc, in_maps, core_ids=list(range(8)))
    return np.stack([np.asarray(r["y"]) for r in res.results], axis=0)



def build_l1_ffn(T=T, NT=NT, fc=None):
    NGRP = 2 if T >= 4096 else 1
    GT = T // NGRP
    gNT = GT // 128
    gn512 = GT // 512
    NH = 16

    nc = fc["nc"] if fc else bass.Bass("TRN2", target_bir_lowering=False, detect_race_conditions=False)
    px = fc["px"] if fc else ""

    x_in = fc["x_in"] if fc else nc.dram_tensor("x", [T, D], F32, kind="ExternalInput").ap()
    rw1_d = nc.dram_tensor("rw1", [D, D], F32, kind="ExternalInput").ap()
    rb1_d = nc.dram_tensor("rb1fm", [128, NCH], F32, kind="ExternalInput").ap()
    rw2_d = nc.dram_tensor("rw2", [D, 8], F32, kind="ExternalInput").ap()
    rb2_d = nc.dram_tensor("rb2row", [1, 8], F32, kind="ExternalInput").ap()
    ew1_d = nc.dram_tensor("ew1", [8, D, 2048], F32, kind="ExternalInput").ap()
    eb1_d = nc.dram_tensor("eb1fm", [8, 128, NH], F32, kind="ExternalInput").ap()
    ew2_d = nc.dram_tensor("ew2", [8, 2048, D], F32, kind="ExternalInput").ap()
    eb2_d = nc.dram_tensor("eb2row", [8, 1, D], F32, kind="ExternalInput").ap()
    identbf_d = _dram_dedup(nc, fc, "identbf", [128, 128], BF16)
    x_out = fc["x_out"] if fc else nc.dram_tensor("y", [T, D], F32, kind="ExternalOutput").ap()

    x_t = x_in.rearrange("(n p) d -> n p d", p=128)
    y_t = x_out.rearrange("(n p) d -> n p d", p=128)

    p = fc["p"] if fc else Prog(nc)

    with ExitStack() as ctx:
        e = ctx.enter_context

        Xbuf = [e(nc.sbuf_tensor(px + f"Xbuf{i}", [128, D], F32)) for i in range(2)]
        Ybuf = [e(nc.sbuf_tensor(px + f"Ybuf{i}", [128, D], F32)) for i in range(2)]
        ssum = e(nc.sbuf_tensor(px + "ssum", [128, NT], F32))
        ssq = e(nc.sbuf_tensor(px + "ssq", [128, NT], F32))
        mcol = e(nc.sbuf_tensor(px + "mcol", [128, NT], F32))
        Ccol = e(nc.sbuf_tensor(px + "Ccol", [128, NT], F32))
        tmp1 = e(nc.sbuf_tensor(px + "tmp1", [128, NT], F32))
        tmp2 = e(nc.sbuf_tensor(px + "tmp2", [128, NT], F32))
        xnbuf = [e(nc.sbuf_tensor(px + f"xnbuf{i}", [128, D], BF16)) for i in range(2)]
        xn_fm = [e(nc.sbuf_tensor(px + f"xnfm{c}", [128, GT], BF16)) for c in range(NCH)]
        yacc = [e(nc.sbuf_tensor(px + f"yacc{t}", [128, D], F32)) for t in range(gNT)]
        g_sb = e(nc.sbuf_tensor(px + "g_sb", [128, 8 * gNT], F32))
        gsc = e(nc.sbuf_tensor(px + "gsc", [128, 8], F32))
        gex = e(nc.sbuf_tensor(px + "gex", [128, 8], F32))
        gm = e(nc.sbuf_tensor(px + "gm", [128, 1], F32))
        gs = e(nc.sbuf_tensor(px + "gs", [128, 1], F32))
        gscr = e(nc.sbuf_tensor(px + "gscr", [128, 8], F32))
        eh_fm = [e(nc.sbuf_tensor(px + f"ehfm{c}", [128, 512], BF16)) for c in range(NH)]
        ew1s = e(nc.sbuf_tensor(px + "ew1s", [128, NCH, 2048], BF16))
        ew2s = e(nc.sbuf_tensor(px + "ew2s", [128, NH, D], BF16))
        eb1s = e(nc.sbuf_tensor(px + "eb1s", [128, NH], F32))
        eb2s = e(nc.sbuf_tensor(px + "eb2s", [1, D], BF16))
        rw2s = e(nc.sbuf_tensor(px + "rw2s", [128, NCH, 8], BF16))
        rb1s = e(nc.sbuf_tensor(px + "rb1s", [128, NCH], F32))
        rb2s = e(nc.sbuf_tensor(px + "rb2s", [1, 8], BF16))
        identbf = e(nc.sbuf_tensor(px + "identbf_s", [128, 128], BF16))
        onescol = e(nc.sbuf_tensor(px + "onescol", [1, 128], BF16))
        dumV = e(nc.sbuf_tensor(px + "dumV", [1, 1], F32))
        dumS = e(nc.sbuf_tensor(px + "dumS", [1, 1], F32))
        dumG = e(nc.sbuf_tensor(px + "dumG", [1, 1], F32))

        psumT8 = e(nc.psum_tensor(px + "psumT8", [128, 1024], BF16))
        psL = e(nc.psum_tensor(px + "psL", [128, 512], F32))
        psA = e(nc.psum_tensor(px + "psA", [128, 512], F32))
        psC = e(nc.psum_tensor(px + "psC", [128, 512], F32))

        sem_names = ["dma_ge", "dma_sy", "dma_st", "bar",
                     "a0", "a1x", "a1t", "a1e",
                     "b_l", "b_g", "b_m", "b_ms",
                     "b_a", "b_as", "b_o", "b_y",
                     "a3v"]
        S = fc["S"] if fc else {n: e(nc.semaphore(n)) for n in sem_names}
        block = None if fc else e(nc.Block())

        p.load("sy", identbf[:], identbf_d)
        p.load("sy", rb1s[:], rb1_d)
        p.load("ge", rb2s[:], rb2_d)
        p.load("ge", rw2s[:], rw2_d.rearrange("(c p) e -> p c e", p=128))
        rw1cnt = p.load("ge", ew1s[:, :, 0:D],
                        rw1_d.rearrange("(c p) n -> p c n", p=128))

        wcnt = {}

        def load_expert(eidx):
            p.load("ge", ew1s[:], ew1_d[eidx].rearrange("(c p) n -> p c n", p=128))
            p.load("sy", eb1s[:], eb1_d[eidx])
            p.load("ge", eb2s[:], eb2_d[eidx])
            wcnt[eidx] = p.load("ge", ew2s[:],
                                ew2_d[eidx].rearrange("(c p) n -> p c n", p=128))

        def st_memset(key, ap, val):
            def fn(eng, ap=ap, val=val):
                return eng.memset(ap, val)
            p.step(key, fn)

        st_memset("ve", onescol[:], 1.0)
        st_memset("ve", dumV[:], 0.0)
        st_memset("ge", dumG[:], 0.0)

        def fn_dumS(eng):
            return eng.memzero(dumS[:])
        p.step("se", fn_dumS)

        p.barrier()

        _a0b = p.val("a0")
        for t in range(NT):
            if t >= 2:
                def fn_w(eng, need=_a0b + 2 * (t - 1)):
                    eng.wait_ge(S["a0"], need)
                    return None
                p.step("sy", fn_w)
            lc = p.load("sy", Xbuf[t % 2][:], x_t[t])

            def fn_v(eng, t=t, lc=lc):
                eng.wait_ge(S["dma_sy"], lc)
                return eng.tensor_scalar(
                    out=Ybuf[0][:], in0=Xbuf[t % 2][:], scalar1=1.0, scalar2=0.0,
                    op0=ALU.mult, op1=ALU.add,
                    accum_out=ssum[:, t:t + 1]).then_inc(S["a0"], 1)
            p.step("ve", fn_v)
            p.inc("a0")

            def fn_s(eng, t=t, lc=lc):
                eng.wait_ge(S["dma_sy"], lc)
                return eng.activation(
                    out=Ybuf[1][:], in_=Xbuf[t % 2][:], func=AF.Square,
                    accum_out=ssq[:, t:t + 1]).then_inc(S["a0"], 1)
            p.step("se", fn_s)
            p.inc("a0")

        p.barrier()

        def fn_mc1(eng):
            eng.drain()
            eng.tensor_scalar(out=mcol[:], in0=ssum[:], scalar1=1.0 / D,
                              scalar2=None, op0=ALU.mult, op1=ALU.bypass)
            eng.tensor_scalar(out=tmp1[:], in0=ssq[:], scalar1=1.0 / D,
                              scalar2=None, op0=ALU.mult, op1=ALU.bypass)
            eng.drain()
            eng.tensor_tensor(out=tmp2[:], in0=mcol[:], in1=mcol[:], op=ALU.mult)
            eng.drain()
            eng.tensor_tensor(out=tmp1[:], in0=tmp1[:], in1=tmp2[:], op=ALU.subtract)
            eng.drain()
            return eng.tensor_scalar(out=tmp2[:], in0=tmp1[:], scalar1=1e-5,
                                     scalar2=None, op0=ALU.add, op1=ALU.bypass)
        p.step("ve", fn_mc1)
        p.barrier()

        def fn_mc2(eng):
            return eng.activation(out=tmp1[:], in_=tmp2[:], func=AF.Sqrt)
        p.step("se", fn_mc2)
        p.barrier()

        def fn_mc3(eng):
            return eng.reciprocal(out=Ccol[:], in_=tmp1[:])
        p.step("ve", fn_mc3)
        p.barrier()

        for grp in range(NGRP):
            for ti in range(gNT):
                t = grp * gNT + ti
                nx = p.val("a1x")
                if nx >= 2:
                    def fn_w(eng, need=nx - 1):
                        eng.wait_ge(S["a1x"], need)
                        return None
                    p.step("sy", fn_w)
                lc = p.load("sy", Xbuf[ti % 2][:], x_t[t])

                _tw = p.val("a1t")

                def fn_v(eng, t=t, ti=ti, lc=lc, tw=_tw):
                    eng.wait_ge(S["dma_sy"], lc)
                    if tw >= 2:
                        eng.wait_ge(S["a1t"], tw - 1)
                    return eng.tensor_scalar(
                        out=xnbuf[ti % 2][:], in0=Xbuf[ti % 2][:],
                        scalar1=mcol[:, t:t + 1], scalar2=Ccol[:, t:t + 1],
                        op0=ALU.subtract, op1=ALU.mult).then_inc(S["a1x"], 1)
                p.step("ve", fn_v)
                x_tgt = p.inc("a1x")

                _ew = p.val("a1e")

                def fn_t(eng, ti=ti, xt=x_tgt, ew=_ew):
                    eng.wait_ge(S["a1x"], xt)
                    if ew >= 1:
                        eng.wait_ge(S["a1e"], ew)
                    r = None
                    for c in range(NCH):
                        r = eng.transpose(psumT8[:, c * 128:(c + 1) * 128],
                                          xnbuf[ti % 2][:, c * 128:(c + 1) * 128],
                                          identbf[:])
                    return r.then_inc(S["a1t"], 1)
                p.step("te", fn_t)
                t_tgt = p.inc("a1t")

                def fn_e(eng, ti=ti, tt=t_tgt):
                    eng.wait_ge(S["a1t"], tt)
                    r = None
                    for c in range(NCH):
                        r = eng.tensor_copy(
                            out=xn_fm[c][:, ti * 128:(ti + 1) * 128],
                            in_=psumT8[:, c * 128:(c + 1) * 128])
                    return r.then_inc(S["a1e"], 1)
                p.step("ve", fn_e)
                p.inc("a1e")

            p.barrier()

            if grp > 0:
                rw1cnt = p.load("ge", ew1s[:, :, 0:D],
                                rw1_d.rearrange("(c p) n -> p c n", p=128))
            for t8 in range(gn512):
                sl512 = slice(t8 * 512, (t8 + 1) * 512)
                for hc in range(NCH):
                    _asw = p.val("b_ms")

                    def fn_mmm(eng, hc=hc, sl512=sl512, rc=rw1cnt, war=_asw):
                        eng.wait_ge(S["dma_ge"], rc)
                        if war >= 1:
                            eng.wait_ge(S["b_ms"], war)
                        r = None
                        for c in range(NCH):
                            r = eng.matmul(psA[:, 0:512],
                                           ew1s[:, c, hc * 128:(hc + 1) * 128],
                                           xn_fm[c][:, sl512],
                                           start=(c == 0), stop=(c == NCH - 1))
                        return r.then_inc(S["b_m"], 1)
                    p.step("te", fn_mmm)
                    m_tgt = p.inc("b_m")

                    def fn_msilu(eng, hc=hc, m=m_tgt):
                        eng.wait_ge(S["b_m"], m)
                        eng.activation(out=eh_fm[hc][:], in_=psA[:, 0:512],
                                       func=AF.Silu, bias=rb1s[:, hc:hc + 1])
                        eng.drain()
                        return eng.copy(out=dumS[:], in_=dumS[:]
                                        ).then_inc(S["b_ms"], 1)
                    p.step("se", fn_msilu)
                    p.inc("b_ms")

                mid_done = p.val("b_ms")
                for tt in range(4):
                    ti = t8 * 4 + tt
                    _gw = p.val("b_g")

                    def fn_lmm(eng, tt=tt, md=mid_done, war=_gw):
                        eng.wait_ge(S["b_ms"], md)
                        if war >= 2:
                            eng.wait_ge(S["b_g"], war - 2)
                        r = None
                        for c in range(NCH):
                            eng.matmul(psL[:, 0:8],
                                       eh_fm[c][:, tt * 128:(tt + 1) * 128],
                                       rw2s[:, c, :], start=(c == 0), stop=False)
                        r = eng.matmul(psL[:, 0:8], onescol[:], rb2s[:],
                                       start=False, stop=True)
                        return r.then_inc(S["b_l"], 1)
                    p.step("te", fn_lmm)
                    bl_tgt = p.inc("b_l")

                    def fn_gate(eng, bl=bl_tgt):
                        eng.wait_ge(S["b_l"], bl)
                        eng.tensor_copy(out=gsc[:], in_=psL[:, 0:8])
                        eng.drain()
                        eng.reduce_max(out=gm[:], in_=gsc[:], axis=AX, negate=False)
                        eng.drain()
                        eng.tensor_scalar(out=gex[:], in0=gsc[:], scalar1=gm[:],
                                          scalar2=None, op0=ALU.subtract,
                                          op1=ALU.bypass)
                        eng.drain()
                        eng.tensor_scalar(out=gscr[:], in0=gsc[:], scalar1=gm[:],
                                          scalar2=None, op0=ALU.is_ge, op1=ALU.bypass)
                        eng.drain()
                        eng.scalar_tensor_tensor(out=gscr[:], in0=gscr[:],
                                                 scalar=-1e9, in1=gsc[:],
                                                 op0=ALU.mult, op1=ALU.add)
                        eng.drain()
                        eng.reduce_max(out=gm[:], in_=gscr[:], axis=AX, negate=False)
                        eng.drain()
                        eng.tensor_scalar(out=gscr[:], in0=gsc[:], scalar1=gm[:],
                                          scalar2=None, op0=ALU.is_ge, op1=ALU.bypass)
                        eng.drain()
                        return eng.tensor_copy(out=dumV[:], in_=dumV[:]
                                               ).then_inc(S["b_g"], 1)
                    p.step("ve", fn_gate)
                    gv = p.inc("b_g")

                    def fn_sexp(eng, gv=gv):
                        eng.wait_ge(S["b_g"], gv)
                        eng.activation(out=gex[:], in_=gex[:], func=AF.Exp)
                        eng.drain()
                        return eng.copy(out=dumS[:], in_=dumS[:]).then_inc(S["b_g"], 1)
                    p.step("se", fn_sexp)
                    gv2 = p.inc("b_g")

                    def fn_gate3(eng, ti=ti, gv2=gv2):
                        eng.wait_ge(S["b_g"], gv2)
                        eng.tensor_tensor(out=gex[:], in0=gex[:], in1=gscr[:],
                                          op=ALU.mult)
                        eng.drain()
                        eng.tensor_scalar(out=gscr[:], in0=gex[:], scalar1=1.0,
                                          scalar2=0.0, op0=ALU.mult, op1=ALU.add,
                                          accum_out=gs[:])
                        eng.drain()
                        eng.reciprocal(out=gs[:], in_=gs[:])
                        eng.drain()
                        return eng.tensor_scalar(out=g_sb[:, ti * 8:(ti + 1) * 8],
                                                 in0=gex[:], scalar1=gs[:],
                                                 scalar2=None, op0=ALU.mult,
                                                 op1=ALU.bypass).then_inc(S["b_g"], 1)
                    p.step("ve", fn_gate3)
                    p.inc("b_g")

            for ti in range(gNT):
                st_memset("ve", yacc[ti][:], 0.0)
            p.barrier()

            for ei in range(8):
                _a = p.val("b_a"); _o = p.val("b_o"); _l = p.val("b_l")
                _as_ = p.val("b_as")

                def fn_wp(eng, a=_a, o=_o, l=_l, as_=_as_):
                    if a: eng.wait_ge(S["b_a"], a)
                    if o: eng.wait_ge(S["b_o"], o)
                    if l: eng.wait_ge(S["b_l"], l)
                    if as_: eng.wait_ge(S["b_as"], as_)
                    return None
                p.step("ge", fn_wp)
                def fn_wp_sy(eng, as_=_as_):
                    if as_: eng.wait_ge(S["b_as"], as_)
                    return None
                p.step("sy", fn_wp_sy)
                load_expert(ei)
                ebcnt = p.dma_cnt["sy"] * 16

                for t8 in range(gn512):
                    sl512 = slice(t8 * 512, (t8 + 1) * 512)
                    for hc in range(NH):
                        _asw = p.val("b_as")

                        def fn_amm(eng, hc=hc, sl512=sl512, wc=wcnt[ei],
                                   war=_asw):
                            eng.wait_ge(S["dma_ge"], wc)
                            if war >= 1:
                                eng.wait_ge(S["b_as"], war)
                            r = None
                            for c in range(NCH):
                                r = eng.matmul(psA[:, 0:512],
                                               ew1s[:, c, hc * 128:(hc + 1) * 128],
                                               xn_fm[c][:, sl512],
                                               start=(c == 0), stop=(c == NCH - 1))
                            return r.then_inc(S["b_a"], 1)
                        p.step("te", fn_amm)
                        a_tgt = p.inc("b_a")

                        _ow = p.val("b_o")

                        def fn_silu(eng, hc=hc, a=a_tgt, eb=ebcnt, war=_ow):
                            eng.wait_ge(S["b_a"], a)
                            eng.wait_ge(S["dma_sy"], eb)
                            if war >= 1:
                                eng.wait_ge(S["b_o"], war)
                            eng.activation(out=eh_fm[hc][:], in_=psA[:, 0:512],
                                           func=AF.Silu, bias=eb1s[:, hc:hc + 1])
                            eng.drain()
                            return eng.copy(out=dumS[:], in_=dumS[:]
                                            ).then_inc(S["b_as"], 1)
                        p.step("se", fn_silu)
                        p.inc("b_as")

                    hdone = p.val("b_as")
                    for tt in range(4):
                        ti = t8 * 4 + tt
                        for half in range(2):
                            _yw = p.val("b_y")

                            def fn_omm(eng, tt=tt, half=half, hd=hdone,
                                       war=_yw, wc=wcnt[ei]):
                                eng.wait_ge(S["b_as"], hd)
                                if war >= 1:
                                    eng.wait_ge(S["b_y"], war)
                                r = None
                                for hc in range(NH):
                                    eng.matmul(
                                        psC[:, 0:512],
                                        eh_fm[hc][:, tt * 128:(tt + 1) * 128],
                                        ew2s[:, hc, half * 512:(half + 1) * 512],
                                        start=(hc == 0), stop=False)
                                r = eng.matmul(psC[:, 0:512], onescol[:],
                                               eb2s[:, half * 512:(half + 1) * 512],
                                               start=False, stop=True)
                                return r.then_inc(S["b_o"], 1)
                            p.step("te", fn_omm)
                            o_tgt = p.inc("b_o")

                            def fn_yadd(eng, ti=ti, half=half, o=o_tgt, ei=ei):
                                eng.wait_ge(S["b_o"], o)
                                return eng.scalar_tensor_tensor(
                                    out=yacc[ti][:, half * 512:(half + 1) * 512],
                                    in0=psC[:, 0:512],
                                    scalar=g_sb[:, ti * 8 + ei:ti * 8 + ei + 1],
                                    in1=yacc[ti][:, half * 512:(half + 1) * 512],
                                    op0=ALU.mult, op1=ALU.add).then_inc(S["b_y"], 1)
                            p.step("ve", fn_yadd)
                            p.inc("b_y")

            p.barrier()

            for ti in range(gNT):
                t = grp * gNT + ti
                _a3p = p.val("a3v")
                if _a3p >= 2:
                    def fn_w(eng, need=_a3p - 1):
                        eng.wait_ge(S["a3v"], need)
                        return None
                    p.step("sy", fn_w)
                lc = p.load("sy", Xbuf[ti % 2][:], x_t[t])
                _stp = (p.dma_cnt["st"] - 1) * 16

                def fn_v(eng, t=t, ti=ti, lc=lc, stp=_stp):
                    eng.wait_ge(S["dma_sy"], lc)
                    if stp >= 16:
                        eng.wait_ge(S["dma_st"], stp)
                    eng.tensor_scalar(out=Ybuf[ti % 2][:], in0=Xbuf[ti % 2][:],
                                      scalar1=mcol[:, t:t + 1],
                                      scalar2=Ccol[:, t:t + 1],
                                      op0=ALU.subtract, op1=ALU.mult)
                    eng.drain()
                    eng.tensor_tensor(out=Ybuf[ti % 2][:], in0=Ybuf[ti % 2][:],
                                      in1=Xbuf[ti % 2][:], op=ALU.add)
                    eng.drain()
                    return eng.tensor_tensor(out=Ybuf[ti % 2][:],
                                             in0=Ybuf[ti % 2][:],
                                             in1=yacc[ti][:],
                                             op=ALU.add).then_inc(S["a3v"], 1)
                p.step("ve", fn_v)
                a3_tgt = p.inc("a3v")

                def fn_stw(eng, tgt=a3_tgt):
                    eng.wait_ge(S["a3v"], tgt)
                    return None
                p.step("sy", fn_stw)
                p.store(y_t[t], Ybuf[ti % 2][:])

            p.barrier()

        if not fc:
            p.finalize(block, S, {"ve": dumV[:], "se": dumS[:], "ge": dumG[:]})

    return nc


def _run_l1_ffn(x, rw1, rb1, rw2, rb2, ew1, eb1, ew2, eb2):
    identf, identbf, _ = _consts()
    if "l1f" not in _CACHE:
        _CACHE["l1f"] = build_l1_ffn()
    nc = _CACHE["l1f"]
    base = {
        "rw1": np.ascontiguousarray(rw1),
        "rb1fm": np.ascontiguousarray(rb1.reshape(NCH, 128).T),
        "rw2": np.ascontiguousarray(rw2),
        "rb2row": np.ascontiguousarray(rb2.reshape(1, 8)),
        "ew1": np.ascontiguousarray(ew1),
        "eb1fm": np.ascontiguousarray(eb1.reshape(8, 16, 128).transpose(0, 2, 1)),
        "ew2": np.ascontiguousarray(ew2),
        "eb2row": np.ascontiguousarray(eb2.reshape(8, 1, D)),
        "identbf": identbf,
    }
    in_maps = []
    for c in range(8):
        m = dict(base)
        m["x"] = np.ascontiguousarray(x[c])
        in_maps.append(m)
    res = run_bass_kernel_spmd(nc, in_maps, core_ids=list(range(8)))
    return np.stack([np.asarray(r["y"]) for r in res.results], axis=0)



def build_l1_attn(T=T, NT=NT, fc=None):
    GS = 512
    NG = T // GS
    CH = 128
    NCK = GS // CH
    H = 16

    nc = fc["nc"] if fc else bass.Bass("TRN2", target_bir_lowering=False, detect_race_conditions=False)
    px = fc["px"] if fc else ""

    x_in = fc["x_in"] if fc else nc.dram_tensor("x", [T, D], F32, kind="ExternalInput").ap()
    wqT_d = nc.dram_tensor("wqT", [D, D], F32, kind="ExternalInput").ap()
    wkT_d = nc.dram_tensor("wkT", [D, D], F32, kind="ExternalInput").ap()
    wvT_d = nc.dram_tensor("wvT", [D, 2048], F32, kind="ExternalInput").ap()
    waT_d = nc.dram_tensor("waT", [D, H], F32, kind="ExternalInput").ap()
    wbT_d = nc.dram_tensor("wbT", [D, H], F32, kind="ExternalInput").ap()
    woT_d = nc.dram_tensor("woT", [2048, D], F32, kind="ExternalInput").ap()
    cq_d = nc.dram_tensor("cqfm", [128, NCH, 4], F32, kind="ExternalInput").ap()
    ck_d = nc.dram_tensor("ckfm", [128, NCH, 4], F32, kind="ExternalInput").ap()
    cv_d = nc.dram_tensor("cvfm", [128, 16, 4], F32, kind="ExternalInput").ap()
    alog_d = nc.dram_tensor("alogrow", [1, H], F32, kind="ExternalInput").ap()
    dtb_d = nc.dram_tensor("dtbcol", [H, 1], F32, kind="ExternalInput").ap()
    identf_d = _dram_dedup(nc, fc, "identf", [128, 128], F32)
    identbf_d = _dram_dedup(nc, fc, "identbf", [128, 128], BF16)
    epat_d = nc.dram_tensor("epat", [H, H * 128], F32, kind="ExternalInput").ap()
    lmask_d = nc.dram_tensor("lmask", [128, 128], F32, kind="ExternalInput").ap()
    smask_d = nc.dram_tensor("smask", [128, 128], F32, kind="ExternalInput").ap()
    uti_d = nc.dram_tensor("uti", [128, 128], F32, kind="ExternalInput").ap()
    uts_d = nc.dram_tensor("uts", [128, 128], F32, kind="ExternalInput").ap()
    bd2_d = nc.dram_tensor("bd2", [128, 2], BF16, kind="ExternalInput").ap()
    hd2_d = nc.dram_tensor("hd2", [2, 128], F32, kind="ExternalInput").ap()
    elast_d = nc.dram_tensor("elast", [128, 1], F32, kind="ExternalInput").ap()
    x_out = fc["x_out"] if fc else nc.dram_tensor("y", [T, D], F32, kind="ExternalOutput").ap()

    x_t = x_in.rearrange("(n p) d -> n p d", p=128)
    y_t = x_out.rearrange("(n p) d -> n p d", p=128)

    p = fc["p"] if fc else Prog(nc)

    with ExitStack() as ctx:
        e = ctx.enter_context

        Xbuf = [e(nc.sbuf_tensor(px + f"Xbuf{i}", [128, D], F32)) for i in range(2)]
        Ybuf = [e(nc.sbuf_tensor(px + f"Ybuf{i}", [128, D], F32)) for i in range(2)]
        ssum = e(nc.sbuf_tensor(px + "ssum", [128, NT], F32))
        ssq = e(nc.sbuf_tensor(px + "ssq", [128, NT], F32))
        mcol = e(nc.sbuf_tensor(px + "mcol", [128, NT], F32))
        Ccol = e(nc.sbuf_tensor(px + "Ccol", [128, NT], F32))
        tmp1 = e(nc.sbuf_tensor(px + "tmp1", [128, NT], F32))
        tmp2 = e(nc.sbuf_tensor(px + "tmp2", [128, NT], F32))
        xnbuf = [e(nc.sbuf_tensor(px + f"xnbuf{i}", [128, D], BF16)) for i in range(2)]
        xn_fm = [e(nc.sbuf_tensor(px + f"xnfm{c}", [128, GS], BF16)) for c in range(NCH)]
        wbig = e(nc.sbuf_tensor(px + "wbig", [128, NCH, 2048], BF16))
        woT = e(nc.sbuf_tensor(px + "woT_s", [128, 16, D], BF16))
        wabT = e(nc.sbuf_tensor(px + "wabT_s", [128, NCH, 2 * H], BF16))
        cqfm = e(nc.sbuf_tensor(px + "cqfm_s", [128, NCH, 4], F32))
        ckfm = e(nc.sbuf_tensor(px + "ckfm_s", [128, NCH, 4], F32))
        cvfm = e(nc.sbuf_tensor(px + "cvfm_s", [128, 16, 4], F32))
        qtail = e(nc.sbuf_tensor(px + "qtail", [128, NCH, 3], F32))
        ktail = e(nc.sbuf_tensor(px + "ktail", [128, NCH, 3], F32))
        vtail = e(nc.sbuf_tensor(px + "vtail", [128, 16, 3], F32))
        qpre = [e(nc.sbuf_tensor(px + f"qpre{c}", [128, GS], F32)) for c in range(NCH)]
        qtmp = e(nc.sbuf_tensor(px + "qtmp", [128, GS], F32))
        qtmp2 = e(nc.sbuf_tensor(px + "qtmp2", [128, GS], BF16))
        q_fm = [e(nc.sbuf_tensor(px + f"qfm{c}", [128, GS], BF16)) for c in range(NCH)]
        k_fm = [e(nc.sbuf_tensor(px + f"kfm{c}", [128, GS], BF16)) for c in range(NCH)]
        v_fm = [e(nc.sbuf_tensor(px + f"vfm{c}", [128, GS], BF16)) for c in range(16)]
        nrm2 = e(nc.sbuf_tensor(px + "nrm2", [2, GS], F32))
        alin = e(nc.sbuf_tensor(px + "alin", [16, GS], F32))
        blin = e(nc.sbuf_tensor(px + "blin", [16, GS], F32))
        gsc1 = e(nc.sbuf_tensor(px + "gsc1", [16, GS], F32))
        gsc2 = e(nc.sbuf_tensor(px + "gsc2", [16, GS], F32))
        loga = e(nc.sbuf_tensor(px + "loga", [16, GS], F32))
        lnc = e(nc.sbuf_tensor(px + "lnc", [16, GS], F32))
        theta = e(nc.sbuf_tensor(px + "theta", [16, 1], F32))
        dtbc = e(nc.sbuf_tensor(px + "dtbc", [16, 1], F32))
        LT = e(nc.sbuf_tensor(px + "LT", [128, H], F32))
        LpT = e(nc.sbuf_tensor(px + "LpT", [128, H], F32))
        vT = e(nc.sbuf_tensor(px + "vT", [128, H], F32))
        rT = e(nc.sbuf_tensor(px + "rT", [128, H], F32))
        gT = e(nc.sbuf_tensor(px + "gT", [128, H], F32))
        wT = e(nc.sbuf_tensor(px + "wT", [128, H], F32))
        LCbc = e(nc.sbuf_tensor(px + "LCbc", [128, H], F32))
        negLT = e(nc.sbuf_tensor(px + "negLT", [128, H], F32))
        logaT = e(nc.sbuf_tensor(px + "logaT", [128, H], F32))
        lncT = e(nc.sbuf_tensor(px + "lncT", [128, H], F32))
        LCrow = e(nc.sbuf_tensor(px + "LCrow", [1, H], F32))
        gCc = e(nc.sbuf_tensor(px + "gCc", [128, 1], F32))
        vh = e(nc.sbuf_tensor(px + "vh", [16, CH], F32))
        negLh = e(nc.sbuf_tensor(px + "negLh", [16, CH], F32))
        Lh = e(nc.sbuf_tensor(px + "Lh", [16, CH], F32))
        Eex = e(nc.sbuf_tensor(px + "Eex", [128, CH], F32))
        Eex2 = e(nc.sbuf_tensor(px + "Eex2", [128, CH], F32))
        Msb = e(nc.sbuf_tensor(px + "Msb", [128, CH], BF16))
        Mtsb = e(nc.sbuf_tensor(px + "Mtsb", [128, CH], BF16))
        Msq = e(nc.sbuf_tensor(px + "Msq", [128, CH], BF16))
        Mtsq = e(nc.sbuf_tensor(px + "Mtsq", [128, CH], BF16))
        Pa = e(nc.sbuf_tensor(px + "Pa", [128, CH], BF16))
        Pb = e(nc.sbuf_tensor(px + "Pb", [128, CH], BF16))
        Btsb = e(nc.sbuf_tensor(px + "Btsb", [128, CH], BF16))
        Rsb = e(nc.sbuf_tensor(px + "Rsb", [128, CH], BF16))
        Usb = e(nc.sbuf_tensor(px + "Usb", [128, CH], BF16))
        Uw = e(nc.sbuf_tensor(px + "Uw", [128, CH], BF16))
        Osb = e(nc.sbuf_tensor(px + "Osb", [128, CH], F32))
        Oscr = e(nc.sbuf_tensor(px + "Oscr", [128, CH], F32))
        Vtok = e(nc.sbuf_tensor(px + "Vtok", [128, CH], BF16))
        Ktok = e(nc.sbuf_tensor(px + "Ktok", [128, 128], BF16))
        S0sc = e(nc.sbuf_tensor(px + "S0sc", [128, CH], F32))
        oms = e(nc.sbuf_tensor(px + "oms", [128, 1], F32))
        Ssb = e(nc.sbuf_tensor(px + "Ssb", [128, H * 128], F32))
        S0bf = e(nc.sbuf_tensor(px + "S0bf", [128, H * 128], BF16))
        o_fm = [e(nc.sbuf_tensor(px + f"ofm{h}", [128, GS], BF16)) for h in range(16)]
        identf = e(nc.sbuf_tensor(px + "identf_s", [128, 128], F32))
        identbf = e(nc.sbuf_tensor(px + "identbf_s", [128, 128], BF16))
        epat = e(nc.sbuf_tensor(px + "epat_s", [H, H * 128], F32))
        lmask = e(nc.sbuf_tensor(px + "lmask_s", [128, 128], F32))
        smask = e(nc.sbuf_tensor(px + "smask_s", [128, 128], F32))
        uti = e(nc.sbuf_tensor(px + "uti_s", [128, 128], F32))
        uts = e(nc.sbuf_tensor(px + "uts_s", [128, 128], F32))
        bd2 = e(nc.sbuf_tensor(px + "bd2_s", [128, 2], BF16))
        hd2 = e(nc.sbuf_tensor(px + "hd2_s", [2, 128], F32))
        elast = e(nc.sbuf_tensor(px + "elast_s", [128, 1], F32))
        alogr = e(nc.sbuf_tensor(px + "alogr", [1, H], F32))
        onesf = e(nc.sbuf_tensor(px + "onesf", [1, 128], F32))
        dumV = e(nc.sbuf_tensor(px + "dumV", [1, 1], F32))
        dumS = e(nc.sbuf_tensor(px + "dumS", [1, 1], F32))
        dumG = e(nc.sbuf_tensor(px + "dumG", [1, 1], F32))

        psTf = e(nc.psum_tensor(px + "psTf", [128, 512], F32))
        psTb = e(nc.psum_tensor(px + "psTb", [128, 1024], BF16))
        psD1 = e(nc.psum_tensor(px + "psD1", [128, 512], F32))
        psD2 = e(nc.psum_tensor(px + "psD2", [128, 512], F32))
        psN1 = e(nc.psum_tensor(px + "psN1", [128, 512], F32))
        psN2 = e(nc.psum_tensor(px + "psN2", [128, 512], F32))
        psP = e(nc.psum_tensor(px + "psP", [128, 512], F32))
        psG = e(nc.psum_tensor(px + "psG", [128, 512], F32))

        sem_names = ["dma_ge", "dma_sy", "dma_st", "bar",
                     "a0", "a1x", "a1t", "a1e",
                     "c_pm", "c_pe", "c_cv", "c_sl", "c_nm", "c_nv",
                     "c_gm", "c_gv", "c_gs",
                     "c_tr", "c_tv", "c_cs",
                     "c_kk", "c_dd", "c_de", "c_dm",
                     "c_nmm", "c_nev", "c_pp",
                     "c_am", "c_av", "c_as",
                     "c_ot", "c_oe", "c_sb",
                     "c_op", "a3v"]
        S = fc["S"] if fc else {n: e(nc.semaphore(n)) for n in sem_names}
        block = None if fc else e(nc.Block())

        def st_memset(key, ap, val):
            def fn(eng, ap=ap, val=val):
                return eng.memset(ap, val)
            p.step(key, fn)

        p.load("sy", identf[:], identf_d)
        p.load("sy", identbf[:], identbf_d)
        p.load("sy", epat[:], epat_d)
        p.load("sy", lmask[:], lmask_d)
        p.load("sy", smask[:], smask_d)
        p.load("sy", uti[:], uti_d)
        p.load("sy", uts[:], uts_d)
        p.load("sy", bd2[:], bd2_d)
        p.load("sy", hd2[:], hd2_d)
        p.load("sy", elast[:], elast_d)
        p.load("sy", alogr[:], alog_d)
        p.load("sy", dtbc[:], dtb_d)
        p.load("sy", cqfm[:], cq_d)
        p.load("sy", ckfm[:], ck_d)
        p.load("sy", cvfm[:], cv_d)
        p.load("ge", wabT[:, :, 0:H], waT_d.rearrange("(c p) h -> p c h", p=128))
        p.load("ge", wabT[:, :, H:2 * H], wbT_d.rearrange("(c p) h -> p c h", p=128))
        p.load("ge", woT[:], woT_d.rearrange("(c p) n -> p c n", p=128))
        st_memset("ve", dumV[:], 0.0)
        st_memset("ve", onesf[:], 1.0)
        st_memset("ge", dumG[:], 0.0)
        st_memset("ve", Ssb[:], 0.0)
        st_memset("ve", S0bf[:], 0.0)
        st_memset("ve", qtail[:], 0.0)
        st_memset("ve", ktail[:], 0.0)
        st_memset("ve", vtail[:], 0.0)

        def fn_dumS(eng):
            return eng.memzero(dumS[:])
        p.step("se", fn_dumS)

        p.barrier()

        def fn_th2(eng):
            return eng.transpose(psG[0:H, 0:1], alogr[:], identf[0:1, 0:1]
                                 ).then_inc(S["c_gm"], 1)
        p.step("te", fn_th2)
        p.inc("c_gm")

        def fn_th3(eng):
            eng.wait_ge(S["c_gm"], 1)
            eng.activation(out=theta[:], in_=psG[0:H, 0:1], func=AF.Exp)
            eng.drain()
            return eng.copy(out=dumS[:], in_=dumS[:]).then_inc(S["c_gs"], 1)
        p.step("se", fn_th3)
        p.inc("c_gs")

        p.barrier()

        _a0b = p.val("a0")
        for t in range(NT):
            if t >= 2:
                def fn_w(eng, need=_a0b + 2 * (t - 1)):
                    eng.wait_ge(S["a0"], need)
                    return None
                p.step("sy", fn_w)
            lc = p.load("sy", Xbuf[t % 2][:], x_t[t])

            def fn_v(eng, t=t, lc=lc):
                eng.wait_ge(S["dma_sy"], lc)
                return eng.tensor_scalar(
                    out=Ybuf[0][:], in0=Xbuf[t % 2][:], scalar1=1.0, scalar2=0.0,
                    op0=ALU.mult, op1=ALU.add,
                    accum_out=ssum[:, t:t + 1]).then_inc(S["a0"], 1)
            p.step("ve", fn_v)
            p.inc("a0")

            def fn_s(eng, t=t, lc=lc):
                eng.wait_ge(S["dma_sy"], lc)
                return eng.activation(
                    out=Ybuf[1][:], in_=Xbuf[t % 2][:], func=AF.Square,
                    accum_out=ssq[:, t:t + 1]).then_inc(S["a0"], 1)
            p.step("se", fn_s)
            p.inc("a0")

        p.barrier()

        def fn_mc1(eng):
            eng.drain()
            eng.tensor_scalar(out=mcol[:], in0=ssum[:], scalar1=1.0 / D,
                              scalar2=None, op0=ALU.mult, op1=ALU.bypass)
            eng.tensor_scalar(out=tmp1[:], in0=ssq[:], scalar1=1.0 / D,
                              scalar2=None, op0=ALU.mult, op1=ALU.bypass)
            eng.drain()
            eng.tensor_tensor(out=tmp2[:], in0=mcol[:], in1=mcol[:], op=ALU.mult)
            eng.drain()
            eng.tensor_tensor(out=tmp1[:], in0=tmp1[:], in1=tmp2[:], op=ALU.subtract)
            eng.drain()
            return eng.tensor_scalar(out=tmp2[:], in0=tmp1[:], scalar1=1e-5,
                                     scalar2=None, op0=ALU.add, op1=ALU.bypass)
        p.step("ve", fn_mc1)
        p.barrier()

        def fn_mc2(eng):
            return eng.activation(out=tmp1[:], in_=tmp2[:], func=AF.Sqrt)
        p.step("se", fn_mc2)
        p.barrier()

        def fn_mc3(eng):
            return eng.reciprocal(out=Ccol[:], in_=tmp1[:])
        p.step("ve", fn_mc3)
        p.barrier()

        def emit_chunk(g, ci):
            csl = slice(ci * CH, (ci + 1) * CH)
            _pdd = p.val("c_dd")
            _pas = p.val("c_as")
            _psb = p.val("c_sb")

            _tv = p.val("c_tv")
            _poe = p.val("c_oe")

            def fn_tr1(eng, csl=csl, war=_tv, poe=_poe):
                if war >= 1:
                    eng.wait_ge(S["c_tv"], war)
                if poe >= 1:
                    eng.wait_ge(S["c_oe"], poe)
                eng.transpose(psTf[:, 0:16], loga[:, csl], identf[0:16, 0:16])
                return eng.transpose(psTf[:, 16:32], lnc[:, csl],
                                     identf[0:16, 0:16]).then_inc(S["c_tr"], 1)
            p.step("te", fn_tr1)
            tr1 = p.inc("c_tr")

            def fn_tv1(eng, tr=tr1):
                eng.wait_ge(S["c_tr"], tr)
                eng.tensor_copy(out=logaT[:], in_=psTf[:, 0:16])
                eng.tensor_copy(out=lncT[:], in_=psTf[:, 16:32])
                eng.drain()
                return eng.tensor_copy(out=dumV[:], in_=dumV[:]).then_inc(S["c_tv"], 1)
            p.step("ve", fn_tv1)
            tv1 = p.inc("c_tv")

            def fn_cu(eng, tv=tv1):
                eng.wait_ge(S["c_tv"], tv)
                eng.matmul(psTf[:, 32:48], uti[:], logaT[:], start=True, stop=True)
                return eng.matmul(psTf[:, 48:64], uts[:], logaT[:], start=True,
                                  stop=True).then_inc(S["c_tr"], 1)
            p.step("te", fn_cu)
            cu1 = p.inc("c_tr")

            def fn_cv1(eng, cu=cu1):
                eng.wait_ge(S["c_tr"], cu)
                eng.tensor_copy(out=LT[:], in_=psTf[:, 32:48])
                eng.tensor_copy(out=LpT[:], in_=psTf[:, 48:64])
                eng.drain()
                eng.tensor_scalar(out=negLT[:], in0=LT[:], scalar1=-1.0,
                                  scalar2=None, op0=ALU.mult, op1=ALU.bypass)
                eng.tensor_tensor(out=vT[:], in0=LpT[:], in1=lncT[:], op=ALU.add)
                eng.drain()
                return eng.tensor_copy(out=dumV[:], in_=dumV[:]).then_inc(S["c_tv"], 1)
            p.step("ve", fn_cv1)
            cv1 = p.inc("c_tv")

            def fn_lc(eng, tv=cv1):
                eng.wait_ge(S["c_tv"], tv)
                eng.matmul(psTf[0:1, 80:96], elast[:], LT[:], start=True, stop=True)
                eng.transpose(psTf[0:16, 96:224], LT[:], identf[:])
                eng.transpose(psTf[0:16, 224:352], negLT[:], identf[:])
                return eng.transpose(psTf[0:16, 352:480], vT[:],
                                     identf[:]).then_inc(S["c_tr"], 1)
            p.step("te", fn_lc)
            lc1 = p.inc("c_tr")

            def fn_lcv(eng, tr=lc1, pdd=_pdd):
                eng.wait_ge(S["c_tr"], tr)
                if pdd >= 1:
                    eng.wait_ge(S["c_dd"], pdd)
                eng.tensor_copy(out=LCrow[:], in_=psTf[0:1, 80:96])
                eng.tensor_copy(out=Lh[:], in_=psTf[0:16, 96:224])
                eng.tensor_copy(out=negLh[:], in_=psTf[0:16, 224:352])
                eng.tensor_copy(out=vh[:], in_=psTf[0:16, 352:480])
                eng.drain()
                return eng.tensor_copy(out=dumV[:], in_=dumV[:]).then_inc(S["c_tv"], 1)
            p.step("ve", fn_lcv)
            lcv = p.inc("c_tv")

            def fn_lbc(eng, tv=lcv):
                eng.wait_ge(S["c_tv"], tv)
                return eng.matmul(psTf[:, 480:496], onesf[:], LCrow[:],
                                  start=True, stop=True).then_inc(S["c_tr"], 1)
            p.step("te", fn_lbc)
            lbc = p.inc("c_tr")

            def fn_lbcv(eng, tr=lbc, pas=_pas, psb=_psb):
                eng.wait_ge(S["c_tr"], tr)
                if pas >= 1:
                    eng.wait_ge(S["c_as"], pas)
                if psb >= 1:
                    eng.wait_ge(S["c_sb"], psb)
                eng.tensor_copy(out=LCbc[:], in_=psTf[:, 480:496])
                eng.drain()
                eng.tensor_tensor(out=wT[:], in0=LCbc[:], in1=LT[:], op=ALU.subtract)
                eng.drain()
                return eng.tensor_copy(out=dumV[:], in_=dumV[:]).then_inc(S["c_tv"], 1)
            p.step("ve", fn_lbcv)
            lbcv = p.inc("c_tv")

            def fn_exps(eng, tv=lbcv, psb=_psb):
                eng.wait_ge(S["c_tv"], tv)
                if psb >= 1:
                    eng.wait_ge(S["c_sb"], psb)
                eng.activation(out=rT[:], in_=vT[:], func=AF.Exp)
                eng.activation(out=gT[:], in_=LT[:], func=AF.Exp)
                eng.activation(out=wT[:], in_=wT[:], func=AF.Exp)
                eng.drain()
                return eng.copy(out=dumS[:], in_=dumS[:]).then_inc(S["c_cs"], 1)
            p.step("se", fn_exps)
            ecs = p.inc("c_cs")

            for h in range(16):
                chp = h // 2
                off = (h % 2) * 64
                Kc = k_fm[chp][off:off + 64, csl] if off == 0 else \
                    k_fm[chp][64:128, csl]
                Qc = q_fm[chp][off:off + 64, csl] if off == 0 else \
                    q_fm[chp][64:128, csl]
                ep = epat[:, h * 128:(h + 1) * 128]

                _tvw = p.val("c_tv")

                def fn_vt(eng, h=h, csl=csl, war=_tvw):
                    if war >= 1:
                        eng.wait_ge(S["c_tv"], war)
                    eng.transpose(psTb[:, 0:128], v_fm[h][:, csl], identbf[:])
                    off = (h % 2) * 64
                    kin = k_fm[h // 2][off:off + 64, csl]
                    return eng.transpose(psTb[:, 128:192], kin,
                                         identbf[off:off + 64, off:off + 64]
                                         ).then_inc(S["c_tr"], 1)
                p.step("te", fn_vt)
                vt1 = p.inc("c_tr")

                def fn_vte(eng, tr=vt1):
                    eng.wait_ge(S["c_tr"], tr)
                    eng.tensor_copy(out=Vtok[:], in_=psTb[:, 0:128])
                    eng.tensor_copy(out=Ktok[:, 0:64], in_=psTb[:, 128:192])
                    eng.tensor_copy(out=Ktok[:, 64:128], in_=psTb[:, 128:192])
                    eng.drain()
                    return eng.tensor_copy(out=dumV[:], in_=dumV[:]
                                           ).then_inc(S["c_tv"], 1)
                p.step("ve", fn_vte)
                vte = p.inc("c_tv")

                _dm = p.val("c_dm")

                def fn_kk(eng, Kc=Kc, Qc=Qc, war=_dm):
                    if war >= 1:
                        eng.wait_ge(S["c_dm"], war)
                    eng.matmul(psD2[:, 0:128], Kc, Kc, start=True, stop=True)
                    return eng.matmul(psD2[:, 128:256], Kc, Qc, start=True,
                                      stop=True).then_inc(S["c_kk"], 1)
                p.step("te", fn_kk)
                kk1 = p.inc("c_kk")

                _de = p.val("c_de")

                def fn_dd(eng, ep=ep, cs=ecs, war=_de):
                    eng.wait_ge(S["c_cs"], cs)
                    if war >= 1:
                        eng.wait_ge(S["c_de"], war)
                    eng.matmul(psD1[:, 0:128], ep, vh[:], start=True, stop=False)
                    eng.matmul(psD1[:, 0:128], negLh[:], ep, start=False, stop=False)
                    eng.matmul(psD1[:, 0:128], identf[:], lmask[:], start=False,
                               stop=True)
                    eng.matmul(psD1[:, 128:256], ep, Lh[:], start=True, stop=False)
                    eng.matmul(psD1[:, 128:256], negLh[:], ep, start=False,
                               stop=False)
                    return eng.matmul(psD1[:, 128:256], identf[:], smask[:],
                                      start=False, stop=True).then_inc(S["c_dd"], 1)
                p.step("te", fn_dd)
                dd1 = p.inc("c_dd")

                def fn_de(eng, dd=dd1):
                    eng.wait_ge(S["c_dd"], dd)
                    eng.activation(out=Eex[:], in_=psD1[:, 0:128], func=AF.Exp)
                    eng.activation(out=Eex2[:], in_=psD1[:, 128:256], func=AF.Exp)
                    eng.drain()
                    return eng.copy(out=dumS[:], in_=dumS[:]).then_inc(S["c_de"], 1)
                p.step("se", fn_de)
                de1 = p.inc("c_de")

                _ne = p.val("c_nev")

                def fn_dm(eng, de=de1, kk=kk1, war=_ne):
                    eng.wait_ge(S["c_de"], de)
                    eng.wait_ge(S["c_kk"], kk)
                    if war >= 1:
                        eng.wait_ge(S["c_nev"], war)
                    eng.tensor_tensor(out=Msb[:], in0=Eex[:],
                                      in1=psD2[:, 0:128], op=ALU.mult)
                    eng.tensor_tensor(out=Btsb[:], in0=Eex2[:],
                                      in1=psD2[:, 128:256], op=ALU.mult)
                    eng.drain()
                    eng.tensor_tensor(out=Pa[:], in0=identbf[:], in1=Msb[:],
                                      op=ALU.subtract)
                    eng.drain()
                    return eng.tensor_copy(out=dumV[:], in_=dumV[:]
                                           ).then_inc(S["c_dm"], 1)
                p.step("ve", fn_dm)
                dm1 = p.inc("c_dm")

                def fn_mt(eng, dm=dm1):
                    eng.wait_ge(S["c_dm"], dm)
                    return eng.transpose(psTb[:, 256:384], Msb[:], identbf[:]
                                         ).then_inc(S["c_tr"], 1)
                p.step("te", fn_mt)
                mt1 = p.inc("c_tr")

                def fn_mte(eng, tr=mt1):
                    eng.wait_ge(S["c_tr"], tr)
                    eng.tensor_copy(out=Mtsb[:], in_=psTb[:, 256:384])
                    eng.drain()
                    return eng.tensor_copy(out=dumV[:], in_=dumV[:]
                                           ).then_inc(S["c_tv"], 1)
                p.step("ve", fn_mte)
                mte = p.inc("c_tv")

                Ms = [Msb, Msq]
                Mts = [Mtsb, Mtsq]
                Ps = [Pa, Pb]
                for j in range(1, 7):
                    cur, nxt = (j - 1) % 2, j % 2
                    _nv2 = p.val("c_nev")

                    def fn_nm(eng, cur=cur, mte=mte, war=_nv2):
                        eng.wait_ge(S["c_tv"], mte)
                        if war >= 1:
                            eng.wait_ge(S["c_nev"], war)
                        eng.matmul(psN1[:, 0:128], Mts[cur][:], Ms[cur][:],
                                   start=True, stop=True)
                        return eng.matmul(psN1[:, 128:256], Ms[cur][:],
                                          Mts[cur][:], start=True,
                                          stop=True).then_inc(S["c_nmm"], 1)
                    p.step("te", fn_nm)
                    nm1 = p.inc("c_nmm")

                    def fn_nev(eng, nxt=nxt, nm=nm1):
                        eng.wait_ge(S["c_nmm"], nm)
                        eng.tensor_copy(out=Ms[nxt][:], in_=psN1[:, 0:128])
                        eng.tensor_copy(out=Mts[nxt][:], in_=psN1[:, 128:256])
                        eng.drain()
                        return eng.tensor_copy(out=dumV[:], in_=dumV[:]
                                               ).then_inc(S["c_nev"], 1)
                    p.step("ve", fn_nev)
                    ne1 = p.inc("c_nev")

                    def fn_pd(eng, nxt=nxt, cur=cur, ne=ne1):
                        eng.wait_ge(S["c_nev"], ne)
                        return eng.matmul(psN1[:, 256:384], Mts[nxt][:],
                                          Ps[cur][:], start=True,
                                          stop=True).then_inc(S["c_nmm"], 1)
                    p.step("te", fn_pd)
                    pd1 = p.inc("c_nmm")

                    def fn_pu(eng, cur=cur, nxt=nxt, pd=pd1):
                        eng.wait_ge(S["c_nmm"], pd)
                        eng.tensor_tensor(out=Ps[nxt][:], in0=Ps[cur][:],
                                          in1=psN1[:, 256:384], op=ALU.add)
                        eng.drain()
                        return eng.tensor_copy(out=dumV[:], in_=dumV[:]
                                               ).then_inc(S["c_nev"], 1)
                    p.step("ve", fn_pu)
                    p.inc("c_nev")
                Pfin = Ps[6 % 2]
                pp = p.val("c_nev")

                s0sl = slice(h * 128, (h + 1) * 128)

                _av = p.val("c_av")

                def fn_ks0(eng, Kc=Kc, s0sl=s0sl, off=off, vte=vte,
                           sb=p.val("c_sb"), war=_av):
                    eng.wait_ge(S["c_tv"], vte)
                    if sb >= 1:
                        eng.wait_ge(S["c_sb"], sb)
                    if war >= 1:
                        eng.wait_ge(S["c_av"], war)
                    return eng.matmul(psN2[:, 0:128], Kc,
                                      S0bf[off:off + 64, s0sl],
                                      start=True, stop=True).then_inc(S["c_am"], 1)
                p.step("te", fn_ks0)
                ks0 = p.inc("c_am")

                def fn_r(eng, h=h, ks0=ks0, cs=ecs):
                    eng.wait_ge(S["c_am"], ks0)
                    eng.wait_ge(S["c_cs"], cs)
                    eng.scalar_tensor_tensor(out=Rsb[:], in0=psN2[:, 0:128],
                                             scalar=rT[:, h:h + 1], in1=Vtok[:],
                                             op0=ALU.mult, op1=ALU.subtract_rev
                                             ) if False else None
                    eng.tensor_scalar(out=Oscr[:], in0=psN2[:, 0:128],
                                      scalar1=rT[:, h:h + 1], scalar2=-1.0,
                                      op0=ALU.mult, op1=ALU.mult)
                    eng.drain()
                    eng.tensor_tensor(out=Rsb[:], in0=Oscr[:], in1=Vtok[:],
                                      op=ALU.add)
                    eng.drain()
                    return eng.tensor_copy(out=dumV[:], in_=dumV[:]
                                           ).then_inc(S["c_av"], 1)
                p.step("ve", fn_r)
                rv = p.inc("c_av")

                def fn_u(eng, rv=rv, pp=pp, Pfin=Pfin):
                    eng.wait_ge(S["c_av"], rv)
                    eng.wait_ge(S["c_nev"], pp)
                    return eng.matmul(psN2[:, 128:256], Pfin[:], Rsb[:],
                                      start=True, stop=True).then_inc(S["c_am"], 1)
                p.step("te", fn_u)
                um = p.inc("c_am")

                def fn_ue(eng, h=h, um=um):
                    eng.wait_ge(S["c_am"], um)
                    eng.tensor_copy(out=Usb[:], in_=psN2[:, 128:256])
                    eng.tensor_scalar(out=Uw[:], in0=psN2[:, 128:256],
                                      scalar1=wT[:, h:h + 1], scalar2=None,
                                      op0=ALU.mult, op1=ALU.bypass)
                    eng.drain()
                    return eng.tensor_copy(out=dumV[:], in_=dumV[:]
                                           ).then_inc(S["c_av"], 1)
                p.step("ve", fn_ue)
                ue = p.inc("c_av")

                def fn_qs0(eng, Qc=Qc, s0sl=s0sl, off=off, ue=ue):
                    eng.wait_ge(S["c_av"], ue)
                    eng.matmul(psN2[:, 256:384], Qc, S0bf[off:off + 64, s0sl],
                               start=True, stop=True)
                    return eng.matmul(psN2[:, 384:512], Btsb[:], Usb[:],
                                      start=True, stop=True).then_inc(S["c_am"], 1)
                p.step("te", fn_qs0)
                qs0 = p.inc("c_am")

                def fn_ov(eng, h=h, qs0=qs0):
                    eng.wait_ge(S["c_am"], qs0)
                    eng.tensor_scalar(out=Osb[:], in0=psN2[:, 256:384],
                                      scalar1=gT[:, h:h + 1], scalar2=None,
                                      op0=ALU.mult, op1=ALU.bypass)
                    eng.drain()
                    eng.tensor_tensor(out=Osb[:], in0=Osb[:], in1=psN2[:, 384:512],
                                      op=ALU.add)
                    eng.drain()
                    return eng.tensor_copy(out=dumV[:], in_=dumV[:]
                                           ).then_inc(S["c_av"], 1)
                p.step("ve", fn_ov)
                ov = p.inc("c_av")

                def fn_orm(eng, ov=ov):
                    eng.wait_ge(S["c_av"], ov)
                    eng.activation(out=Oscr[:], in_=Osb[:], func=AF.Square,
                                   accum_out=oms[:])
                    eng.drain()
                    return eng.copy(out=dumS[:], in_=dumS[:]).then_inc(S["c_as"], 1)
                p.step("se", fn_orm)
                orm = p.inc("c_as")

                def fn_orv(eng, orm=orm):
                    eng.wait_ge(S["c_as"], orm)
                    eng.tensor_scalar(out=oms[:], in0=oms[:], scalar1=1.0 / 128.0,
                                      scalar2=1e-6, op0=ALU.mult, op1=ALU.add)
                    eng.drain()
                    return eng.tensor_copy(out=dumV[:], in_=dumV[:]
                                           ).then_inc(S["c_av"], 1)
                p.step("ve", fn_orv)
                orv = p.inc("c_av")

                def fn_orl(eng, orv=orv):
                    eng.wait_ge(S["c_av"], orv)
                    eng.activation(out=oms[:], in_=oms[:], func=AF.Ln)
                    eng.drain()
                    eng.activation(out=oms[:], in_=oms[:], func=AF.Exp, scale=-0.5)
                    eng.drain()
                    return eng.copy(out=dumS[:], in_=dumS[:]).then_inc(S["c_as"], 1)
                p.step("se", fn_orl)
                orl = p.inc("c_as")

                def fn_osc(eng, orl=orl):
                    eng.wait_ge(S["c_as"], orl)
                    eng.tensor_scalar(out=Oscr[:], in0=Osb[:], scalar1=oms[:],
                                      scalar2=None, op0=ALU.mult, op1=ALU.bypass)
                    eng.drain()
                    return eng.tensor_copy(out=dumV[:], in_=dumV[:]
                                           ).then_inc(S["c_av"], 1)
                p.step("ve", fn_osc)
                osc = p.inc("c_av")

                _oew = p.val("c_oe")

                def fn_ot(eng, osc=osc, lb=lbcv, oew=_oew):
                    eng.wait_ge(S["c_av"], osc)
                    eng.wait_ge(S["c_tv"], lb)
                    if oew >= 1:
                        eng.wait_ge(S["c_oe"], oew)
                    return eng.transpose(psTf[:, 0:128], Oscr[:], identf[:]
                                         ).then_inc(S["c_ot"], 1)
                p.step("te", fn_ot)
                ot = p.inc("c_ot")

                def fn_oe(eng, h=h, csl=csl, ot=ot):
                    eng.wait_ge(S["c_ot"], ot)
                    eng.tensor_copy(out=o_fm[h][:, csl], in_=psTf[:, 0:128])
                    eng.drain()
                    return eng.tensor_copy(out=dumV[:], in_=dumV[:]
                                           ).then_inc(S["c_oe"], 1)
                p.step("ve", fn_oe)
                p.inc("c_oe")

                def fn_gc(eng, h=h, ue=ue):
                    eng.wait_ge(S["c_av"], ue)
                    eng.activation(out=gCc[:], in_=LCbc[:, h:h + 1], func=AF.Exp)
                    eng.drain()
                    return eng.copy(out=dumS[:], in_=dumS[:]).then_inc(S["c_as"], 1)
                p.step("se", fn_gc)
                gc1 = p.inc("c_as")

                _avw = p.val("c_av")

                def fn_sc(eng, avw=_avw):
                    eng.wait_ge(S["c_av"], avw)
                    return eng.matmul(psN2[:, 384:512], Ktok[:],
                                      Uw[:], start=True, stop=True
                                      ).then_inc(S["c_am"], 1)
                p.step("te", fn_sc)
                sc1 = p.inc("c_am")

                def fn_su(eng, h=h, s0sl=s0sl, sc=sc1, gc=gc1):
                    eng.wait_ge(S["c_am"], sc)
                    eng.wait_ge(S["c_as"], gc)
                    eng.tensor_scalar(out=S0sc[:], in0=Ssb[:, s0sl],
                                      scalar1=gCc[:], scalar2=None,
                                      op0=ALU.mult, op1=ALU.bypass)
                    eng.drain()
                    eng.tensor_tensor(out=Ssb[:, s0sl], in0=S0sc[:],
                                      in1=psN2[:, 384:512], op=ALU.add)
                    eng.drain()
                    eng.tensor_copy(out=S0bf[:, s0sl], in_=Ssb[:, s0sl])
                    eng.drain()
                    return eng.tensor_copy(out=dumV[:], in_=dumV[:]
                                           ).then_inc(S["c_sb"], 1)
                p.step("ve", fn_su)
                p.inc("c_sb")

        for g in range(NG):
            for ti in range(GS // 128):
                t = g * (GS // 128) + ti
                nx = p.val("a1x")
                if nx >= 2:
                    def fn_w(eng, need=nx - 1):
                        eng.wait_ge(S["a1x"], need)
                        return None
                    p.step("sy", fn_w)
                lc = p.load("sy", Xbuf[ti % 2][:], x_t[t])

                _tw = p.val("a1t")

                def fn_v(eng, t=t, ti=ti, lc=lc, tw=_tw):
                    eng.wait_ge(S["dma_sy"], lc)
                    if tw >= 2:
                        eng.wait_ge(S["a1t"], tw - 1)
                    return eng.tensor_scalar(
                        out=xnbuf[ti % 2][:], in0=Xbuf[ti % 2][:],
                        scalar1=mcol[:, t:t + 1], scalar2=Ccol[:, t:t + 1],
                        op0=ALU.subtract, op1=ALU.mult).then_inc(S["a1x"], 1)
                p.step("ve", fn_v)
                x_tgt = p.inc("a1x")

                _ew = p.val("a1e")

                def fn_t(eng, ti=ti, xt=x_tgt, ew=_ew):
                    eng.wait_ge(S["a1x"], xt)
                    if ew >= 1:
                        eng.wait_ge(S["a1e"], ew)
                    r = None
                    for c in range(NCH):
                        r = eng.transpose(psTb[:, c * 128:(c + 1) * 128],
                                          xnbuf[ti % 2][:, c * 128:(c + 1) * 128],
                                          identbf[:])
                    return r.then_inc(S["a1t"], 1)
                p.step("te", fn_t)
                t_tgt = p.inc("a1t")

                def fn_e(eng, ti=ti, tt=t_tgt):
                    eng.wait_ge(S["a1t"], tt)
                    r = None
                    for c in range(NCH):
                        r = eng.tensor_copy(
                            out=xn_fm[c][:, ti * 128:(ti + 1) * 128],
                            in_=psTb[:, c * 128:(c + 1) * 128])
                    return r.then_inc(S["a1e"], 1)
                p.step("ve", fn_e)
                p.inc("a1e")

            p.barrier()

            def emit_proj(kind):
                if kind == "q":
                    wsl = lambda c, m: wbig[:, c, m * 128:(m + 1) * 128]
                    nchunks, dst, tail, cw = NCH, q_fm, qtail, cqfm
                elif kind == "k":
                    wsl = lambda c, m: wbig[:, c, 1024 + m * 128:1024 + (m + 1) * 128]
                    nchunks, dst, tail, cw = NCH, k_fm, ktail, ckfm
                else:
                    wsl = lambda c, m: wbig[:, c, m * 128:(m + 1) * 128]
                    nchunks, dst, tail, cw = 16, v_fm, vtail, cvfm

                for m in range(nchunks):
                    _pev = p.val("c_pe")
                    _wcnt = p.dma_cnt["ge"] * 16

                    def fn_mm(eng, m=m, wsl=wsl, war=_pev, wcnt=_wcnt):
                        eng.wait_ge(S["dma_ge"], wcnt)
                        if war >= 1:
                            eng.wait_ge(S["c_pe"], war)
                        r = None
                        for c in range(NCH):
                            r = eng.matmul(psP[:, 0:512], wsl(c, m),
                                           xn_fm[c][:], start=(c == 0),
                                           stop=(c == NCH - 1))
                        return r.then_inc(S["c_pm"], 1)
                    p.step("te", fn_mm)
                    pm_tgt = p.inc("c_pm")

                    _cvw = p.val("c_cv")

                    def fn_ev(eng, m=m, pm=pm_tgt, war=_cvw):
                        eng.wait_ge(S["c_pm"], pm)
                        if war >= 1:
                            eng.wait_ge(S["c_cv"], war)
                        eng.tensor_copy(out=qpre[m % NCH][:], in_=psP[:, 0:512])
                        eng.drain()
                        return eng.tensor_copy(out=dumV[:], in_=dumV[:]
                                               ).then_inc(S["c_pe"], 1)
                    p.step("ve", fn_ev)
                    pe_tgt = p.inc("c_pe")

                    ceng = "ve"
                    _slw = p.val("c_sl")
                    _nvw = p.val("c_nv")

                    def fn_conv(eng, m=m, pe=pe_tgt, tail=tail, cw=cw,
                                kind=kind, slw=_slw, nvw=_nvw):
                        eng.wait_ge(S["c_pe"], pe)
                        if slw >= 1:
                            eng.wait_ge(S["c_sl"], slw)
                        if nvw >= 1 and kind != "v":
                            eng.wait_ge(S["c_nv"], nvw)
                        src = qpre[m % NCH]
                        eng.tensor_scalar(out=qtmp[:], in0=src[:],
                                          scalar1=cw[:, m, 3:4], scalar2=None,
                                          op0=ALU.mult, op1=ALU.bypass)
                        eng.drain()
                        for dly in (1, 2, 3):
                            wcol = cw[:, m, 3 - dly:4 - dly]
                            eng.scalar_tensor_tensor(
                                out=qtmp[:, dly:GS], in0=src[:, 0:GS - dly],
                                scalar=wcol, in1=qtmp[:, dly:GS],
                                op0=ALU.mult, op1=ALU.add)
                            eng.drain()
                            eng.scalar_tensor_tensor(
                                out=qtmp[:, 0:dly], in0=tail[:, m, 3 - dly:3],
                                scalar=wcol, in1=qtmp[:, 0:dly],
                                op0=ALU.mult, op1=ALU.add)
                            eng.drain()
                        eng.tensor_copy(out=tail[:, m, :], in_=src[:, GS - 3:GS])
                        eng.drain()
                        return eng.tensor_copy(out=dumV[:], in_=dumV[:]
                                               ).then_inc(S["c_cv"], 1)
                    p.step(ceng, fn_conv)
                    cv_tgt = p.inc("c_cv")

                    def fn_silu(eng, m=m, cv=cv_tgt, dst=dst, kind=kind):
                        eng.wait_ge(S["c_cv"], cv)
                        if kind == "v":
                            eng.activation(out=dst[m][:], in_=qtmp[:], func=AF.Silu)
                        else:
                            eng.activation(out=qtmp[:], in_=qtmp[:], func=AF.Silu)
                        eng.drain()
                        return eng.copy(out=dumS[:], in_=dumS[:]).then_inc(S["c_sl"], 1)
                    p.step("se", fn_silu)
                    sl_tgt = p.inc("c_sl")

                    if kind != "v":
                        def fn_sq(eng, sl=sl_tgt):
                            eng.wait_ge(S["c_sl"], sl)
                            eng.activation(out=qtmp2[:], in_=qtmp[:], func=AF.Square)
                            eng.drain()
                            return eng.copy(out=dumS[:], in_=dumS[:]
                                            ).then_inc(S["c_sl"], 1)
                        p.step("se", fn_sq)
                        sq_tgt = p.inc("c_sl")

                        _nv = p.val("c_nv")

                        def fn_nmm(eng, sq=sq_tgt, war=_nv):
                            eng.wait_ge(S["c_sl"], sq)
                            if war >= 1:
                                eng.wait_ge(S["c_nv"], war)
                            return eng.matmul(psG[0:2, 0:512], bd2[:], qtmp2[:],
                                              start=True, stop=True
                                              ).then_inc(S["c_nm"], 1)
                        p.step("te", fn_nmm)
                        nm_tgt = p.inc("c_nm")

                        def fn_nrm(eng, nm=nm_tgt):
                            eng.wait_ge(S["c_nm"], nm)
                            eng.tensor_scalar(out=nrm2[:], in0=psG[0:2, 0:512],
                                              scalar1=1e-12, scalar2=None,
                                              op0=ALU.max, op1=ALU.bypass)
                            eng.drain()
                            return eng.tensor_copy(out=dumV[:], in_=dumV[:]
                                                   ).then_inc(S["c_nv"], 1)
                        p.step("ve", fn_nrm)
                        nv_tgt = p.inc("c_nv")

                        def fn_nln(eng, nv=nv_tgt):
                            eng.wait_ge(S["c_nv"], nv)
                            eng.activation(out=nrm2[:], in_=nrm2[:], func=AF.Ln)
                            eng.drain()
                            eng.activation(out=nrm2[:], in_=nrm2[:], func=AF.Exp,
                                           scale=-0.5)
                            eng.drain()
                            return eng.copy(out=dumS[:], in_=dumS[:]
                                            ).then_inc(S["c_sl"], 1)
                        p.step("se", fn_nln)
                        nl_tgt = p.inc("c_sl")

                        def fn_nbc(eng, nl=nl_tgt):
                            eng.wait_ge(S["c_sl"], nl)
                            return eng.matmul(psG[:, 0:512], hd2[:], nrm2[:],
                                              start=True, stop=True
                                              ).then_inc(S["c_nm"], 1)
                        p.step("te", fn_nbc)
                        nb_tgt = p.inc("c_nm")

                        def fn_nsc(eng, m=m, nb=nb_tgt, dst=dst):
                            eng.wait_ge(S["c_nm"], nb)
                            eng.tensor_tensor(out=dst[m][:], in0=qtmp[:],
                                              in1=psG[:, 0:512], op=ALU.mult)
                            eng.drain()
                            return eng.tensor_copy(out=dumV[:], in_=dumV[:]
                                                   ).then_inc(S["c_nv"], 1)
                        p.step("ve", fn_nsc)
                        p.inc("c_nv")

            _pm0 = p.val("c_pm")

            def fn_wg1(eng, pm=_pm0):
                if pm:
                    eng.wait_ge(S["c_pm"], pm)
                return None
            p.step("ge", fn_wg1)
            p.load("ge", wbig[:, :, 0:1024],
                   wqT_d.rearrange("(c p) n -> p c n", p=128))
            p.load("ge", wbig[:, :, 1024:2048],
                   wkT_d.rearrange("(c p) n -> p c n", p=128))
            emit_proj("q")
            emit_proj("k")
            _pm1 = p.val("c_pm")

            def fn_wg2(eng, pm=_pm1):
                eng.wait_ge(S["c_pm"], pm)
                return None
            p.step("ge", fn_wg2)
            p.load("ge", wbig[:, :, 0:2048],
                   wvT_d.rearrange("(c p) n -> p c n", p=128))
            emit_proj("v")

            _nvx = p.val("c_nv")

            def fn_gmm(eng, war=_nvx):
                if war >= 1:
                    eng.wait_ge(S["c_nv"], war)
                r = None
                for c in range(NCH):
                    eng.matmul(psG[0:16, 0:GS], wabT[:, c, 0:H], xn_fm[c][:],
                               start=(c == 0), stop=(c == NCH - 1))
                for c in range(NCH):
                    r = eng.matmul(psD1[0:16, 0:GS], wabT[:, c, H:2 * H],
                                   xn_fm[c][:], start=(c == 0),
                                   stop=(c == NCH - 1))
                return r.then_inc(S["c_gm"], 1)
            p.step("te", fn_gmm)
            gm_tgt = p.inc("c_gm")

            def fn_gev(eng, gm=gm_tgt):
                eng.wait_ge(S["c_gm"], gm)
                eng.tensor_scalar(out=alin[:], in0=psG[0:16, 0:GS],
                                  scalar1=dtbc[:], scalar2=None,
                                  op0=ALU.add, op1=ALU.bypass)
                eng.tensor_copy(out=blin[:], in_=psD1[0:16, 0:GS])
                eng.drain()
                eng.tensor_scalar(out=gsc1[:], in0=alin[:], scalar1=0.0,
                                  scalar2=None, op0=ALU.max, op1=ALU.bypass)
                eng.tensor_scalar(out=gsc2[:], in0=alin[:], scalar1=-1.0,
                                  scalar2=0.0, op0=ALU.mult, op1=ALU.max)
                eng.drain()
                eng.tensor_tensor(out=gsc2[:], in0=gsc2[:], in1=gsc1[:],
                                  op=ALU.add)
                eng.drain()
                return eng.tensor_copy(out=dumV[:], in_=dumV[:]).then_inc(S["c_gv"], 1)
            p.step("ve", fn_gev)
            gv_tgt = p.inc("c_gv")

            def fn_gse(eng, gv=gv_tgt):
                eng.wait_ge(S["c_gv"], gv)
                eng.activation(out=gsc2[:], in_=gsc2[:], func=AF.Exp, scale=-1.0)
                eng.drain()
                eng.activation(out=gsc2[:], in_=gsc2[:], func=AF.Ln, bias=1.0)
                eng.drain()
                return eng.copy(out=dumS[:], in_=dumS[:]).then_inc(S["c_gs"], 1)
            p.step("se", fn_gse)
            gs_tgt = p.inc("c_gs")

            def fn_gv2(eng, gs=gs_tgt):
                eng.wait_ge(S["c_gs"], gs)
                eng.tensor_tensor(out=gsc1[:], in0=gsc1[:], in1=gsc2[:], op=ALU.add)
                eng.drain()
                eng.tensor_scalar(out=loga[:], in0=gsc1[:], scalar1=theta[:],
                                  scalar2=-1.0, op0=ALU.mult, op1=ALU.mult)
                eng.drain()
                eng.tensor_scalar(out=gsc1[:], in0=blin[:], scalar1=-1.0,
                                  scalar2=0.0, op0=ALU.mult, op1=ALU.max)
                eng.tensor_scalar(out=gsc2[:], in0=blin[:], scalar1=0.0,
                                  scalar2=None, op0=ALU.max, op1=ALU.bypass)
                eng.drain()
                eng.tensor_tensor(out=gsc2[:], in0=gsc2[:], in1=gsc1[:],
                                  op=ALU.add)
                eng.drain()
                return eng.tensor_copy(out=dumV[:], in_=dumV[:]).then_inc(S["c_gv"], 1)
            p.step("ve", fn_gv2)
            gv2_tgt = p.inc("c_gv")

            def fn_gse2(eng, gv=gv2_tgt):
                eng.wait_ge(S["c_gv"], gv)
                eng.activation(out=gsc2[:], in_=gsc2[:], func=AF.Exp, scale=-1.0)
                eng.drain()
                eng.activation(out=gsc2[:], in_=gsc2[:], func=AF.Ln, bias=1.0)
                eng.drain()
                return eng.copy(out=dumS[:], in_=dumS[:]).then_inc(S["c_gs"], 1)
            p.step("se", fn_gse2)
            gs2_tgt = p.inc("c_gs")

            def fn_gv3(eng, gs=gs2_tgt):
                eng.wait_ge(S["c_gs"], gs)
                eng.tensor_tensor(out=gsc1[:], in0=gsc1[:], in1=gsc2[:], op=ALU.add)
                eng.drain()
                eng.tensor_tensor(out=lnc[:], in0=loga[:], in1=gsc1[:],
                                  op=ALU.subtract)
                eng.drain()
                eng.tensor_scalar(out=lnc[:], in0=lnc[:],
                                  scalar1=0.6931471805599453, scalar2=None,
                                  op0=ALU.add, op1=ALU.bypass)
                eng.drain()
                return eng.tensor_copy(out=dumV[:], in_=dumV[:]).then_inc(S["c_gv"], 1)
            p.step("ve", fn_gv3)
            p.inc("c_gv")

            p.barrier()

            for ci in range(NCK):
                emit_chunk(g, ci)

            p.barrier()

            for tt in range(GS // 128):
                t = g * (GS // 128) + tt
                lc = p.load("sy", Xbuf[tt % 2][:], x_t[t])
                for half in range(2):
                    _yv = p.val("a3v")

                    def fn_omm(eng, tt=tt, half=half, war=_yv):
                        if war >= 1:
                            eng.wait_ge(S["a3v"], war)
                        r = None
                        for hc in range(16):
                            r = eng.matmul(
                                psP[:, 0:512],
                                o_fm[hc][:, tt * 128:(tt + 1) * 128],
                                woT[:, hc, half * 512:(half + 1) * 512],
                                start=(hc == 0), stop=(hc == 15))
                        return r.then_inc(S["c_op"], 1)
                    p.step("te", fn_omm)
                    op_tgt = p.inc("c_op")

                    _stp = (p.dma_cnt["st"] - 1) * 16

                    def fn_y(eng, tt=tt, half=half, op=op_tgt, lc=lc, stp=_stp):
                        eng.wait_ge(S["c_op"], op)
                        eng.wait_ge(S["dma_sy"], lc)
                        if stp >= 16:
                            eng.wait_ge(S["dma_st"], stp)
                        return eng.tensor_tensor(
                            out=Ybuf[tt % 2][:, half * 512:(half + 1) * 512],
                            in0=Xbuf[tt % 2][:, half * 512:(half + 1) * 512],
                            in1=psP[:, 0:512], op=ALU.add).then_inc(S["a3v"], 1)
                    p.step("ve", fn_y)
                    p.inc("a3v")

                _yv2 = p.val("a3v")

                def fn_stw(eng, tgt=_yv2):
                    eng.wait_ge(S["a3v"], tgt)
                    return None
                p.step("sy", fn_stw)
                p.store(y_t[t], Ybuf[tt % 2][:])

            p.barrier()

        if not fc:
            p.finalize(block, S, {"ve": dumV[:], "se": dumS[:], "ge": dumG[:]})

    return nc


def _run_l1_attn(x, Wq, Wk, Wv, Wb, Wa, A_log, dt_bias, Wo, conv_q, conv_k, conv_v):
    identf, identbf, _ = _consts()
    H = 16
    epat = np.zeros((H, H * 128), dtype=np.float32)
    for h in range(H):
        epat[h, h * 128:(h + 1) * 128] = 1.0
    BIG = 30000.0
    ii = np.arange(128)
    lmask = np.where(ii[:, None] >= ii[None, :], -BIG, 0.0).astype(np.float32)
    smask = np.where(ii[:, None] > ii[None, :], -BIG, 0.0).astype(np.float32)
    uti = (ii[:, None] <= ii[None, :]).astype(np.float32)
    uts = (ii[:, None] < ii[None, :]).astype(np.float32)
    bd2 = np.zeros((128, 2), dtype=ml_dtypes.bfloat16)
    bd2[0:64, 0] = 1.0
    bd2[64:128, 1] = 1.0
    hd2 = np.zeros((2, 128), dtype=np.float32)
    hd2[0, 0:64] = 1.0
    hd2[1, 64:128] = 1.0
    elast = np.zeros((128, 1), dtype=np.float32)
    elast[127, 0] = 1.0

    if "l1a" not in _CACHE:
        _CACHE["l1a"] = build_l1_attn()
    nc = _CACHE["l1a"]
    base = {
        "wqT": np.ascontiguousarray(Wq.T), "wkT": np.ascontiguousarray(Wk.T),
        "wvT": np.ascontiguousarray(Wv.T), "waT": np.ascontiguousarray(Wa.T),
        "wbT": np.ascontiguousarray(Wb.T), "woT": np.ascontiguousarray(Wo.T),
        "cqfm": np.ascontiguousarray(conv_q.reshape(8, 128, 4).transpose(1, 0, 2)),
        "ckfm": np.ascontiguousarray(conv_k.reshape(8, 128, 4).transpose(1, 0, 2)),
        "cvfm": np.ascontiguousarray(conv_v.reshape(16, 128, 4).transpose(1, 0, 2)),
        "alogrow": np.ascontiguousarray(A_log.reshape(1, H)),
        "dtbcol": np.ascontiguousarray(dt_bias.reshape(H, 1)),
        "identf": identf, "identbf": identbf, "epat": epat,
        "lmask": lmask, "smask": smask, "uti": uti, "uts": uts,
        "bd2": bd2, "hd2": hd2, "elast": elast,
    }
    in_maps = []
    for c in range(8):
        m = dict(base)
        m["x"] = np.ascontiguousarray(x[c])
        in_maps.append(m)
    res = run_bass_kernel_spmd(nc, in_maps, core_ids=list(range(8)))
    return np.stack([np.asarray(r["y"]) for r in res.results], axis=0)



def _consts():
    identf = np.eye(128, dtype=np.float32)
    identbf = np.eye(128, dtype=ml_dtypes.bfloat16)
    bd65 = np.zeros((128, 65), dtype=ml_dtypes.bfloat16)
    bd65[0:64, 64] = 1.0
    bd65[64:128, 32] = 1.0
    return identf, identbf, bd65


_CACHE = {}


def _run_l0_attn(x, ln_scale, ln_bias, wq, wk, wv):
    identf, identbf, bd65 = _consts()
    lns_fm = np.ascontiguousarray(ln_scale.reshape(NCH, 128).T)
    lnb_fm = np.ascontiguousarray(ln_bias.reshape(NCH, 128).T)

    import os
    _ph = os.environ.get("K_PHASES")
    if _ph:
        nc = build_l0_attn(phases=tuple(_ph.split(",")))
    else:
        if "l0a" not in _CACHE:
            _CACHE["l0a"] = build_l0_attn()
        nc = _CACHE["l0a"]

    base = {
        "wq": np.ascontiguousarray(wq), "wk": np.ascontiguousarray(wk),
        "wv": np.ascontiguousarray(wv),
        "lns_fm": lns_fm, "lnb_fm": lnb_fm,
        "lns_row": np.ascontiguousarray(ln_scale.reshape(1, D)),
        "lnb_row": np.ascontiguousarray(ln_bias.reshape(1, D)),
        "identbf": identbf, "identf": identf, "bd65": bd65,
    }
    in_maps = []
    for c in range(8):
        m = dict(base)
        m["x"] = np.ascontiguousarray(x[c])
        in_maps.append(m)
    res = run_bass_kernel_spmd(nc, in_maps, core_ids=list(range(8)))
    return np.stack([np.asarray(r["y"]) for r in res.results], axis=0)




_ALL_SEMS = [
    "dma_ge", "dma_sy", "dma_st", "bar", "bc",
    "a0", "a1x", "a1t", "a1e", "proj",
    "s_q", "s_k", "s_v", "s_evq", "s_evk", "s_evv",
    "s_nq", "s_nk", "s_evn",
    "s_s", "s_p", "s_pv", "s_eo", "s_tr", "s_no", "a3v",
    "b_l", "b_g", "b_row", "b_bc", "b_x",
    "b_a", "b_as", "b_b", "b_h", "b_o", "b_y", "b_m", "b_ms",
    "c_pm", "c_pe", "c_cv", "c_sl", "c_nm", "c_nv",
    "c_gm", "c_gv", "c_gs",
    "c_tr", "c_tv", "c_cs",
    "c_kk", "c_dd", "c_de", "c_dm",
    "c_nmm", "c_nev", "c_pp",
    "c_am", "c_av", "c_as",
    "c_ot", "c_oe", "c_sb", "c_op",
]


def build_fused(T=T, NT=NT):
    nc = bass.Bass("TRN2", target_bir_lowering=False, detect_race_conditions=False)
    x_d = nc.dram_tensor("x", [T, D], F32, kind="ExternalInput").ap()
    x1_d = nc.dram_tensor("x1buf", [T, D], F32, kind="Internal").ap()
    x2_d = nc.dram_tensor("x2buf", [T, D], F32, kind="Internal").ap()
    x3_d = nc.dram_tensor("x3buf", [T, D], F32, kind="Internal").ap()
    y_d = nc.dram_tensor("y", [T, D], F32, kind="ExternalOutput").ap()
    p = Prog(nc)
    with ExitStack() as ctx:
        e = ctx.enter_context
        S = {n: e(nc.semaphore(n)) for n in _ALL_SEMS}
        gdV = e(nc.sbuf_tensor("G_dumV", [1, 1], F32))
        gdS = e(nc.sbuf_tensor("G_dumS", [1, 1], F32))
        gdG = e(nc.sbuf_tensor("G_dumG", [1, 1], F32))

        def fnv(eng):
            return eng.memset(gdV[:], 0.0)
        p.step("ve", fnv)

        def fng(eng):
            return eng.memset(gdG[:], 0.0)
        p.step("ge", fng)

        def fns(eng):
            return eng.memzero(gdS[:])
        p.step("se", fns)
        p.barrier()

        fc = {"nc": nc, "p": p, "S": S, "dram_cache": {}}
        build_l0_attn(T=T, NT=NT, fc=dict(fc, px="A_", x_in=x_d, x_out=x1_d))
        p.barrier()
        build_l0_ffn(T=T, NT=NT, fc=dict(fc, px="B_", x_in=x1_d, x_out=x2_d,
                                         dram_cache=fc["dram_cache"]))
        p.barrier()
        build_l1_attn(T=T, NT=NT, fc=dict(fc, px="C_", x_in=x2_d, x_out=x3_d,
                                          dram_cache=fc["dram_cache"]))
        p.barrier()
        build_l1_ffn(T=T, NT=NT, fc=dict(fc, px="D_", x_in=x3_d, x_out=y_d,
                                         dram_cache=fc["dram_cache"]))
        p.barrier()

        block = e(nc.Block())
        p.finalize(block, S, {"ve": gdV[:], "se": gdS[:], "ge": gdG[:]})
    return nc


def _fused_base(inputs):
    f = lambda k: np.asarray(inputs[k], dtype=np.float32)
    identf, identbf, bd65 = _consts()
    H = 16
    epat = np.zeros((H, H * 128), dtype=np.float32)
    for h in range(H):
        epat[h, h * 128:(h + 1) * 128] = 1.0
    BIG = 30000.0
    ii = np.arange(128)
    lmask = np.where(ii[:, None] >= ii[None, :], -BIG, 0.0).astype(np.float32)
    smask = np.where(ii[:, None] > ii[None, :], -BIG, 0.0).astype(np.float32)
    uti = (ii[:, None] <= ii[None, :]).astype(np.float32)
    uts = (ii[:, None] < ii[None, :]).astype(np.float32)
    bd2 = np.zeros((128, 2), dtype=ml_dtypes.bfloat16)
    bd2[0:64, 0] = 1.0
    bd2[64:128, 1] = 1.0
    hd2 = np.zeros((2, 128), dtype=np.float32)
    hd2[0, 0:64] = 1.0
    hd2[1, 64:128] = 1.0
    elast = np.zeros((128, 1), dtype=np.float32)
    elast[127, 0] = 1.0
    ln_scale = f("L0_attn_ln_scale")
    ln_bias = f("L0_attn_ln_bias")
    w1a = np.concatenate([f("L0_ffn_w1"), f("L0_ffn_w1s")], axis=0)
    w2a = np.concatenate([f("L0_ffn_w2"), f("L0_ffn_w2s")], axis=0)
    w3a = np.concatenate([f("L0_ffn_w3"), f("L0_ffn_w3s")], axis=0)
    base = {
        "wq": np.ascontiguousarray(f("L0_attn_wq")),
        "wk": np.ascontiguousarray(f("L0_attn_wk")),
        "wv": np.ascontiguousarray(f("L0_attn_wv")),
        "lns_fm": np.ascontiguousarray(ln_scale.reshape(NCH, 128).T),
        "lnb_fm": np.ascontiguousarray(ln_bias.reshape(NCH, 128).T),
        "lns_row": np.ascontiguousarray(ln_scale.reshape(1, D)),
        "lnb_row": np.ascontiguousarray(ln_bias.reshape(1, D)),
        "bd65": bd65,
        "gwT": np.ascontiguousarray(f("L0_ffn_gate_w").T),
        "w1": np.ascontiguousarray(w1a), "w2": np.ascontiguousarray(w2a),
        "w3": np.ascontiguousarray(w3a),
        "wqT": np.ascontiguousarray(f("L1_attn_Wq").T),
        "wkT": np.ascontiguousarray(f("L1_attn_Wk").T),
        "wvT": np.ascontiguousarray(f("L1_attn_Wv").T),
        "waT": np.ascontiguousarray(f("L1_attn_Wa").T),
        "wbT": np.ascontiguousarray(f("L1_attn_Wb").T),
        "woT": np.ascontiguousarray(f("L1_attn_Wo").T),
        "cqfm": np.ascontiguousarray(f("L1_attn_conv_q").reshape(8, 128, 4).transpose(1, 0, 2)),
        "ckfm": np.ascontiguousarray(f("L1_attn_conv_k").reshape(8, 128, 4).transpose(1, 0, 2)),
        "cvfm": np.ascontiguousarray(f("L1_attn_conv_v").reshape(16, 128, 4).transpose(1, 0, 2)),
        "alogrow": np.ascontiguousarray(f("L1_attn_A_log").reshape(1, H)),
        "dtbcol": np.ascontiguousarray(f("L1_attn_dt_bias").reshape(H, 1)),
        "epat": epat, "lmask": lmask, "smask": smask, "uti": uti, "uts": uts,
        "bd2": bd2, "hd2": hd2, "elast": elast,
        "rw1": np.ascontiguousarray(f("L1_ffn_rw1")),
        "rb1fm": np.ascontiguousarray(f("L1_ffn_rb1").reshape(NCH, 128).T),
        "rw2": np.ascontiguousarray(f("L1_ffn_rw2")),
        "rb2row": np.ascontiguousarray(f("L1_ffn_rb2").reshape(1, 8)),
        "ew1": np.ascontiguousarray(f("L1_ffn_ew1")),
        "eb1fm": np.ascontiguousarray(f("L1_ffn_eb1").reshape(8, 16, 128).transpose(0, 2, 1)),
        "ew2": np.ascontiguousarray(f("L1_ffn_ew2")),
        "eb2row": np.ascontiguousarray(f("L1_ffn_eb2").reshape(8, 1, D)),
        "identf": identf, "identbf": identbf,
    }
    return base


def kernel(**inputs):
    x = np.asarray(inputs["L0_attn_x"], dtype=np.float32)
    if "fused" not in _CACHE:
        _CACHE["fused"] = build_fused()
    nc = _CACHE["fused"]
    base = _fused_base(inputs)
    in_maps = []
    for c in range(8):
        m = dict(base)
        m["x"] = np.ascontiguousarray(x[c])
        in_maps.append(m)
    res = run_bass_kernel_spmd(nc, in_maps, core_ids=list(range(8)))
    return np.stack([np.asarray(r["y"]) for r in res.results], axis=0)


# version 03
# speedup vs baseline: 1.2448x; 1.2448x over previous
import numpy as np
import ml_dtypes
from contextlib import ExitStack

import concourse.bass as bass
import concourse.mybir as mybir
from concourse.bass_utils import run_bass_kernel_spmd
from concourse.alu_op_type import AluOpType as ALU

F32 = mybir.dt.float32
BF16 = mybir.dt.bfloat16
AF = mybir.ActivationFunctionType
AX = mybir.AxisListType.X

T = 4096
D = 1024
NT = T // 128
NCH = D // 128



class Prog:

    KEYS = ("te", "ve", "se", "ge", "sy")

    def __init__(self, nc):
        self.nc = nc
        self.steps = []
        self.nbar = 0
        self.dma_cnt = {"ge": 0, "sy": 0, "st": 0}
        self.sems = {}
        self.sem_cnt = {}

    def inc(self, name):
        self.sem_cnt[name] = self.sem_cnt.get(name, 0) + 1
        return self.sem_cnt[name]

    def val(self, name):
        return self.sem_cnt.get(name, 0)

    def step(self, key, fn):
        assert key in self.KEYS
        self.steps.append(("step", key, fn))

    def barrier(self):
        self.steps.append(("bar",))
        self.nbar += 1

    def load(self, key, out_ap, in_ap, **kw):
        self.dma_cnt[key] += 1
        tgt = self.dma_cnt[key] * 16
        sem_name = "dma_" + key

        def fn(eng, out_ap=out_ap, in_ap=in_ap, kw=kw, sem_name=sem_name):
            return eng.dma_start(out=out_ap, in_=in_ap, **kw).then_inc(
                self.sems[sem_name], 16)

        fn._dma_kind = key
        self.step(key, fn)
        return tgt

    def store(self, out_ap, in_ap, **kw):
        self.dma_cnt["st"] += 1
        tgt = self.dma_cnt["st"] * 16

        def fn(eng, out_ap=out_ap, in_ap=in_ap, kw=kw):
            return eng.dma_start(out=out_ap, in_=in_ap, **kw).then_inc(
                self.sems["dma_st"], 16)

        fn._dma_kind = "st"
        self.step("sy", fn)
        return tgt

    def finalize(self, block, sem_handles, dummies):
        self.sems = sem_handles
        steps = self.steps

        def run_engine(eng, key):
            bar_i = 0
            seen = {"ge": 0, "sy": 0, "st": 0}
            for st_ in steps:
                if st_[0] == "bar":
                    bar_i += 1
                    if key == "sy":
                        if seen["sy"]:
                            eng.wait_ge(self.sems["dma_sy"], seen["sy"] * 16)
                        if seen["st"]:
                            eng.wait_ge(self.sems["dma_st"], seen["st"] * 16)
                        eng.sem_inc(self.sems["bar"], 1)
                    elif key == "ge":
                        if seen["ge"]:
                            eng.wait_ge(self.sems["dma_ge"], seen["ge"] * 16)
                        eng.drain()
                        eng.tensor_copy(out=dummies["ge"], in_=dummies["ge"]
                                        ).then_inc(self.sems["bar"], 1)
                    elif key == "te":
                        eng.drain()
                        eng.nop().then_inc(self.sems["bar"], 1)
                    elif key == "ve":
                        eng.drain()
                        eng.tensor_copy(out=dummies["ve"], in_=dummies["ve"]
                                        ).then_inc(self.sems["bar"], 1)
                    else:
                        eng.drain()
                        eng.copy(out=dummies["se"], in_=dummies["se"]
                                 ).then_inc(self.sems["bar"], 1)
                    eng.wait_ge(self.sems["bar"], 5 * bar_i)
                    continue
                _, k, fn = st_
                kind = getattr(fn, "_dma_kind", None)
                if kind is not None:
                    seen[kind] += 1
                if k != key:
                    continue
                fn(eng)

        @block.tensor
        def _(te):
            run_engine(te, "te")

        @block.vector
        def _(ve):
            run_engine(ve, "ve")

        @block.scalar
        def _(se):
            run_engine(se, "se")

        @block.gpsimd
        def _(ge):
            run_engine(ge, "ge")

        @block.sync
        def _(sy):
            run_engine(sy, "sy")


def _dram_dedup(nc, fc, name, shape, dt_):
    if fc is not None:
        cache = fc.setdefault("dram_cache", {})
        if name not in cache:
            cache[name] = nc.dram_tensor(name, shape, dt_, kind="ExternalInput").ap()
        return cache[name]
    return nc.dram_tensor(name, shape, dt_, kind="ExternalInput").ap()



def build_l0_attn(T=T, NT=NT, fc=None, phases=('A0','MC','A1','A2','A2qk','A2v','A2n','A2a','A3')):
    n512 = T // 512
    nc = fc["nc"] if fc else bass.Bass("TRN2", target_bir_lowering=False, detect_race_conditions=False)
    px = fc["px"] if fc else ""

    x_in = fc["x_in"] if fc else nc.dram_tensor("x", [T, D], F32, kind="ExternalInput").ap()
    wq_d = nc.dram_tensor("wq", [D, D], F32, kind="ExternalInput").ap()
    wk_d = nc.dram_tensor("wk", [D, D], F32, kind="ExternalInput").ap()
    wv_d = nc.dram_tensor("wv", [D, D], F32, kind="ExternalInput").ap()
    lns_fm_d = nc.dram_tensor("lns_fm", [128, NCH], F32, kind="ExternalInput").ap()
    lnb_fm_d = nc.dram_tensor("lnb_fm", [128, NCH], F32, kind="ExternalInput").ap()
    lns_row_d = nc.dram_tensor("lns_row", [1, D], F32, kind="ExternalInput").ap()
    lnb_row_d = nc.dram_tensor("lnb_row", [1, D], F32, kind="ExternalInput").ap()
    identbf_d = _dram_dedup(nc, fc, "identbf", [128, 128], BF16)
    identf_d = _dram_dedup(nc, fc, "identf", [128, 128], F32)
    bd65_d = nc.dram_tensor("bd65", [128, 65], BF16, kind="ExternalInput").ap()
    x_out = fc["x_out"] if fc else nc.dram_tensor("y", [T, D], F32, kind="ExternalOutput").ap()
    dbg = "DBG" in phases
    if dbg:
        dbg_outs = {n: nc.dram_tensor("dbg_" + n, sh, F32, kind="ExternalOutput").ap()
                    for n, sh in [("qa0", [128, T]), ("ka0", [128, T]), ("qa1", [128, T]),
                                  ("vp0", [128, 130]), ("otm0", [128, D]),
                                  ("Sb", [128, D]), ("Bb", [128, D]),
                                  ("mcol", [128, NT]), ("Ccol", [128, NT]),
                                  ("ssum", [128, NT]), ("ssq", [128, NT]),
                                  ("tmp1", [128, NT]), ("tmp2", [128, NT]),
                                  ("xnfm0", [128, T]), ("mk2a", [128, 1])]}

    x_t = x_in.rearrange("(n p) d -> n p d", p=128)
    y_t = x_out.rearrange("(n p) d -> n p d", p=128)

    p = fc["p"] if fc else Prog(nc)

    with ExitStack() as ctx:
        e = ctx.enter_context

        Xbuf = [e(nc.sbuf_tensor(px + f"Xbuf{i}", [128, D], F32)) for i in range(2)]
        Ybuf = [e(nc.sbuf_tensor(px + f"Ybuf{i}", [128, D], F32)) for i in range(2)]
        ssum = e(nc.sbuf_tensor(px + "ssum", [128, NT], F32))
        ssq = e(nc.sbuf_tensor(px + "ssq", [128, NT], F32))
        mcol = e(nc.sbuf_tensor(px + "mcol", [128, NT], F32))
        Ccol = e(nc.sbuf_tensor(px + "Ccol", [128, NT], F32))
        tmp1 = e(nc.sbuf_tensor(px + "tmp1", [128, NT], F32))
        tmp2 = e(nc.sbuf_tensor(px + "tmp2", [128, NT], F32))
        xnbuf = [e(nc.sbuf_tensor(px + f"xnbuf{i}", [128, D], BF16)) for i in range(2)]
        xn_fm = [e(nc.sbuf_tensor(px + f"xnfm{c}", [128, T], BF16)) for c in range(NCH)]
        o_tm = [e(nc.sbuf_tensor(px + f"otm{t}", [128, D], BF16)) for t in range(NT)]
        v_pair = [e(nc.sbuf_tensor(px + f"vp{t}", [128, 130], BF16)) for t in range(NT)]
        qa = [e(nc.sbuf_tensor(px + f"qa{h}", [128, T], BF16)) for h in range(2)]
        ka = [e(nc.sbuf_tensor(px + f"ka{h}", [128, T], BF16)) for h in range(2)]
        mk2a = e(nc.sbuf_tensor(px + "mk2a", [128, 1], F32))
        mk2t = e(nc.sbuf_tensor(px + "mk2t", [128, 1], F32))
        dumV = e(nc.sbuf_tensor(px + "dumV", [1, 1], F32))
        dumS = e(nc.sbuf_tensor(px + "dumS", [1, 1], F32))
        dumG = e(nc.sbuf_tensor(px + "dumG", [1, 1], F32))
        Pt = [e(nc.sbuf_tensor(px + f"Pt{i}", [128, 512], BF16)) for i in range(2)]
        oT = e(nc.sbuf_tensor(px + "oT", [65, 512], F32))
        den = e(nc.sbuf_tensor(px + "den", [128, 1], F32))
        sqq = e(nc.sbuf_tensor(px + "sqq", [128, 512], BF16))
        sqk = e(nc.sbuf_tensor(px + "sqk", [128, 512], BF16))
        Sb = e(nc.sbuf_tensor(px + "Sb", [128, D], BF16))
        Bb = e(nc.sbuf_tensor(px + "Bb", [128, D], BF16))
        sfm = e(nc.sbuf_tensor(px + "sfm", [128, NCH], F32))
        bfm = e(nc.sbuf_tensor(px + "bfm", [128, NCH], F32))
        identbf = e(nc.sbuf_tensor(px + "identbf_s", [128, 128], BF16))
        identf = e(nc.sbuf_tensor(px + "identf_s", [128, 128], F32))
        bd65 = e(nc.sbuf_tensor(px + "bd65_s", [128, 65], BF16))
        ones1 = e(nc.sbuf_tensor(px + "ones1", [1, 128], BF16))
        wqs = e(nc.sbuf_tensor(px + "wqs", [128, NCH, 128], BF16))
        wks = e(nc.sbuf_tensor(px + "wks", [128, NCH, 128], BF16))
        wvs = e(nc.sbuf_tensor(px + "wvs", [128, NCH, 128], BF16))

        psumT8 = e(nc.psum_tensor(px + "psumT8", [128, 1024], BF16))
        psA = e(nc.psum_tensor(px + "psA", [128, 512], F32))
        psB = e(nc.psum_tensor(px + "psB", [128, 512], F32))
        psC = e(nc.psum_tensor(px + "psC", [128, 512], F32))
        psD = e(nc.psum_tensor(px + "psD", [128, 512], F32))
        psE = e(nc.psum_tensor(px + "psE", [128, 512], F32))
        psF = e(nc.psum_tensor(px + "psF", [128, 512], F32))

        sem_names = ["dma_ge", "dma_sy", "dma_st", "bar", "bc",
                     "a0", "a1x", "a1t", "a1e", "proj",
                     "s_q", "s_k", "s_v", "s_evq", "s_evk", "s_evv",
                     "s_nq", "s_nk", "s_evn",
                     "s_s", "s_p", "s_pv", "s_eo", "s_tr", "s_no",
                     "a3v"]
        S = fc["S"] if fc else {n: e(nc.semaphore(n)) for n in sem_names}
        block = None if fc else e(nc.Block())

        if "noC0sy" not in phases:
            p.load("sy", identbf[:], identbf_d)
            p.load("sy", identf[:], identf_d)
            p.load("sy", bd65[:], bd65_d)
            p.load("sy", sfm[:], lns_fm_d)
            p.load("sy", bfm[:], lnb_fm_d)
        if "noC0ge" not in phases:
            p.load("ge", Sb[0:1, :], lns_row_d)
            p.load("ge", Bb[0:1, :], lnb_row_d)

        wcnt = [None] * 8

        def _wslice(wd, pair):
            return wd[:, pair * 128:(pair + 1) * 128].rearrange(
                "(c p) m -> p c m", p=128)

        if "noC0w" not in phases:
            p.load("ge", wqs[:], _wslice(wq_d, 0))
            p.load("ge", wks[:], _wslice(wk_d, 0))
            wcnt[0] = p.load("ge", wvs[:], _wslice(wv_d, 0))

        def st_memset(key, ap, val):
            def fn(eng, ap=ap, val=val):
                return eng.memset(ap, val)
            p.step(key, fn)

        st_memset("ve", ones1[:], 1.0)
        st_memset("ve", dumV[:], 0.0)
        st_memset("ge", dumG[:], 0.0)

        def fn_dumS(eng):
            return eng.memzero(dumS[:])
        p.step("se", fn_dumS)
        for t in (range(NT) if "noC0ms" not in phases else []):
            st_memset("ve", v_pair[t][:, 64:65], 1.0)
            st_memset("ve", v_pair[t][:, 129:130], 1.0)
        if "noC0ms" not in phases:
            for h in range(2):
                st_memset("ve", qa[h][:], 0.0)
                st_memset("ve", ka[h][:], 0.0)
            st_memset("ve", ka[0][64:65, :], -1.0)
            st_memset("ve", ka[1][32:33, :], -1.0)

        p.barrier()

        _bc = "noC0bc" not in phases
        def fn_bc1(eng):
            eng.matmul(psA[:, 0:512], ones1[:], Sb[0:1, 0:512], start=True, stop=True)
            eng.matmul(psB[:, 0:512], ones1[:], Sb[0:1, 512:1024], start=True, stop=True)
            eng.matmul(psC[:, 0:512], ones1[:], Bb[0:1, 0:512], start=True, stop=True)
            return eng.matmul(psD[:, 0:512], ones1[:], Bb[0:1, 512:1024], start=True,
                              stop=True).then_inc(S["bc"], 1)
        if _bc: p.step("te", fn_bc1)
        p.barrier()

        def fn_bc2(eng):
            eng.wait_ge(S["bc"], 1)
            eng.tensor_copy(out=Sb[:, 0:512], in_=psA[:, 0:512])
            eng.tensor_copy(out=Sb[:, 512:1024], in_=psB[:, 0:512])
            eng.tensor_copy(out=Bb[:, 0:512], in_=psC[:, 0:512])
            return eng.tensor_copy(out=Bb[:, 512:1024], in_=psD[:, 0:512])
        if _bc and "noC0bc2" not in phases: p.step("ve", fn_bc2)
        p.barrier()

        for t in (range(NT) if 'A0' in phases else []):
            if t >= 2:
                need = 2 * (t - 1)

                def fn_w(eng, need=need):
                    eng.wait_ge(S["a0"], need)
                    return None
                p.step("sy", fn_w)
            lc = p.load("sy", Xbuf[t % 2][:], x_t[t])

            def fn_v(eng, t=t, lc=lc):
                eng.wait_ge(S["dma_sy"], lc)
                return eng.tensor_scalar(
                    out=Ybuf[0][:], in0=Xbuf[t % 2][:], scalar1=1.0, scalar2=0.0,
                    op0=ALU.mult, op1=ALU.add,
                    accum_out=ssum[:, t:t + 1]).then_inc(S["a0"], 1)
            p.step("ve", fn_v)
            p.inc("a0")

            def fn_s(eng, t=t, lc=lc):
                eng.wait_ge(S["dma_sy"], lc)
                return eng.activation(
                    out=Ybuf[1][:], in_=Xbuf[t % 2][:], func=AF.Square,
                    accum_out=ssq[:, t:t + 1]).then_inc(S["a0"], 1)
            p.step("se", fn_s)
            p.inc("a0")

        p.barrier()

        _do_mc = 'MC' in phases
        def fn_mc1(eng):
            eng.drain()
            eng.tensor_scalar(out=mcol[:], in0=ssum[:], scalar1=1.0 / D,
                              scalar2=None, op0=ALU.mult, op1=ALU.bypass)
            eng.tensor_scalar(out=tmp1[:], in0=ssq[:], scalar1=1.0 / D,
                              scalar2=None, op0=ALU.mult, op1=ALU.bypass)
            eng.drain()
            eng.tensor_tensor(out=tmp2[:], in0=mcol[:], in1=mcol[:], op=ALU.mult)
            eng.drain()
            eng.tensor_tensor(out=tmp1[:], in0=tmp1[:], in1=tmp2[:], op=ALU.subtract)
            eng.drain()
            eng.tensor_scalar(out=tmp2[:], in0=tmp1[:], scalar1=1e-5,
                              scalar2=None, op0=ALU.add, op1=ALU.bypass)
            eng.drain()
            eng.reciprocal(out=Ccol[:], in_=tmp2[:])
            eng.drain()
            eng.tensor_tensor(out=tmp1[:], in0=tmp1[:], in1=Ccol[:], op=ALU.mult)
            eng.drain()
            eng.tensor_scalar(out=tmp1[:], in0=tmp1[:], scalar1=1e-6,
                              scalar2=None, op0=ALU.add, op1=ALU.bypass)
            eng.drain()
            return eng.tensor_tensor(out=tmp1[:], in0=tmp2[:], in1=tmp1[:], op=ALU.mult)
        if _do_mc: p.step("ve", fn_mc1)
        p.barrier()

        def fn_mc2(eng):
            return eng.activation(out=tmp2[:], in_=tmp1[:], func=AF.Sqrt)
        if _do_mc: p.step("se", fn_mc2)
        p.barrier()

        def fn_mc3(eng):
            return eng.reciprocal(out=Ccol[:], in_=tmp2[:])
        if _do_mc: p.step("ve", fn_mc3)
        p.barrier()

        for t in (range(NT) if 'A1' in phases else []):
            if t >= 2:
                def fn_w(eng, need=t - 1):
                    eng.wait_ge(S["a1x"], need)
                    return None
                p.step("sy", fn_w)
            lc = p.load("sy", Xbuf[t % 2][:], x_t[t])

            def fn_v(eng, t=t, lc=lc):
                eng.wait_ge(S["dma_sy"], lc)
                if t >= 2:
                    eng.wait_ge(S["a1t"], t - 1)
                return eng.tensor_scalar(
                    out=xnbuf[t % 2][:], in0=Xbuf[t % 2][:],
                    scalar1=mcol[:, t:t + 1], scalar2=Ccol[:, t:t + 1],
                    op0=ALU.subtract, op1=ALU.mult).then_inc(S["a1x"], 1)
            p.step("ve", fn_v)
            p.inc("a1x")

            def fn_t(eng, t=t):
                eng.wait_ge(S["a1x"], t + 1)
                if t >= 1:
                    eng.wait_ge(S["a1e"], t)
                r = None
                for c in range(NCH):
                    r = eng.transpose(psumT8[:, c * 128:(c + 1) * 128],
                                      xnbuf[t % 2][:, c * 128:(c + 1) * 128],
                                      identbf[:])
                return r.then_inc(S["a1t"], 1)
            p.step("te", fn_t)
            p.inc("a1t")

            def fn_e(eng, t=t):
                eng.wait_ge(S["a1t"], t + 1)
                r = None
                for c in range(NCH):
                    r = eng.tensor_scalar(
                        out=xn_fm[c][:, t * 128:(t + 1) * 128],
                        in0=psumT8[:, c * 128:(c + 1) * 128],
                        scalar1=sfm[:, c:c + 1], scalar2=bfm[:, c:c + 1],
                        op0=ALU.mult, op1=ALU.add)
                return r.then_inc(S["a1e"], 1)
            p.step("ve", fn_e)
            p.inc("a1e")

        p.barrier()

        for pair in (range(8) if 'A2' in phases else []):
            st_memset("ve", mk2a[:], 0.0)

            for t8 in range(n512):
                sl512 = slice(t8 * 512, (t8 + 1) * 512)

                need = p.val("s_evq")

                def fn_qmm(eng, pair=pair, sl512=sl512, need=need):
                    eng.wait_ge(S["dma_ge"], wcnt[pair])
                    if need:
                        eng.wait_ge(S["s_evq"], need)
                    r = None
                    for c in range(NCH):
                        r = eng.matmul(psA[:, 0:512], wqs[:, c, :],
                                       xn_fm[c][:, sl512],
                                       start=(c == 0), stop=(c == NCH - 1))
                    return r.then_inc(S["s_q"], 1)
                if 'A2qk' in phases:
                    p.step("te", fn_qmm)
                sq_tgt = p.inc("s_q") if 'A2qk' in phases else 0

                def fn_qev(eng, sl512=sl512, tgt=sq_tgt):
                    eng.wait_ge(S["s_q"], tgt)
                    eng.copy(out=qa[0][0:64, sl512], in_=psA[0:64, 0:512])
                    eng.copy(out=qa[1][64:128, sl512], in_=psA[64:128, 0:512])
                    return eng.activation(out=sqq[:], in_=psA[:, 0:512],
                                          func=AF.Square).then_inc(S["s_evq"], 1)
                if 'A2qk' in phases:
                    p.step("se", fn_qev)
                evq_tgt = p.inc("s_evq") if 'A2qk' in phases else 0

                need = p.val("s_evk")

                def fn_kmm(eng, sl512=sl512, need=need):
                    if need:
                        eng.wait_ge(S["s_evk"], need)
                    r = None
                    for c in range(NCH):
                        r = eng.matmul(psB[:, 0:512], wks[:, c, :],
                                       xn_fm[c][:, sl512],
                                       start=(c == 0), stop=(c == NCH - 1))
                    return r.then_inc(S["s_k"], 1)
                if 'A2qk' in phases:
                    p.step("te", fn_kmm)
                sk_tgt = p.inc("s_k") if 'A2qk' in phases else 0

                def fn_kev(eng, sl512=sl512, tgt=sk_tgt):
                    eng.wait_ge(S["s_k"], tgt)
                    eng.copy(out=ka[0][0:64, sl512], in_=psB[0:64, 0:512])
                    eng.copy(out=ka[1][64:128, sl512], in_=psB[64:128, 0:512])
                    return eng.activation(out=sqk[:], in_=psB[:, 0:512],
                                          func=AF.Square).then_inc(S["s_evk"], 1)
                if 'A2qk' in phases:
                    p.step("se", fn_kev)
                evk_tgt = p.inc("s_evk") if 'A2qk' in phases else 0

                for tt in range(4):
                    ti = t8 * 4 + tt
                    need = p.val("s_evv")

                    def fn_vmm(eng, ti=ti, need=need):
                        if need:
                            eng.wait_ge(S["s_evv"], need)
                        r = None
                        for c in range(NCH):
                            r = eng.matmul(psC[:, 0:128],
                                           xn_fm[c][:, ti * 128:(ti + 1) * 128],
                                           wvs[:, c, :],
                                           start=(c == 0), stop=(c == NCH - 1))
                        return r.then_inc(S["s_v"], 1)
                    if 'A2v' in phases:
                        p.step("te", fn_vmm)
                    sv_tgt = p.inc("s_v") if 'A2v' in phases else 0

                    def fn_vev(eng, ti=ti, tgt=sv_tgt):
                        eng.wait_ge(S["s_v"], tgt)
                        eng.tensor_copy(out=v_pair[ti][:, 0:64], in_=psC[:, 0:64])
                        return eng.tensor_copy(out=v_pair[ti][:, 65:129],
                                               in_=psC[:, 64:128]).then_inc(S["s_evv"], 1)
                    if 'A2v' in phases:
                        p.step("ve", fn_vev)
                        p.inc("s_evv")

                need = p.val("s_evn")

                def fn_nmm(eng, eq=evq_tgt, ek=evk_tgt, need=need):
                    eng.wait_ge(S["s_evq"], eq)
                    eng.wait_ge(S["s_evk"], ek)
                    if need:
                        eng.wait_ge(S["s_evn"], need)
                    eng.matmul(psD[0:65, 0:512], bd65[:], sqq[:], start=True,
                               stop=True).then_inc(S["s_nq"], 1)
                    return eng.matmul(psE[0:65, 0:512], bd65[:], sqk[:],
                                      start=True, stop=True).then_inc(S["s_nk"], 1)
                if 'A2n' in phases:
                    p.step("te", fn_nmm)
                nq_tgt = p.inc("s_nq") if 'A2n' in phases else 0
                nk_tgt = p.inc("s_nk") if 'A2n' in phases else 0

                def fn_nev(eng, sl512=sl512, nq=nq_tgt, nk=nk_tgt):
                    eng.wait_ge(S["s_nq"], nq)
                    eng.wait_ge(S["s_nk"], nk)
                    eng.tensor_copy(out=qa[0][64:65, sl512], in_=psD[64:65, 0:512])
                    eng.tensor_copy(out=qa[1][32:33, sl512], in_=psD[32:33, 0:512])
                    eng.reduce_max(out=mk2t[64:65, :], in_=psE[64:65, 0:512],
                                   axis=AX, negate=False)
                    eng.drain()
                    eng.tensor_tensor(out=mk2a[64:65, :], in0=mk2a[64:65, :],
                                      in1=mk2t[64:65, :], op=ALU.max)
                    eng.reduce_max(out=mk2t[32:33, :], in_=psE[32:33, 0:512],
                                   axis=AX, negate=False)
                    eng.drain()
                    return eng.tensor_tensor(out=mk2a[32:33, :],
                                             in0=mk2a[32:33, :],
                                             in1=mk2t[32:33, :],
                                             op=ALU.max).then_inc(S["s_evn"], 1)
                if 'A2n' in phases:
                    p.step("ve", fn_nev)
                    p.inc("s_evn")

            def fn_projdone(eng):
                eng.sem_inc(S["proj"], 1)
                return None
            p.step("te", fn_projdone)
            p.inc("proj")

            p.barrier()

            def fn_brow(eng):
                eng.drain()
                eng.tensor_scalar(out=qa[0][64:65, :], in0=qa[0][64:65, :],
                                  scalar1=mk2a[64:65, :], scalar2=0.5,
                                  op0=ALU.add, op1=ALU.mult)
                return eng.tensor_scalar(out=qa[1][32:33, :],
                                         in0=qa[1][32:33, :],
                                         scalar1=mk2a[32:33, :], scalar2=0.5,
                                         op0=ALU.add, op1=ALU.mult)
            if 'A2a' in phases: p.step("ve", fn_brow)
            p.barrier()

            for hh in (range(2) if 'A2a' in phases else []):
                head = pair * 2 + hh
                vsl = slice(hh * 65, hh * 65 + 65)
                for q8 in range(n512):
                    qsl = slice(q8 * 512, (q8 + 1) * 512)
                    g = p.val("s_eo")
                    pvbank = psC if g % 2 == 0 else psD
                    exp_tgts = []

                    def emit_S(kt):
                        n = p.val("s_s")
                        war = n - 1
                        sbank = psA if (n % 2 == 0) else psB

                        def fn_smm(eng, hh=hh, kt=kt, qsl=qsl, sbank=sbank,
                                   war=war):
                            if war >= 1:
                                eng.wait_ge(S["s_p"], war)
                            return eng.matmul(
                                sbank[:, 0:512],
                                ka[hh][:, kt * 128:(kt + 1) * 128],
                                qa[hh][:, qsl], start=True,
                                stop=True).then_inc(S["s_s"], 1)
                        p.step("te", fn_smm)
                        ss_tgt = p.inc("s_s")

                        ne = p.val("s_p")
                        ptwar = ne - 1

                        def fn_exp(eng, sbank=sbank, ss=ss_tgt, ne=ne,
                                   ptwar=ptwar):
                            eng.wait_ge(S["s_s"], ss)
                            if ptwar >= 1:
                                eng.wait_ge(S["s_pv"], ptwar)
                            eng.activation(
                                out=Pt[ne % 2][:], in_=sbank[:, 0:512],
                                func=AF.Exp, scale=1.0 / 32.0)
                            eng.drain()
                            return eng.copy(out=dumS[:], in_=dumS[:]
                                            ).then_inc(S["s_p"], 1)
                        p.step("se", fn_exp)
                        exp_tgts.append((p.inc("s_p"), (p.val("s_p") - 1) % 2))

                    def emit_PV(kt, g=g, pvbank=pvbank, vsl=vsl):
                        sp_tgt, pt_i = exp_tgts[kt]
                        eo_war = g - 1 if kt == 0 else -1

                        def fn_pv(eng, kt=kt, sp=sp_tgt, pt_i=pt_i,
                                  eo_war=eo_war, pvbank=pvbank, vsl=vsl):
                            eng.wait_ge(S["s_p"], sp)
                            if eo_war >= 1:
                                eng.wait_ge(S["s_eo"], eo_war)
                            return eng.matmul(
                                pvbank[0:65, 0:512], v_pair[kt][:, vsl],
                                Pt[pt_i][:], start=(kt == 0),
                                stop=(kt == NT - 1)).then_inc(S["s_pv"], 1)
                        p.step("te", fn_pv)
                        p.inc("s_pv")

                    emit_S(0)
                    for kt in range(1, NT):
                        emit_S(kt)
                        emit_PV(kt - 1)
                    emit_PV(NT - 1)

                    pv_done = p.val("s_pv")
                    tr_before = p.val("s_tr")

                    def fn_eo(eng, pvbank=pvbank, pv=pv_done, trb=tr_before):
                        eng.wait_ge(S["s_pv"], pv)
                        if trb >= 1:
                            eng.wait_ge(S["s_tr"], trb)
                        eng.copy(out=oT[:], in_=pvbank[0:65, 0:512])
                        eng.drain()
                        return eng.copy(out=dumS[:], in_=dumS[:]
                                        ).then_inc(S["s_eo"], 1)
                    p.step("se", fn_eo)
                    eo_tgt = p.inc("s_eo")

                    for qq in range(4):
                        ti = q8 * 4 + qq
                        j = p.val("s_tr")
                        trbank = psE if j % 2 == 0 else psF
                        no_war = j - 1

                        def fn_tr(eng, qq=qq, eo=eo_tgt, trbank=trbank,
                                  war=no_war):
                            eng.wait_ge(S["s_eo"], eo)
                            if war >= 1:
                                eng.wait_ge(S["s_no"], war)
                            return eng.transpose(
                                trbank[:, 0:65], oT[:, qq * 128:(qq + 1) * 128],
                                identf[0:65, 0:65]).then_inc(S["s_tr"], 1)
                        p.step("te", fn_tr)
                        tr_tgt = p.inc("s_tr")

                        def fn_no(eng, ti=ti, head=head, trbank=trbank,
                                  tr=tr_tgt):
                            eng.wait_ge(S["s_tr"], tr)
                            eng.reciprocal(out=den[:], in_=trbank[:, 64:65])
                            eng.drain()
                            return eng.scalar_tensor_tensor(
                                out=o_tm[ti][:, head * 64:(head + 1) * 64],
                                in0=trbank[:, 0:64], scalar=den[:],
                                in1=Bb[:, head * 64:(head + 1) * 64],
                                op0=ALU.mult, op1=ALU.add).then_inc(S["s_no"], 1)
                        p.step("ve", fn_no)
                        p.inc("s_no")

            if pair + 1 < 8:
                _wq = p.val("s_evq"); _wk = p.val("s_evk")
                _wv = p.val("s_evv"); _wn = p.val("s_evn")

                def fn_wp(eng, a=_wq, b=_wk, c=_wv, d=_wn):
                    if a: eng.wait_ge(S["s_evq"], a)
                    if b: eng.wait_ge(S["s_evk"], b)
                    if c: eng.wait_ge(S["s_evv"], c)
                    if d: eng.wait_ge(S["s_evn"], d)
                    return None
                p.step("ge", fn_wp)
                p.load("ge", wqs[:], _wslice(wq_d, pair + 1))
                p.load("ge", wks[:], _wslice(wk_d, pair + 1))
                wcnt[pair + 1] = p.load("ge", wvs[:], _wslice(wv_d, pair + 1))

            p.barrier()

        for t in (range(NT) if 'A3' in phases else []):
            _a3p = p.val("a3v")
            if _a3p >= 2:
                def fn_w(eng, need=_a3p - 1):
                    eng.wait_ge(S["a3v"], need)
                    return None
                p.step("sy", fn_w)
            lc = p.load("sy", Xbuf[t % 2][:], x_t[t])
            _stp = (p.dma_cnt["st"] - 1) * 16

            def fn_v(eng, t=t, lc=lc, stp=_stp):
                eng.wait_ge(S["dma_sy"], lc)
                if stp >= 16:
                    eng.wait_ge(S["dma_st"], stp)
                yb = Ybuf[t % 2]
                eng.tensor_scalar(out=yb[:], in0=Xbuf[t % 2][:],
                                  scalar1=mcol[:, t:t + 1], scalar2=Ccol[:, t:t + 1],
                                  op0=ALU.subtract, op1=ALU.mult)
                eng.drain()
                eng.tensor_tensor(out=yb[:], in0=yb[:], in1=Sb[:], op=ALU.mult)
                eng.drain()
                eng.tensor_tensor(out=yb[:], in0=yb[:], in1=Xbuf[t % 2][:],
                                  op=ALU.add)
                eng.drain()
                return eng.tensor_tensor(out=yb[:], in0=yb[:],
                                         in1=o_tm[t][:], op=ALU.add).then_inc(S["a3v"], 1)
            p.step("ve", fn_v)
            a3v_tgt = p.inc("a3v")

            def fn_st_w(eng, tgt=a3v_tgt):
                eng.wait_ge(S["a3v"], tgt)
                return None
            p.step("sy", fn_st_w)
            p.store(y_t[t], Ybuf[t % 2][:])

        p.barrier()
        if dbg:
            def _cast(name, src_tile):
                def fn(eng, name=name, src_tile=src_tile):
                    return eng.tensor_copy(out=dbg_sb[name][:], in_=src_tile)
                p.step("ve", fn)
            dbg_sb = {}
            for name, tile in [("qa0", qa[0]), ("ka0", ka[0]), ("qa1", qa[1]),
                               ("vp0", v_pair[0]), ("otm0", o_tm[0]),
                               ("Sb", Sb), ("Bb", Bb)]:
                dbg_sb[name] = e(nc.sbuf_tensor(px + "dbgs_" + name, list(tile.shape), F32))
                _cast(name, tile[:])
            p.barrier()
            for name in dbg_sb:
                p.store(dbg_outs[name], dbg_sb[name][:])
            p.store(dbg_outs["mcol"], mcol[:])
            p.store(dbg_outs["Ccol"], Ccol[:])
            p.store(dbg_outs["ssum"], ssum[:])
            p.store(dbg_outs["ssq"], ssq[:])
            p.store(dbg_outs["tmp1"], tmp1[:])
            p.store(dbg_outs["tmp2"], tmp2[:])
            p.store(dbg_outs["mk2a"], mk2a[:])
            def fnx(eng):
                return eng.tensor_copy(out=dbg_sb["qa0"][:], in_=xn_fm[0][:])
            p.step("ve", fnx)
            p.barrier()
            p.store(dbg_outs["xnfm0"], dbg_sb["qa0"][:])
            p.barrier()
        if not fc:
            p.finalize(block, S, {"ve": dumV[:], "se": dumS[:], "ge": dumG[:]})

    return nc



def build_l0_ffn(T=T, NT=NT, fc=None):
    n512 = T // 512
    NGRP = 2 if T >= 4096 else 1
    GT = T // NGRP
    gNT = GT // 128
    gn512 = GT // 512

    nc = fc["nc"] if fc else bass.Bass("TRN2", target_bir_lowering=False, detect_race_conditions=False)
    px = fc["px"] if fc else ""

    x_in = fc["x_in"] if fc else nc.dram_tensor("x", [T, D], F32, kind="ExternalInput").ap()
    gw_d = nc.dram_tensor("gwT", [D, 8], F32, kind="ExternalInput").ap()
    w1_d = nc.dram_tensor("w1", [9, D, D], F32, kind="ExternalInput").ap()
    w2_d = nc.dram_tensor("w2", [9, D, D], F32, kind="ExternalInput").ap()
    w3_d = nc.dram_tensor("w3", [9, D, D], F32, kind="ExternalInput").ap()
    identf_d = _dram_dedup(nc, fc, "identf", [128, 128], F32)
    identbf_d = _dram_dedup(nc, fc, "identbf", [128, 128], BF16)
    x_out = fc["x_out"] if fc else nc.dram_tensor("y", [T, D], F32, kind="ExternalOutput").ap()

    x_t = x_in.rearrange("(n p) d -> n p d", p=128)
    y_t = x_out.rearrange("(n p) d -> n p d", p=128)

    p = fc["p"] if fc else Prog(nc)

    with ExitStack() as ctx:
        e = ctx.enter_context

        Xbuf = [e(nc.sbuf_tensor(px + f"Xbuf{i}", [128, D], F32)) for i in range(2)]
        Ybuf = [e(nc.sbuf_tensor(px + f"Ybuf{i}", [128, D], F32)) for i in range(2)]
        ssum = e(nc.sbuf_tensor(px + "ssum", [128, NT], F32))
        ssq = e(nc.sbuf_tensor(px + "ssq", [128, NT], F32))
        mcol = e(nc.sbuf_tensor(px + "mcol", [128, NT], F32))
        Ccol = e(nc.sbuf_tensor(px + "Ccol", [128, NT], F32))
        tmp1 = e(nc.sbuf_tensor(px + "tmp1", [128, NT], F32))
        tmp2 = e(nc.sbuf_tensor(px + "tmp2", [128, NT], F32))
        xnbuf = [e(nc.sbuf_tensor(px + f"xnbuf{i}", [128, D], BF16)) for i in range(2)]
        xn_fm = [e(nc.sbuf_tensor(px + f"xnfm{c}", [128, GT], BF16)) for c in range(NCH)]
        yacc = [e(nc.sbuf_tensor(px + f"yacc{t}", [128, D], F32)) for t in range(gNT)]
        g_sb = e(nc.sbuf_tensor(px + "g_sb", [128, 8 * gNT], F32))
        gsc = e(nc.sbuf_tensor(px + "gsc", [128, 8], F32))
        gex = e(nc.sbuf_tensor(px + "gex", [128, 8], F32))
        gm = e(nc.sbuf_tensor(px + "gm", [128, 1], F32))
        gs = e(nc.sbuf_tensor(px + "gs", [128, 1], F32))
        gscr = e(nc.sbuf_tensor(px + "gscr", [128, 8], F32))
        rowsb = e(nc.sbuf_tensor(px + "rowsb", [1, 512], F32))
        ones1f = e(nc.sbuf_tensor(px + "ones1f", [1, 128], F32))
        Ge = e(nc.sbuf_tensor(px + "Ge", [128, 512], BF16))
        xe_fm = [e(nc.sbuf_tensor(px + f"xefm{c}", [128, 512], BF16)) for c in range(NCH)]
        asb = [e(nc.sbuf_tensor(px + f"asb{i}", [128, 512], BF16)) for i in range(2)]
        h_fm = [e(nc.sbuf_tensor(px + f"hfm{c}", [128, 512], BF16)) for c in range(NCH)]
        w1s = e(nc.sbuf_tensor(px + "w1s", [128, NCH, D], BF16))
        w2s = e(nc.sbuf_tensor(px + "w2s", [128, NCH, D], BF16))
        w3s = e(nc.sbuf_tensor(px + "w3s", [128, NCH, D], BF16))
        gwT = e(nc.sbuf_tensor(px + "gwT_s", [128, NCH, 8], BF16))
        identf = e(nc.sbuf_tensor(px + "identf_s", [128, 128], F32))
        identbf = e(nc.sbuf_tensor(px + "identbf_s", [128, 128], BF16))
        ones1 = e(nc.sbuf_tensor(px + "ones1", [1, 128], BF16))
        dumV = e(nc.sbuf_tensor(px + "dumV", [1, 1], F32))
        dumS = e(nc.sbuf_tensor(px + "dumS", [1, 1], F32))
        dumG = e(nc.sbuf_tensor(px + "dumG", [1, 1], F32))

        psumT8 = e(nc.psum_tensor(px + "psumT8", [128, 1024], BF16))
        psL = e(nc.psum_tensor(px + "psL", [128, 512], F32))
        psA = e(nc.psum_tensor(px + "psA", [128, 512], F32))
        psB = e(nc.psum_tensor(px + "psB", [128, 512], F32))
        psC = e(nc.psum_tensor(px + "psC", [128, 512], F32))
        psR = e(nc.psum_tensor(px + "psR", [128, 512], F32))

        sem_names = ["dma_ge", "dma_sy", "dma_st", "bar",
                     "a0", "a1x", "a1t", "a1e",
                     "b_l", "b_g", "b_row", "b_bc", "b_x",
                     "b_a", "b_as", "b_b", "b_h", "b_o", "b_y",
                     "a3v"]
        S = fc["S"] if fc else {n: e(nc.semaphore(n)) for n in sem_names}
        block = None if fc else e(nc.Block())

        p.load("sy", identf[:], identf_d)
        p.load("sy", identbf[:], identbf_d)
        p.load("ge", gwT[:], gw_d.rearrange("(c p) e -> p c e", p=128))

        wcnt = {}

        def load_expert(eidx):
            p.load("ge", w1s[:], w1_d[eidx].rearrange("(c p) n -> p c n", p=128))
            p.load("ge", w3s[:], w3_d[eidx].rearrange("(c p) n -> p c n", p=128))
            wcnt[eidx] = p.load("ge", w2s[:],
                                w2_d[eidx].rearrange("(c p) n -> p c n", p=128))

        load_expert(0)

        def st_memset(key, ap, val):
            def fn(eng, ap=ap, val=val):
                return eng.memset(ap, val)
            p.step(key, fn)

        st_memset("ve", ones1[:], 1.0)
        st_memset("ve", ones1f[:], 1.0)
        st_memset("ve", dumV[:], 0.0)
        st_memset("ge", dumG[:], 0.0)

        def fn_dumS(eng):
            return eng.memzero(dumS[:])
        p.step("se", fn_dumS)

        p.barrier()

        _a0b = p.val("a0")
        for t in range(NT):
            if t >= 2:
                def fn_w(eng, need=_a0b + 2 * (t - 1)):
                    eng.wait_ge(S["a0"], need)
                    return None
                p.step("sy", fn_w)
            lc = p.load("sy", Xbuf[t % 2][:], x_t[t])

            def fn_v(eng, t=t, lc=lc):
                eng.wait_ge(S["dma_sy"], lc)
                return eng.tensor_scalar(
                    out=Ybuf[0][:], in0=Xbuf[t % 2][:], scalar1=1.0, scalar2=0.0,
                    op0=ALU.mult, op1=ALU.add,
                    accum_out=ssum[:, t:t + 1]).then_inc(S["a0"], 1)
            p.step("ve", fn_v)
            p.inc("a0")

            def fn_s(eng, t=t, lc=lc):
                eng.wait_ge(S["dma_sy"], lc)
                return eng.activation(
                    out=Ybuf[1][:], in_=Xbuf[t % 2][:], func=AF.Square,
                    accum_out=ssq[:, t:t + 1]).then_inc(S["a0"], 1)
            p.step("se", fn_s)
            p.inc("a0")

        p.barrier()

        def fn_mc1(eng):
            eng.drain()
            eng.tensor_scalar(out=mcol[:], in0=ssum[:], scalar1=1.0 / D,
                              scalar2=None, op0=ALU.mult, op1=ALU.bypass)
            eng.tensor_scalar(out=tmp1[:], in0=ssq[:], scalar1=1.0 / D,
                              scalar2=None, op0=ALU.mult, op1=ALU.bypass)
            eng.drain()
            eng.tensor_tensor(out=tmp2[:], in0=mcol[:], in1=mcol[:], op=ALU.mult)
            eng.drain()
            eng.tensor_tensor(out=tmp1[:], in0=tmp1[:], in1=tmp2[:], op=ALU.subtract)
            eng.drain()
            return eng.tensor_scalar(out=tmp2[:], in0=tmp1[:], scalar1=1e-5,
                                     scalar2=None, op0=ALU.add, op1=ALU.bypass)
        p.step("ve", fn_mc1)
        p.barrier()

        def fn_mc2(eng):
            return eng.activation(out=tmp1[:], in_=tmp2[:], func=AF.Sqrt)
        p.step("se", fn_mc2)
        p.barrier()

        def fn_mc3(eng):
            return eng.reciprocal(out=Ccol[:], in_=tmp1[:])
        p.step("ve", fn_mc3)
        p.barrier()

        def emit_b1b2(grp):
            for ti in range(gNT):
                t = grp * gNT + ti
                nx = p.val("a1x")
                if nx >= 2:
                    def fn_w(eng, need=nx - 1):
                        eng.wait_ge(S["a1x"], need)
                        return None
                    p.step("sy", fn_w)
                lc = p.load("sy", Xbuf[ti % 2][:], x_t[t])

                _tw = p.val("a1t")

                def fn_v(eng, t=t, ti=ti, lc=lc, tw=_tw):
                    eng.wait_ge(S["dma_sy"], lc)
                    if tw >= 2:
                        eng.wait_ge(S["a1t"], tw - 1)
                    return eng.tensor_scalar(
                        out=xnbuf[ti % 2][:], in0=Xbuf[ti % 2][:],
                        scalar1=mcol[:, t:t + 1], scalar2=Ccol[:, t:t + 1],
                        op0=ALU.subtract, op1=ALU.mult).then_inc(S["a1x"], 1)
                p.step("ve", fn_v)
                x_tgt = p.inc("a1x")

                _ew = p.val("a1e")

                def fn_t(eng, ti=ti, xt=x_tgt, ew=_ew):
                    eng.wait_ge(S["a1x"], xt)
                    if ew >= 1:
                        eng.wait_ge(S["a1e"], ew)
                    r = None
                    for c in range(NCH):
                        r = eng.transpose(psumT8[:, c * 128:(c + 1) * 128],
                                          xnbuf[ti % 2][:, c * 128:(c + 1) * 128],
                                          identbf[:])
                    return r.then_inc(S["a1t"], 1)
                p.step("te", fn_t)
                t_tgt = p.inc("a1t")

                def fn_e(eng, ti=ti, tt=t_tgt):
                    eng.wait_ge(S["a1t"], tt)
                    r = None
                    for c in range(NCH):
                        r = eng.tensor_copy(
                            out=xn_fm[c][:, ti * 128:(ti + 1) * 128],
                            in_=psumT8[:, c * 128:(c + 1) * 128])
                    return r.then_inc(S["a1e"], 1)
                p.step("ve", fn_e)
                p.inc("a1e")

            p.barrier()

            for ti in range(gNT):
                _gw = p.val("b_g")

                def fn_lmm(eng, ti=ti, war=_gw):
                    if war >= 2:
                        eng.wait_ge(S["b_g"], war - 2)
                    r = None
                    for c in range(NCH):
                        r = eng.matmul(psL[:, 0:8],
                                       xn_fm[c][:, ti * 128:(ti + 1) * 128],
                                       gwT[:, c, :], start=(c == 0),
                                       stop=(c == NCH - 1))
                    return r.then_inc(S["b_l"], 1)
                p.step("te", fn_lmm)
                bl_tgt = p.inc("b_l")

                def fn_gate(eng, bl=bl_tgt):
                    eng.wait_ge(S["b_l"], bl)
                    eng.tensor_copy(out=gsc[:], in_=psL[:, 0:8])
                    eng.drain()
                    eng.reduce_max(out=gm[:], in_=gsc[:], axis=AX, negate=False)
                    eng.drain()
                    eng.tensor_scalar(out=gex[:], in0=gsc[:], scalar1=gm[:],
                                      scalar2=None, op0=ALU.subtract, op1=ALU.bypass)
                    eng.drain()
                    return eng.tensor_copy(out=dumV[:], in_=dumV[:]
                                           ).then_inc(S["b_g"], 1)
                p.step("ve", fn_gate)
                gv = p.inc("b_g")

                def fn_sexp(eng, gv=gv):
                    eng.wait_ge(S["b_g"], gv)
                    eng.activation(out=gex[:], in_=gex[:], func=AF.Exp)
                    eng.drain()
                    return eng.copy(out=dumS[:], in_=dumS[:]).then_inc(S["b_g"], 1)
                p.step("se", fn_sexp)
                gv2 = p.inc("b_g")

                def fn_gate3(eng, ti=ti, gv2=gv2):
                    eng.wait_ge(S["b_g"], gv2)
                    eng.tensor_scalar(out=gscr[:], in0=gex[:], scalar1=1.0,
                                      scalar2=0.0, op0=ALU.mult, op1=ALU.add,
                                      accum_out=gs[:])
                    eng.drain()
                    eng.reciprocal(out=gs[:], in_=gs[:])
                    eng.drain()
                    eng.tensor_scalar(out=gsc[:], in0=gex[:], scalar1=gs[:],
                                      scalar2=None, op0=ALU.mult, op1=ALU.bypass)
                    eng.drain()
                    eng.reduce_max(out=gm[:], in_=gsc[:], axis=AX, negate=False)
                    eng.drain()
                    eng.tensor_scalar(out=gscr[:], in0=gsc[:], scalar1=gm[:],
                                      scalar2=None, op0=ALU.is_ge, op1=ALU.bypass)
                    eng.drain()
                    eng.scalar_tensor_tensor(out=gscr[:], in0=gscr[:], scalar=-2.0,
                                             in1=gsc[:], op0=ALU.mult, op1=ALU.add)
                    eng.drain()
                    eng.reduce_max(out=gm[:], in_=gscr[:], axis=AX, negate=False)
                    eng.drain()
                    eng.tensor_scalar(out=gscr[:], in0=gsc[:], scalar1=gm[:],
                                      scalar2=None, op0=ALU.is_ge, op1=ALU.bypass)
                    eng.drain()
                    return eng.tensor_tensor(out=g_sb[:, ti * 8:(ti + 1) * 8],
                                             in0=gsc[:], in1=gscr[:],
                                             op=ALU.mult).then_inc(S["b_g"], 1)
                p.step("ve", fn_gate3)
                p.inc("b_g")

            p.barrier()

        seq = [(g, ei) for g in range(NGRP) for ei in range(9)]
        for si, (grp, ei) in enumerate(seq):
            routed = ei < 8
            if ei == 0:
                emit_b1b2(grp)
                for ti in range(gNT):
                    st_memset("ve", yacc[ti][:], 0.0)
                p.barrier()

            for t8 in range(gn512):
                sl512 = slice(t8 * 512, (t8 + 1) * 512)

                if routed:
                    _xw = p.val("b_x")

                    def fn_row(eng, ei=ei, t8=t8, war=_xw, wc=wcnt[ei]):
                        eng.wait_ge(S["dma_ge"], wc)
                        if war >= 1:
                            eng.wait_ge(S["b_x"], war)
                        r = None
                        for q in range(4):
                            ti = t8 * 4 + q
                            r = eng.matmul(psR[0:1, q * 128:(q + 1) * 128],
                                           g_sb[:, ti * 8 + ei:ti * 8 + ei + 1],
                                           identf[:], start=True, stop=True)
                        return r.then_inc(S["b_row"], 1)
                    p.step("te", fn_row)
                    row_tgt = p.inc("b_row")

                    _bcw = p.val("b_bc")

                    def fn_rowev(eng, tgt=row_tgt, war=_bcw):
                        eng.wait_ge(S["b_row"], tgt)
                        if war >= 1:
                            eng.wait_ge(S["b_bc"], war)
                        eng.tensor_copy(out=rowsb[:], in_=psR[0:1, 0:512])
                        eng.drain()
                        return eng.tensor_copy(out=dumV[:], in_=dumV[:]
                                               ).then_inc(S["b_row"], 1)
                    p.step("ve", fn_rowev)
                    rowev_tgt = p.inc("b_row")

                    _xw2 = p.val("b_x")

                    def fn_bc(eng, tgt=rowev_tgt, war=_xw2):
                        eng.wait_ge(S["b_row"], tgt)
                        if war >= 1:
                            eng.wait_ge(S["b_x"], war)
                        return eng.matmul(psR[:, 0:512], ones1f[:], rowsb[:],
                                          start=True, stop=True).then_inc(S["b_bc"], 1)
                    p.step("te", fn_bc)
                    bc_tgt = p.inc("b_bc")

                    _aw = p.val("b_a"); _bw = p.val("b_b")

                    def fn_xe(eng, tgt=bc_tgt, aw=_aw, bw=_bw, sl512=sl512):
                        eng.wait_ge(S["b_bc"], tgt)
                        if aw >= 1:
                            eng.wait_ge(S["b_a"], aw)
                        if bw >= 1:
                            eng.wait_ge(S["b_b"], bw)
                        eng.tensor_copy(out=Ge[:], in_=psR[:, 0:512])
                        eng.drain()
                        for c in range(NCH):
                            eng.tensor_tensor(out=xe_fm[c][:],
                                              in0=xn_fm[c][:, sl512],
                                              in1=Ge[:], op=ALU.mult)
                        eng.drain()
                        return eng.tensor_copy(out=dumV[:], in_=dumV[:]
                                               ).then_inc(S["b_x"], 1)
                    p.step("ve", fn_xe)
                    p.inc("b_x")

                xw = p.val("b_x")
                for hc in range(NCH):
                    _asw = p.val("b_as")

                    def fn_amm(eng, hc=hc, xw=xw, war=_asw, sl512=sl512,
                               routed=routed, wc=wcnt[ei]):
                        eng.wait_ge(S["dma_ge"], wc)
                        if routed:
                            eng.wait_ge(S["b_x"], xw)
                        if war >= 1:
                            eng.wait_ge(S["b_as"], war)
                        r = None
                        for c in range(NCH):
                            rhs = xe_fm[c][:] if routed else xn_fm[c][:, sl512]
                            r = eng.matmul(psA[:, 0:512],
                                           w1s[:, c, hc * 128:(hc + 1) * 128],
                                           rhs, start=(c == 0), stop=(c == NCH - 1))
                        return r.then_inc(S["b_a"], 1)
                    p.step("te", fn_amm)
                    a_tgt = p.inc("b_a")

                    _hw = p.val("b_h") - 1

                    def fn_silu(eng, hc=hc, a=a_tgt, war=_hw):
                        eng.wait_ge(S["b_a"], a)
                        if war >= 1:
                            eng.wait_ge(S["b_h"], war)
                        eng.activation(out=asb[hc % 2][:], in_=psA[:, 0:512],
                                       func=AF.Silu)
                        eng.drain()
                        return eng.copy(out=dumS[:], in_=dumS[:]
                                        ).then_inc(S["b_as"], 1)
                    p.step("se", fn_silu)
                    as_tgt = p.inc("b_as")

                    _hw2 = p.val("b_h")

                    def fn_bmm(eng, hc=hc, xw=xw, sl512=sl512, routed=routed,
                               war=_hw2):
                        if routed:
                            eng.wait_ge(S["b_x"], xw)
                        if war >= 1:
                            eng.wait_ge(S["b_h"], war)
                        r = None
                        for c in range(NCH):
                            rhs = xe_fm[c][:] if routed else xn_fm[c][:, sl512]
                            r = eng.matmul(psB[:, 0:512],
                                           w3s[:, c, hc * 128:(hc + 1) * 128],
                                           rhs, start=(c == 0), stop=(c == NCH - 1))
                        return r.then_inc(S["b_b"], 1)
                    p.step("te", fn_bmm)
                    b_tgt = p.inc("b_b")

                    _ow = p.val("b_o")

                    def fn_h(eng, hc=hc, a=as_tgt, b=b_tgt, war=_ow):
                        eng.wait_ge(S["b_as"], a)
                        eng.wait_ge(S["b_b"], b)
                        if war >= 1:
                            eng.wait_ge(S["b_o"], war)
                        eng.tensor_tensor(out=h_fm[hc][:], in0=asb[hc % 2][:],
                                          in1=psB[:, 0:512], op=ALU.mult)
                        eng.drain()
                        return eng.tensor_copy(out=dumV[:], in_=dumV[:]
                                               ).then_inc(S["b_h"], 1)
                    p.step("ve", fn_h)
                    p.inc("b_h")

                hdone = p.val("b_h")
                for tt in range(4):
                    ti = t8 * 4 + tt
                    for half in range(2):
                        _yw = p.val("b_y")

                        def fn_omm(eng, tt=tt, half=half, hd=hdone, war=_yw,
                                   wc=wcnt[ei]):
                            eng.wait_ge(S["b_h"], hd)
                            if war >= 1:
                                eng.wait_ge(S["b_y"], war)
                            r = None
                            for hc in range(NCH):
                                r = eng.matmul(
                                    psC[:, 0:512],
                                    h_fm[hc][:, tt * 128:(tt + 1) * 128],
                                    w2s[:, hc, half * 512:(half + 1) * 512],
                                    start=(hc == 0), stop=(hc == NCH - 1))
                            return r.then_inc(S["b_o"], 1)
                        p.step("te", fn_omm)
                        o_tgt = p.inc("b_o")

                        def fn_yadd(eng, ti=ti, half=half, o=o_tgt):
                            eng.wait_ge(S["b_o"], o)
                            return eng.tensor_tensor(
                                out=yacc[ti][:, half * 512:(half + 1) * 512],
                                in0=yacc[ti][:, half * 512:(half + 1) * 512],
                                in1=psC[:, 0:512], op=ALU.add).then_inc(S["b_y"], 1)
                        p.step("ve", fn_yadd)
                        p.inc("b_y")

            if si + 1 < len(seq):
                _b = p.val("b_b"); _o = p.val("b_o")

                def fn_wp(eng, b=_b, o=_o):
                    eng.wait_ge(S["b_b"], b)
                    eng.wait_ge(S["b_o"], o)
                    return None
                p.step("ge", fn_wp)
                load_expert(seq[si + 1][1])

            if ei == 8:
                p.barrier()
                for ti in range(gNT):
                    t = grp * gNT + ti
                    _a3p = p.val("a3v")
                    if _a3p >= 2:
                        def fn_w(eng, need=_a3p - 1):
                            eng.wait_ge(S["a3v"], need)
                            return None
                        p.step("sy", fn_w)
                    lc = p.load("sy", Xbuf[ti % 2][:], x_t[t])
                    _stp = (p.dma_cnt["st"] - 1) * 16

                    def fn_v(eng, t=t, ti=ti, lc=lc, stp=_stp):
                        eng.wait_ge(S["dma_sy"], lc)
                        if stp >= 16:
                            eng.wait_ge(S["dma_st"], stp)
                        return eng.tensor_tensor(out=Ybuf[ti % 2][:],
                                                 in0=Xbuf[ti % 2][:],
                                                 in1=yacc[ti][:],
                                                 op=ALU.add).then_inc(S["a3v"], 1)
                    p.step("ve", fn_v)
                    a3_tgt = p.inc("a3v")

                    def fn_stw(eng, tgt=a3_tgt):
                        eng.wait_ge(S["a3v"], tgt)
                        return None
                    p.step("sy", fn_stw)
                    p.store(y_t[t], Ybuf[ti % 2][:])

                p.barrier()

        if not fc:
            p.finalize(block, S, {"ve": dumV[:], "se": dumS[:], "ge": dumG[:]})

    return nc


def _run_l0_ffn(x, gate_w, w1, w2, w3, w1s, w2s, w3s):
    identf, identbf, _ = _consts()
    w1a = np.concatenate([w1, w1s], axis=0)
    w2a = np.concatenate([w2, w2s], axis=0)
    w3a = np.concatenate([w3, w3s], axis=0)
    if "l0f" not in _CACHE:
        _CACHE["l0f"] = build_l0_ffn()
    nc = _CACHE["l0f"]
    base = {
        "gwT": np.ascontiguousarray(gate_w.T),
        "w1": np.ascontiguousarray(w1a), "w2": np.ascontiguousarray(w2a),
        "w3": np.ascontiguousarray(w3a),
        "identf": identf, "identbf": identbf,
    }
    in_maps = []
    for c in range(8):
        m = dict(base)
        m["x"] = np.ascontiguousarray(x[c])
        in_maps.append(m)
    res = run_bass_kernel_spmd(nc, in_maps, core_ids=list(range(8)))
    return np.stack([np.asarray(r["y"]) for r in res.results], axis=0)



def build_l1_ffn(T=T, NT=NT, fc=None):
    NGRP = 2 if T >= 4096 else 1
    GT = T // NGRP
    gNT = GT // 128
    gn512 = GT // 512
    NH = 16

    nc = fc["nc"] if fc else bass.Bass("TRN2", target_bir_lowering=False, detect_race_conditions=False)
    px = fc["px"] if fc else ""

    x_in = fc["x_in"] if fc else nc.dram_tensor("x", [T, D], F32, kind="ExternalInput").ap()
    rw1_d = nc.dram_tensor("rw1", [D, D], F32, kind="ExternalInput").ap()
    rb1_d = nc.dram_tensor("rb1fm", [128, NCH], F32, kind="ExternalInput").ap()
    rw2_d = nc.dram_tensor("rw2", [D, 8], F32, kind="ExternalInput").ap()
    rb2_d = nc.dram_tensor("rb2row", [1, 8], F32, kind="ExternalInput").ap()
    ew1_d = nc.dram_tensor("ew1", [8, D, 2048], F32, kind="ExternalInput").ap()
    eb1_d = nc.dram_tensor("eb1fm", [8, 128, NH], F32, kind="ExternalInput").ap()
    ew2_d = nc.dram_tensor("ew2", [8, 2048, D], F32, kind="ExternalInput").ap()
    eb2_d = nc.dram_tensor("eb2row", [8, 1, D], F32, kind="ExternalInput").ap()
    identbf_d = _dram_dedup(nc, fc, "identbf", [128, 128], BF16)
    x_out = fc["x_out"] if fc else nc.dram_tensor("y", [T, D], F32, kind="ExternalOutput").ap()

    x_t = x_in.rearrange("(n p) d -> n p d", p=128)
    y_t = x_out.rearrange("(n p) d -> n p d", p=128)

    p = fc["p"] if fc else Prog(nc)

    with ExitStack() as ctx:
        e = ctx.enter_context

        Xbuf = [e(nc.sbuf_tensor(px + f"Xbuf{i}", [128, D], F32)) for i in range(2)]
        Ybuf = [e(nc.sbuf_tensor(px + f"Ybuf{i}", [128, D], F32)) for i in range(2)]
        ssum = e(nc.sbuf_tensor(px + "ssum", [128, NT], F32))
        ssq = e(nc.sbuf_tensor(px + "ssq", [128, NT], F32))
        mcol = e(nc.sbuf_tensor(px + "mcol", [128, NT], F32))
        Ccol = e(nc.sbuf_tensor(px + "Ccol", [128, NT], F32))
        tmp1 = e(nc.sbuf_tensor(px + "tmp1", [128, NT], F32))
        tmp2 = e(nc.sbuf_tensor(px + "tmp2", [128, NT], F32))
        xnbuf = [e(nc.sbuf_tensor(px + f"xnbuf{i}", [128, D], BF16)) for i in range(2)]
        xn_fm = [e(nc.sbuf_tensor(px + f"xnfm{c}", [128, GT], BF16)) for c in range(NCH)]
        yacc = [e(nc.sbuf_tensor(px + f"yacc{t}", [128, D], F32)) for t in range(gNT)]
        g_sb = e(nc.sbuf_tensor(px + "g_sb", [128, 8 * gNT], F32))
        gsc = e(nc.sbuf_tensor(px + "gsc", [128, 8], F32))
        gex = e(nc.sbuf_tensor(px + "gex", [128, 8], F32))
        gm = e(nc.sbuf_tensor(px + "gm", [128, 1], F32))
        gs = e(nc.sbuf_tensor(px + "gs", [128, 1], F32))
        gscr = e(nc.sbuf_tensor(px + "gscr", [128, 8], F32))
        eh_fm = [e(nc.sbuf_tensor(px + f"ehfm{c}", [128, 512], BF16)) for c in range(NH)]
        ew1s = e(nc.sbuf_tensor(px + "ew1s", [128, NCH, 2048], BF16))
        ew2s = e(nc.sbuf_tensor(px + "ew2s", [128, NH, D], BF16))
        eb1s = e(nc.sbuf_tensor(px + "eb1s", [128, NH], F32))
        eb2s = e(nc.sbuf_tensor(px + "eb2s", [1, D], BF16))
        rw2s = e(nc.sbuf_tensor(px + "rw2s", [128, NCH, 8], BF16))
        rb1s = e(nc.sbuf_tensor(px + "rb1s", [128, NCH], F32))
        rb2s = e(nc.sbuf_tensor(px + "rb2s", [1, 8], BF16))
        identbf = e(nc.sbuf_tensor(px + "identbf_s", [128, 128], BF16))
        onescol = e(nc.sbuf_tensor(px + "onescol", [1, 128], BF16))
        dumV = e(nc.sbuf_tensor(px + "dumV", [1, 1], F32))
        dumS = e(nc.sbuf_tensor(px + "dumS", [1, 1], F32))
        dumG = e(nc.sbuf_tensor(px + "dumG", [1, 1], F32))

        psumT8 = e(nc.psum_tensor(px + "psumT8", [128, 1024], BF16))
        psL = e(nc.psum_tensor(px + "psL", [128, 512], F32))
        psA = e(nc.psum_tensor(px + "psA", [128, 512], F32))
        psC = e(nc.psum_tensor(px + "psC", [128, 512], F32))

        sem_names = ["dma_ge", "dma_sy", "dma_st", "bar",
                     "a0", "a1x", "a1t", "a1e",
                     "b_l", "b_g", "b_m", "b_ms",
                     "b_a", "b_as", "b_o", "b_y",
                     "a3v"]
        S = fc["S"] if fc else {n: e(nc.semaphore(n)) for n in sem_names}
        block = None if fc else e(nc.Block())

        p.load("sy", identbf[:], identbf_d)
        p.load("sy", rb1s[:], rb1_d)
        p.load("ge", rb2s[:], rb2_d)
        p.load("ge", rw2s[:], rw2_d.rearrange("(c p) e -> p c e", p=128))
        rw1cnt = p.load("ge", ew1s[:, :, 0:D],
                        rw1_d.rearrange("(c p) n -> p c n", p=128))

        wcnt = {}

        def load_expert(eidx):
            p.load("ge", ew1s[:], ew1_d[eidx].rearrange("(c p) n -> p c n", p=128))
            p.load("sy", eb1s[:], eb1_d[eidx])
            p.load("ge", eb2s[:], eb2_d[eidx])
            wcnt[eidx] = p.load("ge", ew2s[:],
                                ew2_d[eidx].rearrange("(c p) n -> p c n", p=128))

        def st_memset(key, ap, val):
            def fn(eng, ap=ap, val=val):
                return eng.memset(ap, val)
            p.step(key, fn)

        st_memset("ve", onescol[:], 1.0)
        st_memset("ve", dumV[:], 0.0)
        st_memset("ge", dumG[:], 0.0)

        def fn_dumS(eng):
            return eng.memzero(dumS[:])
        p.step("se", fn_dumS)

        p.barrier()

        _a0b = p.val("a0")
        for t in range(NT):
            if t >= 2:
                def fn_w(eng, need=_a0b + 2 * (t - 1)):
                    eng.wait_ge(S["a0"], need)
                    return None
                p.step("sy", fn_w)
            lc = p.load("sy", Xbuf[t % 2][:], x_t[t])

            def fn_v(eng, t=t, lc=lc):
                eng.wait_ge(S["dma_sy"], lc)
                return eng.tensor_scalar(
                    out=Ybuf[0][:], in0=Xbuf[t % 2][:], scalar1=1.0, scalar2=0.0,
                    op0=ALU.mult, op1=ALU.add,
                    accum_out=ssum[:, t:t + 1]).then_inc(S["a0"], 1)
            p.step("ve", fn_v)
            p.inc("a0")

            def fn_s(eng, t=t, lc=lc):
                eng.wait_ge(S["dma_sy"], lc)
                return eng.activation(
                    out=Ybuf[1][:], in_=Xbuf[t % 2][:], func=AF.Square,
                    accum_out=ssq[:, t:t + 1]).then_inc(S["a0"], 1)
            p.step("se", fn_s)
            p.inc("a0")

        p.barrier()

        def fn_mc1(eng):
            eng.drain()
            eng.tensor_scalar(out=mcol[:], in0=ssum[:], scalar1=1.0 / D,
                              scalar2=None, op0=ALU.mult, op1=ALU.bypass)
            eng.tensor_scalar(out=tmp1[:], in0=ssq[:], scalar1=1.0 / D,
                              scalar2=None, op0=ALU.mult, op1=ALU.bypass)
            eng.drain()
            eng.tensor_tensor(out=tmp2[:], in0=mcol[:], in1=mcol[:], op=ALU.mult)
            eng.drain()
            eng.tensor_tensor(out=tmp1[:], in0=tmp1[:], in1=tmp2[:], op=ALU.subtract)
            eng.drain()
            return eng.tensor_scalar(out=tmp2[:], in0=tmp1[:], scalar1=1e-5,
                                     scalar2=None, op0=ALU.add, op1=ALU.bypass)
        p.step("ve", fn_mc1)
        p.barrier()

        def fn_mc2(eng):
            return eng.activation(out=tmp1[:], in_=tmp2[:], func=AF.Sqrt)
        p.step("se", fn_mc2)
        p.barrier()

        def fn_mc3(eng):
            return eng.reciprocal(out=Ccol[:], in_=tmp1[:])
        p.step("ve", fn_mc3)
        p.barrier()

        for grp in range(NGRP):
            for ti in range(gNT):
                t = grp * gNT + ti
                nx = p.val("a1x")
                if nx >= 2:
                    def fn_w(eng, need=nx - 1):
                        eng.wait_ge(S["a1x"], need)
                        return None
                    p.step("sy", fn_w)
                lc = p.load("sy", Xbuf[ti % 2][:], x_t[t])

                _tw = p.val("a1t")

                def fn_v(eng, t=t, ti=ti, lc=lc, tw=_tw):
                    eng.wait_ge(S["dma_sy"], lc)
                    if tw >= 2:
                        eng.wait_ge(S["a1t"], tw - 1)
                    return eng.tensor_scalar(
                        out=xnbuf[ti % 2][:], in0=Xbuf[ti % 2][:],
                        scalar1=mcol[:, t:t + 1], scalar2=Ccol[:, t:t + 1],
                        op0=ALU.subtract, op1=ALU.mult).then_inc(S["a1x"], 1)
                p.step("ve", fn_v)
                x_tgt = p.inc("a1x")

                _ew = p.val("a1e")

                def fn_t(eng, ti=ti, xt=x_tgt, ew=_ew):
                    eng.wait_ge(S["a1x"], xt)
                    if ew >= 1:
                        eng.wait_ge(S["a1e"], ew)
                    r = None
                    for c in range(NCH):
                        r = eng.transpose(psumT8[:, c * 128:(c + 1) * 128],
                                          xnbuf[ti % 2][:, c * 128:(c + 1) * 128],
                                          identbf[:])
                    return r.then_inc(S["a1t"], 1)
                p.step("te", fn_t)
                t_tgt = p.inc("a1t")

                def fn_e(eng, ti=ti, tt=t_tgt):
                    eng.wait_ge(S["a1t"], tt)
                    r = None
                    for c in range(NCH):
                        r = eng.tensor_copy(
                            out=xn_fm[c][:, ti * 128:(ti + 1) * 128],
                            in_=psumT8[:, c * 128:(c + 1) * 128])
                    return r.then_inc(S["a1e"], 1)
                p.step("ve", fn_e)
                p.inc("a1e")

            p.barrier()

            if grp > 0:
                rw1cnt = p.load("ge", ew1s[:, :, 0:D],
                                rw1_d.rearrange("(c p) n -> p c n", p=128))
            for t8 in range(gn512):
                sl512 = slice(t8 * 512, (t8 + 1) * 512)
                for hc in range(NCH):
                    _asw = p.val("b_ms")

                    def fn_mmm(eng, hc=hc, sl512=sl512, rc=rw1cnt, war=_asw):
                        eng.wait_ge(S["dma_ge"], rc)
                        if war >= 1:
                            eng.wait_ge(S["b_ms"], war)
                        r = None
                        for c in range(NCH):
                            r = eng.matmul(psA[:, 0:512],
                                           ew1s[:, c, hc * 128:(hc + 1) * 128],
                                           xn_fm[c][:, sl512],
                                           start=(c == 0), stop=(c == NCH - 1))
                        return r.then_inc(S["b_m"], 1)
                    p.step("te", fn_mmm)
                    m_tgt = p.inc("b_m")

                    def fn_msilu(eng, hc=hc, m=m_tgt):
                        eng.wait_ge(S["b_m"], m)
                        eng.activation(out=eh_fm[hc][:], in_=psA[:, 0:512],
                                       func=AF.Silu, bias=rb1s[:, hc:hc + 1])
                        eng.drain()
                        return eng.copy(out=dumS[:], in_=dumS[:]
                                        ).then_inc(S["b_ms"], 1)
                    p.step("se", fn_msilu)
                    p.inc("b_ms")

                mid_done = p.val("b_ms")
                for tt in range(4):
                    ti = t8 * 4 + tt
                    _gw = p.val("b_g")

                    def fn_lmm(eng, tt=tt, md=mid_done, war=_gw):
                        eng.wait_ge(S["b_ms"], md)
                        if war >= 2:
                            eng.wait_ge(S["b_g"], war - 2)
                        r = None
                        for c in range(NCH):
                            eng.matmul(psL[:, 0:8],
                                       eh_fm[c][:, tt * 128:(tt + 1) * 128],
                                       rw2s[:, c, :], start=(c == 0), stop=False)
                        r = eng.matmul(psL[:, 0:8], onescol[:], rb2s[:],
                                       start=False, stop=True)
                        return r.then_inc(S["b_l"], 1)
                    p.step("te", fn_lmm)
                    bl_tgt = p.inc("b_l")

                    def fn_gate(eng, bl=bl_tgt):
                        eng.wait_ge(S["b_l"], bl)
                        eng.tensor_copy(out=gsc[:], in_=psL[:, 0:8])
                        eng.drain()
                        eng.reduce_max(out=gm[:], in_=gsc[:], axis=AX, negate=False)
                        eng.drain()
                        eng.tensor_scalar(out=gex[:], in0=gsc[:], scalar1=gm[:],
                                          scalar2=None, op0=ALU.subtract,
                                          op1=ALU.bypass)
                        eng.drain()
                        eng.tensor_scalar(out=gscr[:], in0=gsc[:], scalar1=gm[:],
                                          scalar2=None, op0=ALU.is_ge, op1=ALU.bypass)
                        eng.drain()
                        eng.scalar_tensor_tensor(out=gscr[:], in0=gscr[:],
                                                 scalar=-1e9, in1=gsc[:],
                                                 op0=ALU.mult, op1=ALU.add)
                        eng.drain()
                        eng.reduce_max(out=gm[:], in_=gscr[:], axis=AX, negate=False)
                        eng.drain()
                        eng.tensor_scalar(out=gscr[:], in0=gsc[:], scalar1=gm[:],
                                          scalar2=None, op0=ALU.is_ge, op1=ALU.bypass)
                        eng.drain()
                        return eng.tensor_copy(out=dumV[:], in_=dumV[:]
                                               ).then_inc(S["b_g"], 1)
                    p.step("ve", fn_gate)
                    gv = p.inc("b_g")

                    def fn_sexp(eng, gv=gv):
                        eng.wait_ge(S["b_g"], gv)
                        eng.activation(out=gex[:], in_=gex[:], func=AF.Exp)
                        eng.drain()
                        return eng.copy(out=dumS[:], in_=dumS[:]).then_inc(S["b_g"], 1)
                    p.step("se", fn_sexp)
                    gv2 = p.inc("b_g")

                    def fn_gate3(eng, ti=ti, gv2=gv2):
                        eng.wait_ge(S["b_g"], gv2)
                        eng.tensor_tensor(out=gex[:], in0=gex[:], in1=gscr[:],
                                          op=ALU.mult)
                        eng.drain()
                        eng.tensor_scalar(out=gscr[:], in0=gex[:], scalar1=1.0,
                                          scalar2=0.0, op0=ALU.mult, op1=ALU.add,
                                          accum_out=gs[:])
                        eng.drain()
                        eng.reciprocal(out=gs[:], in_=gs[:])
                        eng.drain()
                        return eng.tensor_scalar(out=g_sb[:, ti * 8:(ti + 1) * 8],
                                                 in0=gex[:], scalar1=gs[:],
                                                 scalar2=None, op0=ALU.mult,
                                                 op1=ALU.bypass).then_inc(S["b_g"], 1)
                    p.step("ve", fn_gate3)
                    p.inc("b_g")

            for ti in range(gNT):
                st_memset("ve", yacc[ti][:], 0.0)
            p.barrier()

            for ei in range(8):
                _a = p.val("b_a"); _o = p.val("b_o"); _l = p.val("b_l")
                _as_ = p.val("b_as")

                def fn_wp(eng, a=_a, o=_o, l=_l, as_=_as_):
                    if a: eng.wait_ge(S["b_a"], a)
                    if o: eng.wait_ge(S["b_o"], o)
                    if l: eng.wait_ge(S["b_l"], l)
                    if as_: eng.wait_ge(S["b_as"], as_)
                    return None
                p.step("ge", fn_wp)
                def fn_wp_sy(eng, as_=_as_):
                    if as_: eng.wait_ge(S["b_as"], as_)
                    return None
                p.step("sy", fn_wp_sy)
                load_expert(ei)
                ebcnt = p.dma_cnt["sy"] * 16

                for t8 in range(gn512):
                    sl512 = slice(t8 * 512, (t8 + 1) * 512)
                    for hc in range(NH):
                        _asw = p.val("b_as")

                        def fn_amm(eng, hc=hc, sl512=sl512, wc=wcnt[ei],
                                   war=_asw):
                            eng.wait_ge(S["dma_ge"], wc)
                            if war >= 1:
                                eng.wait_ge(S["b_as"], war)
                            r = None
                            for c in range(NCH):
                                r = eng.matmul(psA[:, 0:512],
                                               ew1s[:, c, hc * 128:(hc + 1) * 128],
                                               xn_fm[c][:, sl512],
                                               start=(c == 0), stop=(c == NCH - 1))
                            return r.then_inc(S["b_a"], 1)
                        p.step("te", fn_amm)
                        a_tgt = p.inc("b_a")

                        _ow = p.val("b_o")

                        def fn_silu(eng, hc=hc, a=a_tgt, eb=ebcnt, war=_ow):
                            eng.wait_ge(S["b_a"], a)
                            eng.wait_ge(S["dma_sy"], eb)
                            if war >= 1:
                                eng.wait_ge(S["b_o"], war)
                            eng.activation(out=eh_fm[hc][:], in_=psA[:, 0:512],
                                           func=AF.Silu, bias=eb1s[:, hc:hc + 1])
                            eng.drain()
                            return eng.copy(out=dumS[:], in_=dumS[:]
                                            ).then_inc(S["b_as"], 1)
                        p.step("se", fn_silu)
                        p.inc("b_as")

                    hdone = p.val("b_as")
                    for tt in range(4):
                        ti = t8 * 4 + tt
                        for half in range(2):
                            _yw = p.val("b_y")

                            def fn_omm(eng, tt=tt, half=half, hd=hdone,
                                       war=_yw, wc=wcnt[ei]):
                                eng.wait_ge(S["b_as"], hd)
                                if war >= 1:
                                    eng.wait_ge(S["b_y"], war)
                                r = None
                                for hc in range(NH):
                                    eng.matmul(
                                        psC[:, 0:512],
                                        eh_fm[hc][:, tt * 128:(tt + 1) * 128],
                                        ew2s[:, hc, half * 512:(half + 1) * 512],
                                        start=(hc == 0), stop=False)
                                r = eng.matmul(psC[:, 0:512], onescol[:],
                                               eb2s[:, half * 512:(half + 1) * 512],
                                               start=False, stop=True)
                                return r.then_inc(S["b_o"], 1)
                            p.step("te", fn_omm)
                            o_tgt = p.inc("b_o")

                            def fn_yadd(eng, ti=ti, half=half, o=o_tgt, ei=ei):
                                eng.wait_ge(S["b_o"], o)
                                return eng.scalar_tensor_tensor(
                                    out=yacc[ti][:, half * 512:(half + 1) * 512],
                                    in0=psC[:, 0:512],
                                    scalar=g_sb[:, ti * 8 + ei:ti * 8 + ei + 1],
                                    in1=yacc[ti][:, half * 512:(half + 1) * 512],
                                    op0=ALU.mult, op1=ALU.add).then_inc(S["b_y"], 1)
                            p.step("ve", fn_yadd)
                            p.inc("b_y")

            p.barrier()

            for ti in range(gNT):
                t = grp * gNT + ti
                _a3p = p.val("a3v")
                if _a3p >= 2:
                    def fn_w(eng, need=_a3p - 1):
                        eng.wait_ge(S["a3v"], need)
                        return None
                    p.step("sy", fn_w)
                lc = p.load("sy", Xbuf[ti % 2][:], x_t[t])
                _stp = (p.dma_cnt["st"] - 1) * 16

                def fn_v(eng, t=t, ti=ti, lc=lc, stp=_stp):
                    eng.wait_ge(S["dma_sy"], lc)
                    if stp >= 16:
                        eng.wait_ge(S["dma_st"], stp)
                    eng.tensor_scalar(out=Ybuf[ti % 2][:], in0=Xbuf[ti % 2][:],
                                      scalar1=mcol[:, t:t + 1],
                                      scalar2=Ccol[:, t:t + 1],
                                      op0=ALU.subtract, op1=ALU.mult)
                    eng.drain()
                    eng.tensor_tensor(out=Ybuf[ti % 2][:], in0=Ybuf[ti % 2][:],
                                      in1=Xbuf[ti % 2][:], op=ALU.add)
                    eng.drain()
                    return eng.tensor_tensor(out=Ybuf[ti % 2][:],
                                             in0=Ybuf[ti % 2][:],
                                             in1=yacc[ti][:],
                                             op=ALU.add).then_inc(S["a3v"], 1)
                p.step("ve", fn_v)
                a3_tgt = p.inc("a3v")

                def fn_stw(eng, tgt=a3_tgt):
                    eng.wait_ge(S["a3v"], tgt)
                    return None
                p.step("sy", fn_stw)
                p.store(y_t[t], Ybuf[ti % 2][:])

            p.barrier()

        if not fc:
            p.finalize(block, S, {"ve": dumV[:], "se": dumS[:], "ge": dumG[:]})

    return nc


def _run_l1_ffn(x, rw1, rb1, rw2, rb2, ew1, eb1, ew2, eb2):
    identf, identbf, _ = _consts()
    if "l1f" not in _CACHE:
        _CACHE["l1f"] = build_l1_ffn()
    nc = _CACHE["l1f"]
    base = {
        "rw1": np.ascontiguousarray(rw1),
        "rb1fm": np.ascontiguousarray(rb1.reshape(NCH, 128).T),
        "rw2": np.ascontiguousarray(rw2),
        "rb2row": np.ascontiguousarray(rb2.reshape(1, 8)),
        "ew1": np.ascontiguousarray(ew1),
        "eb1fm": np.ascontiguousarray(eb1.reshape(8, 16, 128).transpose(0, 2, 1)),
        "ew2": np.ascontiguousarray(ew2),
        "eb2row": np.ascontiguousarray(eb2.reshape(8, 1, D)),
        "identbf": identbf,
    }
    in_maps = []
    for c in range(8):
        m = dict(base)
        m["x"] = np.ascontiguousarray(x[c])
        in_maps.append(m)
    res = run_bass_kernel_spmd(nc, in_maps, core_ids=list(range(8)))
    return np.stack([np.asarray(r["y"]) for r in res.results], axis=0)



def build_l1_attn(T=T, NT=NT, fc=None):
    GS = 512
    NG = T // GS
    CH = 128
    NCK = GS // CH
    H = 16

    nc = fc["nc"] if fc else bass.Bass("TRN2", target_bir_lowering=False, detect_race_conditions=False)
    px = fc["px"] if fc else ""

    x_in = fc["x_in"] if fc else nc.dram_tensor("x", [T, D], F32, kind="ExternalInput").ap()
    wqT_d = nc.dram_tensor("wqT", [D, D], F32, kind="ExternalInput").ap()
    wkT_d = nc.dram_tensor("wkT", [D, D], F32, kind="ExternalInput").ap()
    wvT_d = nc.dram_tensor("wvT", [D, 2048], F32, kind="ExternalInput").ap()
    waT_d = nc.dram_tensor("waT", [D, H], F32, kind="ExternalInput").ap()
    wbT_d = nc.dram_tensor("wbT", [D, H], F32, kind="ExternalInput").ap()
    woT_d = nc.dram_tensor("woT", [2048, D], F32, kind="ExternalInput").ap()
    cq_d = nc.dram_tensor("cqfm", [128, NCH, 4], F32, kind="ExternalInput").ap()
    ck_d = nc.dram_tensor("ckfm", [128, NCH, 4], F32, kind="ExternalInput").ap()
    cv_d = nc.dram_tensor("cvfm", [128, 16, 4], F32, kind="ExternalInput").ap()
    alog_d = nc.dram_tensor("alogrow", [1, H], F32, kind="ExternalInput").ap()
    dtb_d = nc.dram_tensor("dtbcol", [H, 1], F32, kind="ExternalInput").ap()
    identf_d = _dram_dedup(nc, fc, "identf", [128, 128], F32)
    identbf_d = _dram_dedup(nc, fc, "identbf", [128, 128], BF16)
    epat_d = nc.dram_tensor("epat", [H, H * 128], F32, kind="ExternalInput").ap()
    lmask_d = nc.dram_tensor("lmask", [128, 128], F32, kind="ExternalInput").ap()
    smask_d = nc.dram_tensor("smask", [128, 128], F32, kind="ExternalInput").ap()
    uti_d = nc.dram_tensor("uti", [128, 128], F32, kind="ExternalInput").ap()
    uts_d = nc.dram_tensor("uts", [128, 128], F32, kind="ExternalInput").ap()
    bd2_d = nc.dram_tensor("bd2", [128, 2], BF16, kind="ExternalInput").ap()
    hd2_d = nc.dram_tensor("hd2", [2, 128], F32, kind="ExternalInput").ap()
    elast_d = nc.dram_tensor("elast", [128, 1], F32, kind="ExternalInput").ap()
    x_out = fc["x_out"] if fc else nc.dram_tensor("y", [T, D], F32, kind="ExternalOutput").ap()

    x_t = x_in.rearrange("(n p) d -> n p d", p=128)
    y_t = x_out.rearrange("(n p) d -> n p d", p=128)

    p = fc["p"] if fc else Prog(nc)

    with ExitStack() as ctx:
        e = ctx.enter_context

        Xbuf = [e(nc.sbuf_tensor(px + f"Xbuf{i}", [128, D], F32)) for i in range(2)]
        Ybuf = [e(nc.sbuf_tensor(px + f"Ybuf{i}", [128, D], F32)) for i in range(2)]
        ssum = e(nc.sbuf_tensor(px + "ssum", [128, NT], F32))
        ssq = e(nc.sbuf_tensor(px + "ssq", [128, NT], F32))
        mcol = e(nc.sbuf_tensor(px + "mcol", [128, NT], F32))
        Ccol = e(nc.sbuf_tensor(px + "Ccol", [128, NT], F32))
        tmp1 = e(nc.sbuf_tensor(px + "tmp1", [128, NT], F32))
        tmp2 = e(nc.sbuf_tensor(px + "tmp2", [128, NT], F32))
        xnbuf = [e(nc.sbuf_tensor(px + f"xnbuf{i}", [128, D], BF16)) for i in range(2)]
        xn_fm = [e(nc.sbuf_tensor(px + f"xnfm{c}", [128, GS], BF16)) for c in range(NCH)]
        wbig = e(nc.sbuf_tensor(px + "wbig", [128, NCH, 2048], BF16))
        woT = e(nc.sbuf_tensor(px + "woT_s", [128, 16, D], BF16))
        wabT = e(nc.sbuf_tensor(px + "wabT_s", [128, NCH, 2 * H], BF16))
        cqfm = e(nc.sbuf_tensor(px + "cqfm_s", [128, NCH, 4], F32))
        ckfm = e(nc.sbuf_tensor(px + "ckfm_s", [128, NCH, 4], F32))
        cvfm = e(nc.sbuf_tensor(px + "cvfm_s", [128, 16, 4], F32))
        qtail = e(nc.sbuf_tensor(px + "qtail", [128, NCH, 3], F32))
        ktail = e(nc.sbuf_tensor(px + "ktail", [128, NCH, 3], F32))
        vtail = e(nc.sbuf_tensor(px + "vtail", [128, 16, 3], F32))
        qpre = [e(nc.sbuf_tensor(px + f"qpre{c}", [128, GS], F32)) for c in range(2)]
        qtmp = e(nc.sbuf_tensor(px + "qtmp", [128, GS], F32))
        qtmp2 = e(nc.sbuf_tensor(px + "qtmp2", [128, GS], BF16))
        q_fm = [e(nc.sbuf_tensor(px + f"qfm{c}", [128, GS], BF16)) for c in range(NCH)]
        k_fm = [e(nc.sbuf_tensor(px + f"kfm{c}", [128, GS], BF16)) for c in range(NCH)]
        v_fm = [e(nc.sbuf_tensor(px + f"vfm{c}", [128, GS], BF16)) for c in range(16)]
        nrm2 = e(nc.sbuf_tensor(px + "nrm2", [2, GS], F32))
        alin = e(nc.sbuf_tensor(px + "alin", [16, GS], F32))
        blin = e(nc.sbuf_tensor(px + "blin", [16, GS], F32))
        gsc1 = e(nc.sbuf_tensor(px + "gsc1", [16, GS], F32))
        gsc2 = e(nc.sbuf_tensor(px + "gsc2", [16, GS], F32))
        loga = e(nc.sbuf_tensor(px + "loga", [16, GS], F32))
        lnc = e(nc.sbuf_tensor(px + "lnc", [16, GS], F32))
        theta = e(nc.sbuf_tensor(px + "theta", [16, 1], F32))
        dtbc = e(nc.sbuf_tensor(px + "dtbc", [16, 1], F32))
        LT = e(nc.sbuf_tensor(px + "LT", [128, H], F32))
        LpT = e(nc.sbuf_tensor(px + "LpT", [128, H], F32))
        vT = e(nc.sbuf_tensor(px + "vT", [128, H], F32))
        rT = e(nc.sbuf_tensor(px + "rT", [128, H], F32))
        gT = e(nc.sbuf_tensor(px + "gT", [128, H], F32))
        wT = e(nc.sbuf_tensor(px + "wT", [128, H], F32))
        LCbc = e(nc.sbuf_tensor(px + "LCbc", [128, H], F32))
        negLT = e(nc.sbuf_tensor(px + "negLT", [128, H], F32))
        logaT = e(nc.sbuf_tensor(px + "logaT", [128, H], F32))
        lncT = e(nc.sbuf_tensor(px + "lncT", [128, H], F32))
        LCrow = e(nc.sbuf_tensor(px + "LCrow", [1, H], F32))
        gCc4 = e(nc.sbuf_tensor(px + "gCc4", [128, 4], F32))
        vh = e(nc.sbuf_tensor(px + "vh", [16, CH], F32))
        negLh = e(nc.sbuf_tensor(px + "negLh", [16, CH], F32))
        Lh = e(nc.sbuf_tensor(px + "Lh", [16, CH], F32))
        Eex4 = e(nc.sbuf_tensor(px + "Eex4", [128, 512], BF16))
        Msb4 = e(nc.sbuf_tensor(px + "Msb4", [128, 512], BF16))
        Mta4 = e(nc.sbuf_tensor(px + "Mta4", [128, 512], BF16))
        Pa4 = e(nc.sbuf_tensor(px + "Pa4", [128, 512], BF16))
        Btsb4 = e(nc.sbuf_tensor(px + "Btsb4", [128, 512], BF16))
        Rsb4 = e(nc.sbuf_tensor(px + "Rsb4", [128, 512], BF16))
        Usb4 = e(nc.sbuf_tensor(px + "Usb4", [128, 512], BF16))
        Uw4 = e(nc.sbuf_tensor(px + "Uw4", [128, 512], BF16))
        Osb4 = e(nc.sbuf_tensor(px + "Osb4", [128, 512], F32))
        Vtok4 = e(nc.sbuf_tensor(px + "Vtok4", [128, 512], BF16))
        Ktok4 = e(nc.sbuf_tensor(px + "Ktok4", [128, 512], BF16))
        oms4 = e(nc.sbuf_tensor(px + "oms4", [128, 4], F32))
        ident4 = e(nc.sbuf_tensor(px + "ident4", [128, 512], BF16))
        Ssb = e(nc.sbuf_tensor(px + "Ssb", [128, H * 128], F32))
        S0bf = e(nc.sbuf_tensor(px + "S0bf", [128, H * 128], BF16))
        o_fm = [e(nc.sbuf_tensor(px + f"ofm{h}", [128, GS], BF16)) for h in range(16)]
        identf = e(nc.sbuf_tensor(px + "identf_s", [128, 128], F32))
        identbf = e(nc.sbuf_tensor(px + "identbf_s", [128, 128], BF16))
        epat = e(nc.sbuf_tensor(px + "epat_s", [H, H * 128], F32))
        lmask = e(nc.sbuf_tensor(px + "lmask_s", [128, 128], F32))
        smask = e(nc.sbuf_tensor(px + "smask_s", [128, 128], F32))
        uti = e(nc.sbuf_tensor(px + "uti_s", [128, 128], F32))
        uts = e(nc.sbuf_tensor(px + "uts_s", [128, 128], F32))
        bd2 = e(nc.sbuf_tensor(px + "bd2_s", [128, 2], BF16))
        hd2 = e(nc.sbuf_tensor(px + "hd2_s", [2, 128], F32))
        elast = e(nc.sbuf_tensor(px + "elast_s", [128, 1], F32))
        alogr = e(nc.sbuf_tensor(px + "alogr", [1, H], F32))
        onesf = e(nc.sbuf_tensor(px + "onesf", [1, 128], F32))
        dumV = e(nc.sbuf_tensor(px + "dumV", [1, 1], F32))
        dumS = e(nc.sbuf_tensor(px + "dumS", [1, 1], F32))
        dumG = e(nc.sbuf_tensor(px + "dumG", [1, 1], F32))

        psTf = e(nc.psum_tensor(px + "psTf", [128, 512], F32))
        psTb = e(nc.psum_tensor(px + "psTb", [128, 1024], BF16))
        psD1 = e(nc.psum_tensor(px + "psD1", [128, 512], F32))
        psD2 = e(nc.psum_tensor(px + "psD2", [128, 512], F32))
        psN1 = e(nc.psum_tensor(px + "psN1", [128, 512], F32))
        psN2 = e(nc.psum_tensor(px + "psN2", [128, 512], F32))
        psP = e(nc.psum_tensor(px + "psP", [128, 512], F32))
        psG = e(nc.psum_tensor(px + "psG", [128, 512], F32))

        sem_names = ["dma_ge", "dma_sy", "dma_st", "bar",
                     "a0", "a1x", "a1t", "a1e",
                     "c_pm", "c_pe", "c_cv", "c_sl", "c_nm", "c_nv",
                     "c_gm", "c_gv", "c_gs",
                     "c_tr", "c_tv", "c_cs",
                     "c_kk", "c_dd", "c_de", "c_dm",
                     "c_nmm", "c_nev", "c_pp",
                     "c_am", "c_av", "c_as",
                     "c_ot", "c_oe", "c_sb",
                     "c_op", "a3v"]
        S = fc["S"] if fc else {n: e(nc.semaphore(n)) for n in sem_names}
        block = None if fc else e(nc.Block())

        def st_memset(key, ap, val):
            def fn(eng, ap=ap, val=val):
                return eng.memset(ap, val)
            p.step(key, fn)

        p.load("sy", identf[:], identf_d)
        p.load("sy", identbf[:], identbf_d)
        for _i4 in range(4):
            p.load("sy", ident4[:, _i4 * 128:(_i4 + 1) * 128], identbf_d)
        p.load("sy", epat[:], epat_d)
        p.load("sy", lmask[:], lmask_d)
        p.load("sy", smask[:], smask_d)
        p.load("sy", uti[:], uti_d)
        p.load("sy", uts[:], uts_d)
        p.load("sy", bd2[:], bd2_d)
        p.load("sy", hd2[:], hd2_d)
        p.load("sy", elast[:], elast_d)
        p.load("sy", alogr[:], alog_d)
        p.load("sy", dtbc[:], dtb_d)
        p.load("sy", cqfm[:], cq_d)
        p.load("sy", ckfm[:], ck_d)
        p.load("sy", cvfm[:], cv_d)
        p.load("ge", wabT[:, :, 0:H], waT_d.rearrange("(c p) h -> p c h", p=128))
        p.load("ge", wabT[:, :, H:2 * H], wbT_d.rearrange("(c p) h -> p c h", p=128))
        p.load("ge", woT[:], woT_d.rearrange("(c p) n -> p c n", p=128))
        st_memset("ve", dumV[:], 0.0)
        st_memset("ve", onesf[:], 1.0)
        st_memset("ge", dumG[:], 0.0)
        st_memset("ve", Ssb[:], 0.0)
        st_memset("ve", S0bf[:], 0.0)
        st_memset("ve", qtail[:], 0.0)
        st_memset("ve", ktail[:], 0.0)
        st_memset("ve", vtail[:], 0.0)

        def fn_dumS(eng):
            return eng.memzero(dumS[:])
        p.step("se", fn_dumS)

        p.barrier()

        def fn_th2(eng):
            return eng.transpose(psG[0:H, 0:1], alogr[:], identf[0:1, 0:1]
                                 ).then_inc(S["c_gm"], 1)
        p.step("te", fn_th2)
        p.inc("c_gm")

        def fn_th3(eng):
            eng.wait_ge(S["c_gm"], 1)
            eng.activation(out=theta[:], in_=psG[0:H, 0:1], func=AF.Exp)
            eng.drain()
            return eng.copy(out=dumS[:], in_=dumS[:]).then_inc(S["c_gs"], 1)
        p.step("se", fn_th3)
        p.inc("c_gs")

        p.barrier()

        _a0b = p.val("a0")
        for t in range(NT):
            if t >= 2:
                def fn_w(eng, need=_a0b + 2 * (t - 1)):
                    eng.wait_ge(S["a0"], need)
                    return None
                p.step("sy", fn_w)
            lc = p.load("sy", Xbuf[t % 2][:], x_t[t])

            def fn_v(eng, t=t, lc=lc):
                eng.wait_ge(S["dma_sy"], lc)
                return eng.tensor_scalar(
                    out=Ybuf[0][:], in0=Xbuf[t % 2][:], scalar1=1.0, scalar2=0.0,
                    op0=ALU.mult, op1=ALU.add,
                    accum_out=ssum[:, t:t + 1]).then_inc(S["a0"], 1)
            p.step("ve", fn_v)
            p.inc("a0")

            def fn_s(eng, t=t, lc=lc):
                eng.wait_ge(S["dma_sy"], lc)
                return eng.activation(
                    out=Ybuf[1][:], in_=Xbuf[t % 2][:], func=AF.Square,
                    accum_out=ssq[:, t:t + 1]).then_inc(S["a0"], 1)
            p.step("se", fn_s)
            p.inc("a0")

        p.barrier()

        def fn_mc1(eng):
            eng.drain()
            eng.tensor_scalar(out=mcol[:], in0=ssum[:], scalar1=1.0 / D,
                              scalar2=None, op0=ALU.mult, op1=ALU.bypass)
            eng.tensor_scalar(out=tmp1[:], in0=ssq[:], scalar1=1.0 / D,
                              scalar2=None, op0=ALU.mult, op1=ALU.bypass)
            eng.drain()
            eng.tensor_tensor(out=tmp2[:], in0=mcol[:], in1=mcol[:], op=ALU.mult)
            eng.drain()
            eng.tensor_tensor(out=tmp1[:], in0=tmp1[:], in1=tmp2[:], op=ALU.subtract)
            eng.drain()
            return eng.tensor_scalar(out=tmp2[:], in0=tmp1[:], scalar1=1e-5,
                                     scalar2=None, op0=ALU.add, op1=ALU.bypass)
        p.step("ve", fn_mc1)
        p.barrier()

        def fn_mc2(eng):
            return eng.activation(out=tmp1[:], in_=tmp2[:], func=AF.Sqrt)
        p.step("se", fn_mc2)
        p.barrier()

        def fn_mc3(eng):
            return eng.reciprocal(out=Ccol[:], in_=tmp1[:])
        p.step("ve", fn_mc3)
        p.barrier()

        import os as _os
        _l1grp = int(_os.environ.get("K_L1GRP", "4"))
        _l1stage = _os.environ.get("K_L1STAGE", "full")

        def emit_chunk(g, ci):
            csl = slice(ci * CH, (ci + 1) * CH)
            _pdd = p.val("c_dd")
            _pam = p.val("c_am")
            _pas = p.val("c_as")
            _psb = p.val("c_sb")
            _sb0 = p.val("c_sb")

            _tv = p.val("c_tv")
            _poe = p.val("c_oe")

            def fn_tr1(eng, csl=csl, war=_tv, poe=_poe):
                if war >= 1:
                    eng.wait_ge(S["c_tv"], war)
                if poe >= 1:
                    eng.wait_ge(S["c_oe"], poe)
                eng.transpose(psTf[:, 0:16], loga[:, csl], identf[0:16, 0:16])
                return eng.transpose(psTf[:, 16:32], lnc[:, csl],
                                     identf[0:16, 0:16]).then_inc(S["c_tr"], 1)
            p.step("te", fn_tr1)
            tr1 = p.inc("c_tr")

            def fn_tv1(eng, tr=tr1):
                eng.wait_ge(S["c_tr"], tr)
                eng.tensor_copy(out=logaT[:], in_=psTf[:, 0:16])
                eng.tensor_copy(out=lncT[:], in_=psTf[:, 16:32])
                eng.drain()
                return eng.tensor_copy(out=dumV[:], in_=dumV[:]).then_inc(S["c_tv"], 1)
            p.step("ve", fn_tv1)
            tv1 = p.inc("c_tv")

            def fn_cu(eng, tv=tv1):
                eng.wait_ge(S["c_tv"], tv)
                eng.matmul(psTf[:, 32:48], uti[:], logaT[:], start=True, stop=True)
                return eng.matmul(psTf[:, 48:64], uts[:], logaT[:], start=True,
                                  stop=True).then_inc(S["c_tr"], 1)
            p.step("te", fn_cu)
            cu1 = p.inc("c_tr")

            def fn_cv1(eng, cu=cu1):
                eng.wait_ge(S["c_tr"], cu)
                eng.tensor_copy(out=LT[:], in_=psTf[:, 32:48])
                eng.tensor_copy(out=LpT[:], in_=psTf[:, 48:64])
                eng.drain()
                eng.tensor_scalar(out=negLT[:], in0=LT[:], scalar1=-1.0,
                                  scalar2=None, op0=ALU.mult, op1=ALU.bypass)
                eng.tensor_tensor(out=vT[:], in0=LpT[:], in1=lncT[:], op=ALU.add)
                eng.drain()
                return eng.tensor_copy(out=dumV[:], in_=dumV[:]).then_inc(S["c_tv"], 1)
            p.step("ve", fn_cv1)
            cv1 = p.inc("c_tv")

            def fn_lc(eng, tv=cv1):
                eng.wait_ge(S["c_tv"], tv)
                eng.matmul(psTf[0:1, 80:96], elast[:], LT[:], start=True, stop=True)
                eng.transpose(psTf[0:16, 96:224], LT[:], identf[:])
                eng.transpose(psTf[0:16, 224:352], negLT[:], identf[:])
                return eng.transpose(psTf[0:16, 352:480], vT[:],
                                     identf[:]).then_inc(S["c_tr"], 1)
            p.step("te", fn_lc)
            lc1 = p.inc("c_tr")

            def fn_lcv(eng, tr=lc1, pdd=_pdd, pam=_pam):
                eng.wait_ge(S["c_tr"], tr)
                if pdd >= 1:
                    eng.wait_ge(S["c_dd"], pdd)
                if pam >= 1:
                    eng.wait_ge(S["c_am"], pam)
                eng.tensor_copy(out=LCrow[:], in_=psTf[0:1, 80:96])
                eng.tensor_copy(out=Lh[:], in_=psTf[0:16, 96:224])
                eng.tensor_copy(out=negLh[:], in_=psTf[0:16, 224:352])
                eng.tensor_copy(out=vh[:], in_=psTf[0:16, 352:480])
                eng.drain()
                return eng.tensor_copy(out=dumV[:], in_=dumV[:]).then_inc(S["c_tv"], 1)
            p.step("ve", fn_lcv)
            lcv = p.inc("c_tv")

            def fn_lbc(eng, tv=lcv):
                eng.wait_ge(S["c_tv"], tv)
                return eng.matmul(psTf[:, 480:496], onesf[:], LCrow[:],
                                  start=True, stop=True).then_inc(S["c_tr"], 1)
            p.step("te", fn_lbc)
            lbc = p.inc("c_tr")

            def fn_lbcv(eng, tr=lbc, pas=_pas, psb=_psb):
                eng.wait_ge(S["c_tr"], tr)
                if pas >= 1:
                    eng.wait_ge(S["c_as"], pas)
                if psb >= 1:
                    eng.wait_ge(S["c_sb"], psb)
                eng.tensor_copy(out=LCbc[:], in_=psTf[:, 480:496])
                eng.drain()
                eng.tensor_tensor(out=wT[:], in0=LCbc[:], in1=LT[:], op=ALU.subtract)
                eng.drain()
                return eng.tensor_copy(out=dumV[:], in_=dumV[:]).then_inc(S["c_tv"], 1)
            p.step("ve", fn_lbcv)
            lbcv = p.inc("c_tv")

            def fn_exps(eng, tv=lbcv, psb=_psb):
                eng.wait_ge(S["c_tv"], tv)
                if psb >= 1:
                    eng.wait_ge(S["c_sb"], psb)
                eng.activation(out=rT[:], in_=vT[:], func=AF.Exp)
                eng.activation(out=gT[:], in_=LT[:], func=AF.Exp)
                eng.activation(out=wT[:], in_=wT[:], func=AF.Exp)
                eng.drain()
                return eng.copy(out=dumS[:], in_=dumS[:]).then_inc(S["c_cs"], 1)
            p.step("se", fn_exps)
            ecs = p.inc("c_cs")

            for hg in range(_l1grp):
                h0 = hg * 4
                Kcs, Qcs, eps, s0sls, offs = [], [], [], [], []
                for hh in range(4):
                    h = h0 + hh
                    chp = h // 2
                    off = (h % 2) * 64
                    Kcs.append(k_fm[chp][off:off + 64, csl])
                    Qcs.append(q_fm[chp][off:off + 64, csl])
                    eps.append(epat[:, h * 128:(h + 1) * 128])
                    s0sls.append(slice(h * 128, (h + 1) * 128))
                    offs.append(off)
                grp = slice(h0 * 128, (h0 + 4) * 128)

                _tvv = p.val("c_tv")
                _nevv = p.val("c_nev")

                def fn_vt(eng, h0=h0, csl=csl, tvv=_tvv, nevv=_nevv):
                    if tvv >= 1:
                        eng.wait_ge(S["c_tv"], tvv)
                    if nevv >= 1:
                        eng.wait_ge(S["c_nev"], nevv)

                    def ktr(hh):
                        h = h0 + hh
                        off = (h % 2) * 64
                        kin = k_fm[h // 2][off:off + 64, csl]
                        idb = identbf[off:off + 64, off:off + 64]
                        c0 = 512 + hh * 128
                        eng.transpose(psTb[:, c0:c0 + 64], kin, idb)
                        return eng.transpose(psTb[:, c0 + 64:c0 + 128], kin, idb)

                    eng.transpose(psTb[:, 0:128], v_fm[h0][:, csl], identbf[:])
                    eng.transpose(psTb[:, 128:256], v_fm[h0 + 1][:, csl],
                                  identbf[:])
                    ktr(0)
                    ktr(2)
                    eng.transpose(psTb[:, 256:384], v_fm[h0 + 2][:, csl],
                                  identbf[:])
                    eng.transpose(psTb[:, 384:512], v_fm[h0 + 3][:, csl],
                                  identbf[:])
                    ktr(1)
                    r = ktr(3)
                    return r.then_inc(S["c_tr"], 1)
                p.step("te", fn_vt)
                vt1 = p.inc("c_tr")

                _amv = p.val("c_am")

                def fn_vte(eng, tr=vt1, amv=_amv):
                    eng.wait_ge(S["c_tr"], tr)
                    if amv >= 1:
                        eng.wait_ge(S["c_am"], amv)
                    eng.tensor_copy(out=Vtok4[:], in_=psTb[:, 0:512])
                    eng.tensor_copy(out=Ktok4[:], in_=psTb[:, 512:1024])
                    eng.drain()
                    return eng.tensor_copy(out=dumV[:], in_=dumV[:]
                                           ).then_inc(S["c_tv"], 1)
                p.step("ve", fn_vte)
                vte = p.inc("c_tv")
                if _l1stage == "vt":
                    continue

                _avv = p.val("c_av")
                _dev = p.val("c_de")

                def fn_kd(eng, Kcs=Kcs, eps=eps, cs=ecs, avv=_avv, dev=_dev):
                    eng.wait_ge(S["c_cs"], cs)
                    if avv >= 1:
                        eng.wait_ge(S["c_av"], avv)
                    if dev >= 1:
                        eng.wait_ge(S["c_de"], dev)
                    for hh in (1, 3):
                        sl = slice(hh * 128, (hh + 1) * 128)
                        eng.matmul(psD2[:, sl], Kcs[hh], Kcs[hh],
                                   start=True, stop=True)
                    r = None
                    for hh in range(4):
                        sl = slice(hh * 128, (hh + 1) * 128)
                        eng.matmul(psD1[:, sl], eps[hh], vh[:],
                                   start=True, stop=False)
                        eng.matmul(psD1[:, sl], negLh[:], eps[hh],
                                   start=False, stop=False)
                        r = eng.matmul(psD1[:, sl], identf[:], lmask[:],
                                       start=False, stop=True)
                    for hh in (0, 2):
                        sl = slice(hh * 128, (hh + 1) * 128)
                        r = eng.matmul(psD2[:, sl], Kcs[hh], Kcs[hh],
                                       start=True, stop=True)
                    return r.then_inc(S["c_dd"], 1)
                p.step("te", fn_kd)
                kd1 = p.inc("c_dd")

                _dmv = p.val("c_dm")

                def fn_de(eng, dd=kd1, dmv=_dmv):
                    eng.wait_ge(S["c_dd"], dd)
                    if dmv >= 1:
                        eng.wait_ge(S["c_dm"], dmv)
                    eng.activation(out=Eex4[:], in_=psD1[:, 0:512], func=AF.Exp)
                    eng.drain()
                    return eng.copy(out=dumS[:], in_=dumS[:]
                                    ).then_inc(S["c_de"], 1)
                p.step("se", fn_de)
                de1 = p.inc("c_de")
                if _l1stage == "kd":
                    continue

                def fn_dm(eng, de=de1):
                    eng.wait_ge(S["c_de"], de)
                    eng.tensor_tensor(out=Msb4[:], in0=Eex4[:],
                                      in1=psD2[:, 0:512], op=ALU.mult)
                    eng.drain()
                    return eng.tensor_copy(out=dumV[:], in_=dumV[:]
                                           ).then_inc(S["c_dm"], 1)
                p.step("ve", fn_dm)
                dm1 = p.inc("c_dm")
                if _l1stage == "dm":
                    continue

                def fn_mt(eng, dm=dm1):
                    eng.wait_ge(S["c_dm"], dm)
                    r = None
                    for hh in range(4):
                        sl = slice(hh * 128, (hh + 1) * 128)
                        r = eng.transpose(psTb[:, sl], Msb4[:, sl], identbf[:])
                    return r.then_inc(S["c_tr"], 1)
                p.step("te", fn_mt)
                mt1 = p.inc("c_tr")

                def fn_mte(eng, tr=mt1):
                    eng.wait_ge(S["c_tr"], tr)
                    eng.tensor_copy(out=Mta4[:], in_=psTb[:, 0:512])
                    eng.drain()
                    return eng.tensor_copy(out=dumV[:], in_=dumV[:]
                                           ).then_inc(S["c_tv"], 1)
                p.step("ve", fn_mte)
                mte = p.inc("c_tv")
                if _l1stage == "setup":
                    continue

                _sbgN = p.val("c_sb")
                nprev = None
                for it in range(1, 8):
                    def fn_ni(eng, it=it, mte=mte, nprev=nprev, sbg=_sbgN):
                        if it == 1:
                            eng.wait_ge(S["c_tv"], mte)
                            if sbg >= 1:
                                eng.wait_ge(S["c_sb"], sbg)
                        else:
                            eng.wait_ge(S["c_nev"], nprev)
                        r = None
                        if it <= 6:
                            for hh in range(4):
                                sl = slice(hh * 128, (hh + 1) * 128)
                                eng.matmul(psN1[:, sl], Mta4[:, sl],
                                           Msb4[:, sl], start=True, stop=True)
                                r = eng.matmul(psN2[:, sl],
                                               Msb4[:, sl], Mta4[:, sl],
                                               start=True, stop=True)
                        if it >= 2:
                            for hh in range(4):
                                sl = slice(hh * 128, (hh + 1) * 128)
                                r = eng.matmul(psD2[:, sl], Mta4[:, sl],
                                               Pa4[:, sl], start=True, stop=True)
                        return r.then_inc(S["c_nmm"], 1)
                    p.step("te", fn_ni)
                    ni = p.inc("c_nmm")

                    def fn_nve(eng, it=it, ni=ni):
                        eng.wait_ge(S["c_nmm"], ni)
                        if it == 1:
                            eng.tensor_tensor(out=Pa4[:], in0=ident4[:],
                                              in1=Msb4[:], op=ALU.subtract)
                            eng.drain()
                        else:
                            eng.tensor_tensor(out=Pa4[:], in0=Pa4[:],
                                              in1=psD2[:, 0:512], op=ALU.add)
                        if it <= 6:
                            eng.tensor_copy(out=Msb4[:], in_=psN1[:, 0:512])
                            eng.tensor_copy(out=Mta4[:], in_=psN2[:, 0:512])
                        eng.drain()
                        return eng.tensor_copy(out=dumV[:], in_=dumV[:]
                                               ).then_inc(S["c_nev"], 1)
                    p.step("ve", fn_nve)
                    nprev = p.inc("c_nev")
                pp = nprev
                if _l1stage == "neu":
                    continue

                _poev = p.val("c_oe")

                def fn_te1(eng, Kcs=Kcs, Qcs=Qcs, eps=eps, s0sls=s0sls,
                           offs=offs, pp=pp, sbn=_sb0 - 3 + hg):
                    if sbn >= 1:
                        eng.wait_ge(S["c_sb"], sbn)
                    eng.wait_ge(S["c_nev"], pp)
                    def bb(hh):
                        sl = slice(hh * 128, (hh + 1) * 128)
                        eng.matmul(psD1[:, sl], eps[hh], Lh[:],
                                   start=True, stop=False)
                        eng.matmul(psD1[:, sl], negLh[:], eps[hh],
                                   start=False, stop=False)
                        return eng.matmul(psD1[:, sl], identf[:], smask[:],
                                          start=False, stop=True)
                    for hh in (0, 2):
                        sl = slice(hh * 128, (hh + 1) * 128)
                        eng.matmul(psN1[:, sl], Kcs[hh],
                                   S0bf[offs[hh]:offs[hh] + 64, s0sls[hh]],
                                   start=True, stop=True)
                        eng.matmul(psD2[:, sl], Kcs[hh], Qcs[hh],
                                   start=True, stop=True)
                    bb(0)
                    for hh in (1, 3):
                        sl = slice(hh * 128, (hh + 1) * 128)
                        eng.matmul(psN1[:, sl], Kcs[hh],
                                   S0bf[offs[hh]:offs[hh] + 64, s0sls[hh]],
                                   start=True, stop=True)
                        eng.matmul(psD2[:, sl], Kcs[hh], Qcs[hh],
                                   start=True, stop=True)
                    bb(1)
                    bb(2)
                    r = bb(3)
                    return r.then_inc(S["c_am"], 1)
                p.step("te", fn_te1)
                te1 = p.inc("c_am")

                def fn_se1(eng, am=te1):
                    eng.wait_ge(S["c_am"], am)
                    eng.activation(out=Eex4[:], in_=psD1[:, 0:512], func=AF.Exp)
                    eng.drain()
                    return eng.copy(out=dumS[:], in_=dumS[:]
                                    ).then_inc(S["c_de"], 1)
                p.step("se", fn_se1)
                se1 = p.inc("c_de")

                def fn_ve1(eng, h0=h0, am=te1, de=se1):
                    eng.wait_ge(S["c_am"], am)
                    eng.wait_ge(S["c_de"], de)
                    for hh in range(4):
                        sl = slice(hh * 128, (hh + 1) * 128)
                        eng.tensor_scalar(out=Rsb4[:, sl], in0=psN1[:, sl],
                                          scalar1=rT[:, h0 + hh:h0 + hh + 1],
                                          scalar2=-1.0, op0=ALU.mult,
                                          op1=ALU.mult)
                    eng.drain()
                    eng.tensor_tensor(out=Rsb4[:], in0=Rsb4[:], in1=Vtok4[:],
                                      op=ALU.add)
                    eng.tensor_tensor(out=Btsb4[:], in0=Eex4[:],
                                      in1=psD2[:, 0:512], op=ALU.mult)
                    eng.drain()
                    return eng.tensor_copy(out=dumV[:], in_=dumV[:]
                                           ).then_inc(S["c_av"], 1)
                p.step("ve", fn_ve1)
                ve1 = p.inc("c_av")

                def fn_te2(eng, av=ve1):
                    eng.wait_ge(S["c_av"], av)
                    r = None
                    for hh in range(4):
                        sl = slice(hh * 128, (hh + 1) * 128)
                        r = eng.matmul(psN2[:, sl], Pa4[:, sl], Rsb4[:, sl],
                                       start=True, stop=True)
                    return r.then_inc(S["c_am"], 1)
                p.step("te", fn_te2)
                te2 = p.inc("c_am")

                def fn_ve2(eng, h0=h0, am=te2):
                    eng.wait_ge(S["c_am"], am)
                    eng.tensor_copy(out=Usb4[:], in_=psN2[:, 0:512])
                    for hh in range(4):
                        sl = slice(hh * 128, (hh + 1) * 128)
                        eng.tensor_scalar(out=Uw4[:, sl], in0=psN2[:, sl],
                                          scalar1=wT[:, h0 + hh:h0 + hh + 1],
                                          scalar2=None, op0=ALU.mult,
                                          op1=ALU.bypass)
                    eng.drain()
                    return eng.tensor_copy(out=dumV[:], in_=dumV[:]
                                           ).then_inc(S["c_av"], 1)
                p.step("ve", fn_ve2)
                ve2 = p.inc("c_av")

                def fn_te3(eng, Qcs=Qcs, s0sls=s0sls, offs=offs, av=ve2):
                    eng.wait_ge(S["c_av"], av)
                    for hh in (0, 2):
                        sl = slice(hh * 128, (hh + 1) * 128)
                        eng.matmul(psN1[:, sl], Qcs[hh],
                                   S0bf[offs[hh]:offs[hh] + 64, s0sls[hh]],
                                   start=True, stop=True)
                    for hh in (0, 1):
                        sl = slice(hh * 128, (hh + 1) * 128)
                        eng.matmul(psN2[:, sl], Btsb4[:, sl], Usb4[:, sl],
                                   start=True, stop=True)
                    for hh in (1, 3):
                        sl = slice(hh * 128, (hh + 1) * 128)
                        eng.matmul(psN1[:, sl], Qcs[hh],
                                   S0bf[offs[hh]:offs[hh] + 64, s0sls[hh]],
                                   start=True, stop=True)
                    r = None
                    for hh in (2, 3):
                        sl = slice(hh * 128, (hh + 1) * 128)
                        r = eng.matmul(psN2[:, sl], Btsb4[:, sl], Usb4[:, sl],
                                       start=True, stop=True)
                    return r.then_inc(S["c_am"], 1)
                p.step("te", fn_te3)
                te3 = p.inc("c_am")

                def fn_ve3(eng, h0=h0, am=te3):
                    eng.wait_ge(S["c_am"], am)
                    for hh in range(4):
                        sl = slice(hh * 128, (hh + 1) * 128)
                        eng.tensor_scalar(out=Osb4[:, sl], in0=psN1[:, sl],
                                          scalar1=gT[:, h0 + hh:h0 + hh + 1],
                                          scalar2=None, op0=ALU.mult,
                                          op1=ALU.bypass)
                    eng.drain()
                    eng.tensor_tensor(out=Osb4[:], in0=Osb4[:],
                                      in1=psN2[:, 0:512], op=ALU.add)
                    eng.drain()
                    return eng.tensor_copy(out=dumV[:], in_=dumV[:]
                                           ).then_inc(S["c_av"], 1)
                p.step("ve", fn_ve3)
                ve3 = p.inc("c_av")
                if _l1stage == "app1":
                    continue

                def fn_se2(eng, av=ve3):
                    eng.wait_ge(S["c_av"], av)
                    for hh in range(4):
                        sl = slice(hh * 128, (hh + 1) * 128)
                        eng.activation(out=Rsb4[:, sl], in_=Osb4[:, sl],
                                       func=AF.Square,
                                       accum_out=oms4[:, hh:hh + 1])
                    eng.drain()
                    return eng.copy(out=dumS[:], in_=dumS[:]
                                    ).then_inc(S["c_as"], 1)
                p.step("se", fn_se2)
                se2 = p.inc("c_as")

                def fn_ve4(eng, as_=se2):
                    eng.wait_ge(S["c_as"], as_)
                    eng.tensor_scalar(out=oms4[:], in0=oms4[:],
                                      scalar1=1.0 / 128.0, scalar2=1e-6,
                                      op0=ALU.mult, op1=ALU.add)
                    eng.drain()
                    return eng.tensor_copy(out=dumV[:], in_=dumV[:]
                                           ).then_inc(S["c_av"], 1)
                p.step("ve", fn_ve4)
                ve4 = p.inc("c_av")

                def fn_se3(eng, av=ve4):
                    eng.wait_ge(S["c_av"], av)
                    eng.activation(out=oms4[:], in_=oms4[:], func=AF.Ln)
                    eng.drain()
                    eng.activation(out=oms4[:], in_=oms4[:], func=AF.Exp,
                                   scale=-0.5)
                    eng.drain()
                    return eng.copy(out=dumS[:], in_=dumS[:]
                                    ).then_inc(S["c_as"], 1)
                p.step("se", fn_se3)
                se3 = p.inc("c_as")

                def fn_ve5(eng, as_=se3):
                    eng.wait_ge(S["c_as"], as_)
                    for hh in range(4):
                        sl = slice(hh * 128, (hh + 1) * 128)
                        eng.tensor_scalar(out=Osb4[:, sl], in0=Osb4[:, sl],
                                          scalar1=oms4[:, hh:hh + 1],
                                          scalar2=None, op0=ALU.mult,
                                          op1=ALU.bypass)
                    eng.drain()
                    return eng.tensor_copy(out=dumV[:], in_=dumV[:]
                                           ).then_inc(S["c_av"], 1)
                p.step("ve", fn_ve5)
                ve5 = p.inc("c_av")

                def fn_te4(eng, av=ve5, lb=lbcv, oew=_poev):
                    eng.wait_ge(S["c_av"], av)
                    eng.wait_ge(S["c_tv"], lb)
                    if oew >= 1:
                        eng.wait_ge(S["c_oe"], oew)
                    r = None
                    for hh in range(4):
                        sl = slice(hh * 128, (hh + 1) * 128)
                        r = eng.transpose(psTf[:, sl], Osb4[:, sl], identf[:])
                    return r.then_inc(S["c_ot"], 1)
                p.step("te", fn_te4)
                te4 = p.inc("c_ot")
                if _l1stage == "app2":
                    continue

                def fn_ve6(eng, h0=h0, csl=csl, ot=te4):
                    eng.wait_ge(S["c_ot"], ot)
                    for hh in range(4):
                        sl = slice(hh * 128, (hh + 1) * 128)
                        eng.tensor_copy(out=o_fm[h0 + hh][:, csl],
                                        in_=psTf[:, sl])
                    eng.drain()
                    return eng.tensor_copy(out=dumV[:], in_=dumV[:]
                                           ).then_inc(S["c_oe"], 1)
                p.step("ve", fn_ve6)
                p.inc("c_oe")

                def fn_se4(eng, h0=h0, lb=lbcv, sbn=_sb0 + hg):
                    eng.wait_ge(S["c_tv"], lb)
                    if sbn >= 1:
                        eng.wait_ge(S["c_sb"], sbn)
                    eng.activation(out=gCc4[:], in_=LCbc[:, h0:h0 + 4],
                                   func=AF.Exp)
                    eng.drain()
                    return eng.copy(out=dumS[:], in_=dumS[:]
                                    ).then_inc(S["c_as"], 1)
                p.step("se", fn_se4)
                se4 = p.inc("c_as")

                def fn_te5(eng, av=ve3):
                    eng.wait_ge(S["c_av"], av)
                    r = None
                    for hh in range(4):
                        sl = slice(hh * 128, (hh + 1) * 128)
                        r = eng.matmul(psN1[:, sl], Ktok4[:, sl], Uw4[:, sl],
                                       start=True, stop=True)
                    return r.then_inc(S["c_am"], 1)
                p.step("te", fn_te5)
                te5 = p.inc("c_am")

                def fn_ve7(eng, h0=h0, grp=grp, am=te5, as_=se4):
                    eng.wait_ge(S["c_am"], am)
                    eng.wait_ge(S["c_as"], as_)
                    for hh in range(4):
                        sl = slice((h0 + hh) * 128, (h0 + hh + 1) * 128)
                        eng.tensor_scalar(out=Ssb[:, sl], in0=Ssb[:, sl],
                                          scalar1=gCc4[:, hh:hh + 1],
                                          scalar2=None, op0=ALU.mult,
                                          op1=ALU.bypass)
                    eng.drain()
                    eng.tensor_tensor(out=Ssb[:, grp], in0=Ssb[:, grp],
                                      in1=psN1[:, 0:512], op=ALU.add)
                    eng.drain()
                    eng.tensor_copy(out=S0bf[:, grp], in_=Ssb[:, grp])
                    eng.drain()
                    return eng.tensor_copy(out=dumV[:], in_=dumV[:]
                                           ).then_inc(S["c_sb"], 1)
                p.step("ve", fn_ve7)
                p.inc("c_sb")

        for g in range(NG):
            for ti in range(GS // 128):
                t = g * (GS // 128) + ti
                nx = p.val("a1x")
                if nx >= 2:
                    def fn_w(eng, need=nx - 1):
                        eng.wait_ge(S["a1x"], need)
                        return None
                    p.step("sy", fn_w)
                lc = p.load("sy", Xbuf[ti % 2][:], x_t[t])

                _tw = p.val("a1t")

                def fn_v(eng, t=t, ti=ti, lc=lc, tw=_tw):
                    eng.wait_ge(S["dma_sy"], lc)
                    if tw >= 2:
                        eng.wait_ge(S["a1t"], tw - 1)
                    return eng.tensor_scalar(
                        out=xnbuf[ti % 2][:], in0=Xbuf[ti % 2][:],
                        scalar1=mcol[:, t:t + 1], scalar2=Ccol[:, t:t + 1],
                        op0=ALU.subtract, op1=ALU.mult).then_inc(S["a1x"], 1)
                p.step("ve", fn_v)
                x_tgt = p.inc("a1x")

                _ew = p.val("a1e")

                def fn_t(eng, ti=ti, xt=x_tgt, ew=_ew):
                    eng.wait_ge(S["a1x"], xt)
                    if ew >= 1:
                        eng.wait_ge(S["a1e"], ew)
                    r = None
                    for c in range(NCH):
                        r = eng.transpose(psTb[:, c * 128:(c + 1) * 128],
                                          xnbuf[ti % 2][:, c * 128:(c + 1) * 128],
                                          identbf[:])
                    return r.then_inc(S["a1t"], 1)
                p.step("te", fn_t)
                t_tgt = p.inc("a1t")

                def fn_e(eng, ti=ti, tt=t_tgt):
                    eng.wait_ge(S["a1t"], tt)
                    r = None
                    for c in range(NCH):
                        r = eng.tensor_copy(
                            out=xn_fm[c][:, ti * 128:(ti + 1) * 128],
                            in_=psTb[:, c * 128:(c + 1) * 128])
                    return r.then_inc(S["a1e"], 1)
                p.step("ve", fn_e)
                p.inc("a1e")

            p.barrier()

            def emit_proj(kind):
                if kind == "q":
                    wsl = lambda c, m: wbig[:, c, m * 128:(m + 1) * 128]
                    nchunks, dst, tail, cw = NCH, q_fm, qtail, cqfm
                elif kind == "k":
                    wsl = lambda c, m: wbig[:, c, 1024 + m * 128:1024 + (m + 1) * 128]
                    nchunks, dst, tail, cw = NCH, k_fm, ktail, ckfm
                else:
                    wsl = lambda c, m: wbig[:, c, m * 128:(m + 1) * 128]
                    nchunks, dst, tail, cw = 16, v_fm, vtail, cvfm

                for m in range(nchunks):
                    _pev = p.val("c_pe")
                    _wcnt = p.dma_cnt["ge"] * 16

                    def fn_mm(eng, m=m, wsl=wsl, war=_pev, wcnt=_wcnt):
                        eng.wait_ge(S["dma_ge"], wcnt)
                        if war >= 1:
                            eng.wait_ge(S["c_pe"], war)
                        r = None
                        for c in range(NCH):
                            r = eng.matmul(psP[:, 0:512], wsl(c, m),
                                           xn_fm[c][:], start=(c == 0),
                                           stop=(c == NCH - 1))
                        return r.then_inc(S["c_pm"], 1)
                    p.step("te", fn_mm)
                    pm_tgt = p.inc("c_pm")

                    _cvw = p.val("c_cv")

                    def fn_ev(eng, m=m, pm=pm_tgt, war=_cvw):
                        eng.wait_ge(S["c_pm"], pm)
                        if war >= 1:
                            eng.wait_ge(S["c_cv"], war)
                        eng.tensor_copy(out=qpre[m % 2][:], in_=psP[:, 0:512])
                        eng.drain()
                        return eng.tensor_copy(out=dumV[:], in_=dumV[:]
                                               ).then_inc(S["c_pe"], 1)
                    p.step("ve", fn_ev)
                    pe_tgt = p.inc("c_pe")

                    ceng = "ve"
                    _slw = p.val("c_sl")
                    _nvw = p.val("c_nv")

                    def fn_conv(eng, m=m, pe=pe_tgt, tail=tail, cw=cw,
                                kind=kind, slw=_slw, nvw=_nvw):
                        eng.wait_ge(S["c_pe"], pe)
                        if slw >= 1:
                            eng.wait_ge(S["c_sl"], slw)
                        if nvw >= 1 and kind != "v":
                            eng.wait_ge(S["c_nv"], nvw)
                        src = qpre[m % 2]
                        eng.tensor_scalar(out=qtmp[:], in0=src[:],
                                          scalar1=cw[:, m, 3:4], scalar2=None,
                                          op0=ALU.mult, op1=ALU.bypass)
                        eng.drain()
                        for dly in (1, 2, 3):
                            wcol = cw[:, m, 3 - dly:4 - dly]
                            eng.scalar_tensor_tensor(
                                out=qtmp[:, dly:GS], in0=src[:, 0:GS - dly],
                                scalar=wcol, in1=qtmp[:, dly:GS],
                                op0=ALU.mult, op1=ALU.add)
                            eng.drain()
                            eng.scalar_tensor_tensor(
                                out=qtmp[:, 0:dly], in0=tail[:, m, 3 - dly:3],
                                scalar=wcol, in1=qtmp[:, 0:dly],
                                op0=ALU.mult, op1=ALU.add)
                            eng.drain()
                        eng.tensor_copy(out=tail[:, m, :], in_=src[:, GS - 3:GS])
                        eng.drain()
                        return eng.tensor_copy(out=dumV[:], in_=dumV[:]
                                               ).then_inc(S["c_cv"], 1)
                    p.step(ceng, fn_conv)
                    cv_tgt = p.inc("c_cv")

                    def fn_silu(eng, m=m, cv=cv_tgt, dst=dst, kind=kind):
                        eng.wait_ge(S["c_cv"], cv)
                        if kind == "v":
                            eng.activation(out=dst[m][:], in_=qtmp[:], func=AF.Silu)
                        else:
                            eng.activation(out=qtmp[:], in_=qtmp[:], func=AF.Silu)
                        eng.drain()
                        return eng.copy(out=dumS[:], in_=dumS[:]).then_inc(S["c_sl"], 1)
                    p.step("se", fn_silu)
                    sl_tgt = p.inc("c_sl")

                    if kind != "v":
                        def fn_sq(eng, sl=sl_tgt):
                            eng.wait_ge(S["c_sl"], sl)
                            eng.activation(out=qtmp2[:], in_=qtmp[:], func=AF.Square)
                            eng.drain()
                            return eng.copy(out=dumS[:], in_=dumS[:]
                                            ).then_inc(S["c_sl"], 1)
                        p.step("se", fn_sq)
                        sq_tgt = p.inc("c_sl")

                        _nv = p.val("c_nv")

                        def fn_nmm(eng, sq=sq_tgt, war=_nv):
                            eng.wait_ge(S["c_sl"], sq)
                            if war >= 1:
                                eng.wait_ge(S["c_nv"], war)
                            return eng.matmul(psG[0:2, 0:512], bd2[:], qtmp2[:],
                                              start=True, stop=True
                                              ).then_inc(S["c_nm"], 1)
                        p.step("te", fn_nmm)
                        nm_tgt = p.inc("c_nm")

                        def fn_nrm(eng, nm=nm_tgt):
                            eng.wait_ge(S["c_nm"], nm)
                            eng.tensor_scalar(out=nrm2[:], in0=psG[0:2, 0:512],
                                              scalar1=1e-12, scalar2=None,
                                              op0=ALU.max, op1=ALU.bypass)
                            eng.drain()
                            return eng.tensor_copy(out=dumV[:], in_=dumV[:]
                                                   ).then_inc(S["c_nv"], 1)
                        p.step("ve", fn_nrm)
                        nv_tgt = p.inc("c_nv")

                        def fn_nln(eng, nv=nv_tgt):
                            eng.wait_ge(S["c_nv"], nv)
                            eng.activation(out=nrm2[:], in_=nrm2[:], func=AF.Ln)
                            eng.drain()
                            eng.activation(out=nrm2[:], in_=nrm2[:], func=AF.Exp,
                                           scale=-0.5)
                            eng.drain()
                            return eng.copy(out=dumS[:], in_=dumS[:]
                                            ).then_inc(S["c_sl"], 1)
                        p.step("se", fn_nln)
                        nl_tgt = p.inc("c_sl")

                        def fn_nbc(eng, nl=nl_tgt):
                            eng.wait_ge(S["c_sl"], nl)
                            return eng.matmul(psG[:, 0:512], hd2[:], nrm2[:],
                                              start=True, stop=True
                                              ).then_inc(S["c_nm"], 1)
                        p.step("te", fn_nbc)
                        nb_tgt = p.inc("c_nm")

                        def fn_nsc(eng, m=m, nb=nb_tgt, dst=dst):
                            eng.wait_ge(S["c_nm"], nb)
                            eng.tensor_tensor(out=dst[m][:], in0=qtmp[:],
                                              in1=psG[:, 0:512], op=ALU.mult)
                            eng.drain()
                            return eng.tensor_copy(out=dumV[:], in_=dumV[:]
                                                   ).then_inc(S["c_nv"], 1)
                        p.step("ve", fn_nsc)
                        p.inc("c_nv")

            _pm0 = p.val("c_pm")

            def fn_wg1(eng, pm=_pm0):
                if pm:
                    eng.wait_ge(S["c_pm"], pm)
                return None
            p.step("ge", fn_wg1)
            p.load("ge", wbig[:, :, 0:1024],
                   wqT_d.rearrange("(c p) n -> p c n", p=128))
            p.load("ge", wbig[:, :, 1024:2048],
                   wkT_d.rearrange("(c p) n -> p c n", p=128))
            emit_proj("q")
            emit_proj("k")
            _pm1 = p.val("c_pm")

            def fn_wg2(eng, pm=_pm1):
                eng.wait_ge(S["c_pm"], pm)
                return None
            p.step("ge", fn_wg2)
            p.load("ge", wbig[:, :, 0:2048],
                   wvT_d.rearrange("(c p) n -> p c n", p=128))
            emit_proj("v")

            _nvx = p.val("c_nv")

            def fn_gmm(eng, war=_nvx):
                if war >= 1:
                    eng.wait_ge(S["c_nv"], war)
                r = None
                for c in range(NCH):
                    eng.matmul(psG[0:16, 0:GS], wabT[:, c, 0:H], xn_fm[c][:],
                               start=(c == 0), stop=(c == NCH - 1))
                for c in range(NCH):
                    r = eng.matmul(psD1[0:16, 0:GS], wabT[:, c, H:2 * H],
                                   xn_fm[c][:], start=(c == 0),
                                   stop=(c == NCH - 1))
                return r.then_inc(S["c_gm"], 1)
            p.step("te", fn_gmm)
            gm_tgt = p.inc("c_gm")

            def fn_gev(eng, gm=gm_tgt):
                eng.wait_ge(S["c_gm"], gm)
                eng.tensor_scalar(out=alin[:], in0=psG[0:16, 0:GS],
                                  scalar1=dtbc[:], scalar2=None,
                                  op0=ALU.add, op1=ALU.bypass)
                eng.tensor_copy(out=blin[:], in_=psD1[0:16, 0:GS])
                eng.drain()
                eng.tensor_scalar(out=gsc1[:], in0=alin[:], scalar1=0.0,
                                  scalar2=None, op0=ALU.max, op1=ALU.bypass)
                eng.tensor_scalar(out=gsc2[:], in0=alin[:], scalar1=-1.0,
                                  scalar2=0.0, op0=ALU.mult, op1=ALU.max)
                eng.drain()
                eng.tensor_tensor(out=gsc2[:], in0=gsc2[:], in1=gsc1[:],
                                  op=ALU.add)
                eng.drain()
                return eng.tensor_copy(out=dumV[:], in_=dumV[:]).then_inc(S["c_gv"], 1)
            p.step("ve", fn_gev)
            gv_tgt = p.inc("c_gv")

            def fn_gse(eng, gv=gv_tgt):
                eng.wait_ge(S["c_gv"], gv)
                eng.activation(out=gsc2[:], in_=gsc2[:], func=AF.Exp, scale=-1.0)
                eng.drain()
                eng.activation(out=gsc2[:], in_=gsc2[:], func=AF.Ln, bias=1.0)
                eng.drain()
                return eng.copy(out=dumS[:], in_=dumS[:]).then_inc(S["c_gs"], 1)
            p.step("se", fn_gse)
            gs_tgt = p.inc("c_gs")

            def fn_gv2(eng, gs=gs_tgt):
                eng.wait_ge(S["c_gs"], gs)
                eng.tensor_tensor(out=gsc1[:], in0=gsc1[:], in1=gsc2[:], op=ALU.add)
                eng.drain()
                eng.tensor_scalar(out=loga[:], in0=gsc1[:], scalar1=theta[:],
                                  scalar2=-1.0, op0=ALU.mult, op1=ALU.mult)
                eng.drain()
                eng.tensor_scalar(out=gsc1[:], in0=blin[:], scalar1=-1.0,
                                  scalar2=0.0, op0=ALU.mult, op1=ALU.max)
                eng.tensor_scalar(out=gsc2[:], in0=blin[:], scalar1=0.0,
                                  scalar2=None, op0=ALU.max, op1=ALU.bypass)
                eng.drain()
                eng.tensor_tensor(out=gsc2[:], in0=gsc2[:], in1=gsc1[:],
                                  op=ALU.add)
                eng.drain()
                return eng.tensor_copy(out=dumV[:], in_=dumV[:]).then_inc(S["c_gv"], 1)
            p.step("ve", fn_gv2)
            gv2_tgt = p.inc("c_gv")

            def fn_gse2(eng, gv=gv2_tgt):
                eng.wait_ge(S["c_gv"], gv)
                eng.activation(out=gsc2[:], in_=gsc2[:], func=AF.Exp, scale=-1.0)
                eng.drain()
                eng.activation(out=gsc2[:], in_=gsc2[:], func=AF.Ln, bias=1.0)
                eng.drain()
                return eng.copy(out=dumS[:], in_=dumS[:]).then_inc(S["c_gs"], 1)
            p.step("se", fn_gse2)
            gs2_tgt = p.inc("c_gs")

            def fn_gv3(eng, gs=gs2_tgt):
                eng.wait_ge(S["c_gs"], gs)
                eng.tensor_tensor(out=gsc1[:], in0=gsc1[:], in1=gsc2[:], op=ALU.add)
                eng.drain()
                eng.tensor_tensor(out=lnc[:], in0=loga[:], in1=gsc1[:],
                                  op=ALU.subtract)
                eng.drain()
                eng.tensor_scalar(out=lnc[:], in0=lnc[:],
                                  scalar1=0.6931471805599453, scalar2=None,
                                  op0=ALU.add, op1=ALU.bypass)
                eng.drain()
                return eng.tensor_copy(out=dumV[:], in_=dumV[:]).then_inc(S["c_gv"], 1)
            p.step("ve", fn_gv3)
            p.inc("c_gv")

            p.barrier()

            for ci in range(NCK):
                emit_chunk(g, ci)

            p.barrier()

            for tt in range(GS // 128):
                t = g * (GS // 128) + tt
                lc = p.load("sy", Xbuf[tt % 2][:], x_t[t])
                for half in range(2):
                    _yv = p.val("a3v")

                    def fn_omm(eng, tt=tt, half=half, war=_yv):
                        if war >= 1:
                            eng.wait_ge(S["a3v"], war)
                        r = None
                        for hc in range(16):
                            r = eng.matmul(
                                psP[:, 0:512],
                                o_fm[hc][:, tt * 128:(tt + 1) * 128],
                                woT[:, hc, half * 512:(half + 1) * 512],
                                start=(hc == 0), stop=(hc == 15))
                        return r.then_inc(S["c_op"], 1)
                    p.step("te", fn_omm)
                    op_tgt = p.inc("c_op")

                    _stp = (p.dma_cnt["st"] - 1) * 16

                    def fn_y(eng, tt=tt, half=half, op=op_tgt, lc=lc, stp=_stp):
                        eng.wait_ge(S["c_op"], op)
                        eng.wait_ge(S["dma_sy"], lc)
                        if stp >= 16:
                            eng.wait_ge(S["dma_st"], stp)
                        return eng.tensor_tensor(
                            out=Ybuf[tt % 2][:, half * 512:(half + 1) * 512],
                            in0=Xbuf[tt % 2][:, half * 512:(half + 1) * 512],
                            in1=psP[:, 0:512], op=ALU.add).then_inc(S["a3v"], 1)
                    p.step("ve", fn_y)
                    p.inc("a3v")

                _yv2 = p.val("a3v")

                def fn_stw(eng, tgt=_yv2):
                    eng.wait_ge(S["a3v"], tgt)
                    return None
                p.step("sy", fn_stw)
                p.store(y_t[t], Ybuf[tt % 2][:])

            p.barrier()

        if not fc:
            p.finalize(block, S, {"ve": dumV[:], "se": dumS[:], "ge": dumG[:]})

    return nc


def _run_l1_attn(x, Wq, Wk, Wv, Wb, Wa, A_log, dt_bias, Wo, conv_q, conv_k, conv_v):
    identf, identbf, _ = _consts()
    H = 16
    epat = np.zeros((H, H * 128), dtype=np.float32)
    for h in range(H):
        epat[h, h * 128:(h + 1) * 128] = 1.0
    BIG = 30000.0
    ii = np.arange(128)
    lmask = np.where(ii[:, None] >= ii[None, :], -BIG, 0.0).astype(np.float32)
    smask = np.where(ii[:, None] > ii[None, :], -BIG, 0.0).astype(np.float32)
    uti = (ii[:, None] <= ii[None, :]).astype(np.float32)
    uts = (ii[:, None] < ii[None, :]).astype(np.float32)
    bd2 = np.zeros((128, 2), dtype=ml_dtypes.bfloat16)
    bd2[0:64, 0] = 1.0
    bd2[64:128, 1] = 1.0
    hd2 = np.zeros((2, 128), dtype=np.float32)
    hd2[0, 0:64] = 1.0
    hd2[1, 64:128] = 1.0
    elast = np.zeros((128, 1), dtype=np.float32)
    elast[127, 0] = 1.0

    if "l1a" not in _CACHE:
        _CACHE["l1a"] = build_l1_attn()
    nc = _CACHE["l1a"]
    base = {
        "wqT": np.ascontiguousarray(Wq.T), "wkT": np.ascontiguousarray(Wk.T),
        "wvT": np.ascontiguousarray(Wv.T), "waT": np.ascontiguousarray(Wa.T),
        "wbT": np.ascontiguousarray(Wb.T), "woT": np.ascontiguousarray(Wo.T),
        "cqfm": np.ascontiguousarray(conv_q.reshape(8, 128, 4).transpose(1, 0, 2)),
        "ckfm": np.ascontiguousarray(conv_k.reshape(8, 128, 4).transpose(1, 0, 2)),
        "cvfm": np.ascontiguousarray(conv_v.reshape(16, 128, 4).transpose(1, 0, 2)),
        "alogrow": np.ascontiguousarray(A_log.reshape(1, H)),
        "dtbcol": np.ascontiguousarray(dt_bias.reshape(H, 1)),
        "identf": identf, "identbf": identbf, "epat": epat,
        "lmask": lmask, "smask": smask, "uti": uti, "uts": uts,
        "bd2": bd2, "hd2": hd2, "elast": elast,
    }
    in_maps = []
    for c in range(8):
        m = dict(base)
        m["x"] = np.ascontiguousarray(x[c])
        in_maps.append(m)
    res = run_bass_kernel_spmd(nc, in_maps, core_ids=list(range(8)))
    return np.stack([np.asarray(r["y"]) for r in res.results], axis=0)



def _consts():
    identf = np.eye(128, dtype=np.float32)
    identbf = np.eye(128, dtype=ml_dtypes.bfloat16)
    bd65 = np.zeros((128, 65), dtype=ml_dtypes.bfloat16)
    bd65[0:64, 64] = 1.0
    bd65[64:128, 32] = 1.0
    return identf, identbf, bd65


_CACHE = {}


def _run_l0_attn(x, ln_scale, ln_bias, wq, wk, wv):
    identf, identbf, bd65 = _consts()
    lns_fm = np.ascontiguousarray(ln_scale.reshape(NCH, 128).T)
    lnb_fm = np.ascontiguousarray(ln_bias.reshape(NCH, 128).T)

    import os
    _ph = os.environ.get("K_PHASES")
    if _ph:
        nc = build_l0_attn(phases=tuple(_ph.split(",")))
    else:
        if "l0a" not in _CACHE:
            _CACHE["l0a"] = build_l0_attn()
        nc = _CACHE["l0a"]

    base = {
        "wq": np.ascontiguousarray(wq), "wk": np.ascontiguousarray(wk),
        "wv": np.ascontiguousarray(wv),
        "lns_fm": lns_fm, "lnb_fm": lnb_fm,
        "lns_row": np.ascontiguousarray(ln_scale.reshape(1, D)),
        "lnb_row": np.ascontiguousarray(ln_bias.reshape(1, D)),
        "identbf": identbf, "identf": identf, "bd65": bd65,
    }
    in_maps = []
    for c in range(8):
        m = dict(base)
        m["x"] = np.ascontiguousarray(x[c])
        in_maps.append(m)
    res = run_bass_kernel_spmd(nc, in_maps, core_ids=list(range(8)))
    return np.stack([np.asarray(r["y"]) for r in res.results], axis=0)




_ALL_SEMS = [
    "dma_ge", "dma_sy", "dma_st", "bar", "bc",
    "a0", "a1x", "a1t", "a1e", "proj",
    "s_q", "s_k", "s_v", "s_evq", "s_evk", "s_evv",
    "s_nq", "s_nk", "s_evn",
    "s_s", "s_p", "s_pv", "s_eo", "s_tr", "s_no", "a3v",
    "b_l", "b_g", "b_row", "b_bc", "b_x",
    "b_a", "b_as", "b_b", "b_h", "b_o", "b_y", "b_m", "b_ms",
    "c_pm", "c_pe", "c_cv", "c_sl", "c_nm", "c_nv",
    "c_gm", "c_gv", "c_gs",
    "c_tr", "c_tv", "c_cs",
    "c_kk", "c_dd", "c_de", "c_dm",
    "c_nmm", "c_nev", "c_pp",
    "c_am", "c_av", "c_as",
    "c_ot", "c_oe", "c_sb", "c_op",
]


def build_fused(T=T, NT=NT):
    nc = bass.Bass("TRN2", target_bir_lowering=False, detect_race_conditions=False)
    x_d = nc.dram_tensor("x", [T, D], F32, kind="ExternalInput").ap()
    x1_d = nc.dram_tensor("x1buf", [T, D], F32, kind="Internal").ap()
    x2_d = nc.dram_tensor("x2buf", [T, D], F32, kind="Internal").ap()
    x3_d = nc.dram_tensor("x3buf", [T, D], F32, kind="Internal").ap()
    y_d = nc.dram_tensor("y", [T, D], F32, kind="ExternalOutput").ap()
    p = Prog(nc)
    with ExitStack() as ctx:
        e = ctx.enter_context
        S = {n: e(nc.semaphore(n)) for n in _ALL_SEMS}
        gdV = e(nc.sbuf_tensor("G_dumV", [1, 1], F32))
        gdS = e(nc.sbuf_tensor("G_dumS", [1, 1], F32))
        gdG = e(nc.sbuf_tensor("G_dumG", [1, 1], F32))

        def fnv(eng):
            return eng.memset(gdV[:], 0.0)
        p.step("ve", fnv)

        def fng(eng):
            return eng.memset(gdG[:], 0.0)
        p.step("ge", fng)

        def fns(eng):
            return eng.memzero(gdS[:])
        p.step("se", fns)
        p.barrier()

        fc = {"nc": nc, "p": p, "S": S, "dram_cache": {}}
        build_l0_attn(T=T, NT=NT, fc=dict(fc, px="A_", x_in=x_d, x_out=x1_d))
        p.barrier()
        build_l0_ffn(T=T, NT=NT, fc=dict(fc, px="B_", x_in=x1_d, x_out=x2_d,
                                         dram_cache=fc["dram_cache"]))
        p.barrier()
        build_l1_attn(T=T, NT=NT, fc=dict(fc, px="C_", x_in=x2_d, x_out=x3_d,
                                          dram_cache=fc["dram_cache"]))
        p.barrier()
        build_l1_ffn(T=T, NT=NT, fc=dict(fc, px="D_", x_in=x3_d, x_out=y_d,
                                         dram_cache=fc["dram_cache"]))
        p.barrier()

        block = e(nc.Block())
        p.finalize(block, S, {"ve": gdV[:], "se": gdS[:], "ge": gdG[:]})
    return nc


def _fused_base(inputs):
    f = lambda k: np.asarray(inputs[k], dtype=np.float32)
    identf, identbf, bd65 = _consts()
    H = 16
    epat = np.zeros((H, H * 128), dtype=np.float32)
    for h in range(H):
        epat[h, h * 128:(h + 1) * 128] = 1.0
    BIG = 30000.0
    ii = np.arange(128)
    lmask = np.where(ii[:, None] >= ii[None, :], -BIG, 0.0).astype(np.float32)
    smask = np.where(ii[:, None] > ii[None, :], -BIG, 0.0).astype(np.float32)
    uti = (ii[:, None] <= ii[None, :]).astype(np.float32)
    uts = (ii[:, None] < ii[None, :]).astype(np.float32)
    bd2 = np.zeros((128, 2), dtype=ml_dtypes.bfloat16)
    bd2[0:64, 0] = 1.0
    bd2[64:128, 1] = 1.0
    hd2 = np.zeros((2, 128), dtype=np.float32)
    hd2[0, 0:64] = 1.0
    hd2[1, 64:128] = 1.0
    elast = np.zeros((128, 1), dtype=np.float32)
    elast[127, 0] = 1.0
    ln_scale = f("L0_attn_ln_scale")
    ln_bias = f("L0_attn_ln_bias")
    w1a = np.concatenate([f("L0_ffn_w1"), f("L0_ffn_w1s")], axis=0)
    w2a = np.concatenate([f("L0_ffn_w2"), f("L0_ffn_w2s")], axis=0)
    w3a = np.concatenate([f("L0_ffn_w3"), f("L0_ffn_w3s")], axis=0)
    base = {
        "wq": np.ascontiguousarray(f("L0_attn_wq")),
        "wk": np.ascontiguousarray(f("L0_attn_wk")),
        "wv": np.ascontiguousarray(f("L0_attn_wv")),
        "lns_fm": np.ascontiguousarray(ln_scale.reshape(NCH, 128).T),
        "lnb_fm": np.ascontiguousarray(ln_bias.reshape(NCH, 128).T),
        "lns_row": np.ascontiguousarray(ln_scale.reshape(1, D)),
        "lnb_row": np.ascontiguousarray(ln_bias.reshape(1, D)),
        "bd65": bd65,
        "gwT": np.ascontiguousarray(f("L0_ffn_gate_w").T),
        "w1": np.ascontiguousarray(w1a), "w2": np.ascontiguousarray(w2a),
        "w3": np.ascontiguousarray(w3a),
        "wqT": np.ascontiguousarray(f("L1_attn_Wq").T),
        "wkT": np.ascontiguousarray(f("L1_attn_Wk").T),
        "wvT": np.ascontiguousarray(f("L1_attn_Wv").T),
        "waT": np.ascontiguousarray(f("L1_attn_Wa").T),
        "wbT": np.ascontiguousarray(f("L1_attn_Wb").T),
        "woT": np.ascontiguousarray(f("L1_attn_Wo").T),
        "cqfm": np.ascontiguousarray(f("L1_attn_conv_q").reshape(8, 128, 4).transpose(1, 0, 2)),
        "ckfm": np.ascontiguousarray(f("L1_attn_conv_k").reshape(8, 128, 4).transpose(1, 0, 2)),
        "cvfm": np.ascontiguousarray(f("L1_attn_conv_v").reshape(16, 128, 4).transpose(1, 0, 2)),
        "alogrow": np.ascontiguousarray(f("L1_attn_A_log").reshape(1, H)),
        "dtbcol": np.ascontiguousarray(f("L1_attn_dt_bias").reshape(H, 1)),
        "epat": epat, "lmask": lmask, "smask": smask, "uti": uti, "uts": uts,
        "bd2": bd2, "hd2": hd2, "elast": elast,
        "rw1": np.ascontiguousarray(f("L1_ffn_rw1")),
        "rb1fm": np.ascontiguousarray(f("L1_ffn_rb1").reshape(NCH, 128).T),
        "rw2": np.ascontiguousarray(f("L1_ffn_rw2")),
        "rb2row": np.ascontiguousarray(f("L1_ffn_rb2").reshape(1, 8)),
        "ew1": np.ascontiguousarray(f("L1_ffn_ew1")),
        "eb1fm": np.ascontiguousarray(f("L1_ffn_eb1").reshape(8, 16, 128).transpose(0, 2, 1)),
        "ew2": np.ascontiguousarray(f("L1_ffn_ew2")),
        "eb2row": np.ascontiguousarray(f("L1_ffn_eb2").reshape(8, 1, D)),
        "identf": identf, "identbf": identbf,
    }
    return base


def kernel(**inputs):
    x = np.asarray(inputs["L0_attn_x"], dtype=np.float32)
    if "fused" not in _CACHE:
        _CACHE["fused"] = build_fused()
    nc = _CACHE["fused"]
    base = _fused_base(inputs)
    in_maps = []
    for c in range(8):
        m = dict(base)
        m["x"] = np.ascontiguousarray(x[c])
        in_maps.append(m)
    res = run_bass_kernel_spmd(nc, in_maps, core_ids=list(range(8)))
    return np.stack([np.asarray(r["y"]) for r in res.results], axis=0)
